# version 10
# baseline (speedup 1.0000x reference)
import contextlib
import math

import numpy as np
import ml_dtypes

import concourse.bass as bass
import concourse.mybir as mybir
from concourse.bass_utils import run_bass_kernel_spmd

F32 = mybir.dt.float32
BF16 = mybir.dt.bfloat16
AF = mybir.ActivationFunctionType
ALU = mybir.AluOpType
BF = ml_dtypes.bfloat16

D = 2048
DH = 8192
HA = 16
CA = 128
HK, HV, DK, DV = 16, 32, 128, 128
CK = 64
NEG = -1e30
LN_EPS = 1e-5
L2_EPS = 1e-6
RMS_EPS = 1e-6
SCALE_DK = DK ** -0.5


def _slopes16():
    start = 2.0 ** (-2.0 ** (-(math.log2(16) - 3)))
    return np.array([start ** (i + 1) for i in range(16)], dtype=np.float32)


def host_constants():
    c = {}
    lam = np.exp(-_slopes16()).astype(np.float64)
    pos = np.arange(CA, dtype=np.float64)
    dm = np.where(pos[None, :, None] >= pos[None, None, :],
                  lam[:, None, None] ** (pos[None, :, None] - pos[None, None, :]),
                  0.0)
    dmT = dm.transpose(2, 0, 1)
    c["c_dmaskT"] = np.ascontiguousarray(dmT.reshape(CA, HA * CA)).astype(BF)
    dq = lam[:, None] ** (pos[None, :] + 1.0)
    c["c_decq"] = np.ascontiguousarray(np.broadcast_to(dq[None], (128, HA, CA)).reshape(128, HA * CA)).astype(BF)
    dk_ = lam[:, None] ** (CA - 1.0 - pos[None, :])
    c["c_deck"] = np.ascontiguousarray(dk_.T.astype(np.float32))
    c["c_lamc"] = np.ascontiguousarray(np.broadcast_to((lam ** CA).astype(np.float32)[None], (128, HA)).copy())
    c["c_ones_bf"] = np.ones((128, 128), np.float32).astype(BF)
    c["c_ident_bf"] = np.eye(128, dtype=np.float32).astype(BF)
    tril_s = np.tril(np.ones((CK, CK), np.float32), -1)
    tril_i = np.tril(np.ones((CK, CK), np.float32))
    c["c_ubias_i"] = np.ascontiguousarray(np.tile(np.where(tril_i.T > 0, 0.0, NEG).astype(np.float32), (2, 8)))
    c["c_ubias_s"] = np.ascontiguousarray(np.tile(np.where(tril_s.T > 0, 0.0, NEG).astype(np.float32), (2, 8)))
    c["c_lbias_s"] = np.ascontiguousarray(np.tile(np.where(tril_s > 0, 0.0, NEG).astype(np.float32), (2, 8)))
    b2 = np.zeros((32, HK, 128), np.float32)
    for j in range(HK):
        b2[j, j, 0:64] = 1.0
        b2[j + 16, j, 64:128] = 1.0
    c["c_sel2"] = np.ascontiguousarray(b2.reshape(32, HK * 128))
    c["c_ident_f32"] = np.eye(128, dtype=np.float32)
    b1 = np.zeros((32, HV, 128), np.float32)
    for h in range(HV):
        b1[h, h, :] = 1.0
    c["c_sel1"] = np.ascontiguousarray(b1.reshape(32, HV * 128))
    return c


def host_weights(i):
    w = {}
    wq = i["L0_attn_a_w_qkv"]
    w["w_qT"] = np.ascontiguousarray(wq[0:D].T)
    w["w_kT"] = np.ascontiguousarray(wq[D:2 * D].T)
    w["w_vT"] = np.ascontiguousarray(wq[2 * D:3 * D].T)
    bq = i["L0_attn_a_b_qkv"]
    w["b_q"] = np.ascontiguousarray(bq[0:D].reshape(16, 128).T)
    w["b_k"] = np.ascontiguousarray(bq[D:2 * D].reshape(16, 128).T)
    w["b_v_rep"] = np.ascontiguousarray(np.broadcast_to(bq[2 * D:3 * D][None, :], (128, D)).copy())
    w["b_v"] = np.ascontiguousarray(bq[2 * D:3 * D].reshape(16, 128).T)
    w["w_oT"] = np.ascontiguousarray(i["L0_attn_a_w_out"].T)
    w["b_o_rep"] = np.ascontiguousarray(np.broadcast_to(i["L0_attn_a_b_out"][None, :], (128, D)).copy())
    w["w_gT"] = np.ascontiguousarray(i["L0_ffn_w_gate"].T)
    w["w_uT"] = np.ascontiguousarray(i["L0_ffn_w_up"].T)
    w["w_dT"] = np.ascontiguousarray(i["L0_ffn_w_down"].T)
    VPERM = [2 * j for j in range(16)] + [2 * j + 1 for j in range(16)]
    Wqkvz = i["L0_attn_b_W_qkvz"]
    blk = 2 * DK + 2 * 2 * DV
    cols = []
    for h in range(HK):
        cols.append(np.arange(h * blk, h * blk + DK))
    for h in range(HK):
        cols.append(np.arange(h * blk + DK, h * blk + 2 * DK))
    for dv in VPERM:
        h, r = dv // 2, dv % 2
        cols.append(np.arange(h * blk + 2 * DK + r * DV, h * blk + 2 * DK + (r + 1) * DV))
    for dv in VPERM:
        h, r = dv // 2, dv % 2
        cols.append(np.arange(h * blk + 2 * DK + 2 * DV + r * DV, h * blk + 2 * DK + 2 * DV + (r + 1) * DV))
    w["w_qkvzT"] = np.ascontiguousarray(Wqkvz[:, np.concatenate(cols)])
    Wba = i["L0_attn_b_W_ba"]
    bcols = [(ov // 2) * 4 + (ov % 2) for ov in VPERM]
    acols = [(ov // 2) * 4 + 2 + (ov % 2) for ov in VPERM]
    w["w_baT"] = np.ascontiguousarray(Wba[:, bcols + acols])
    cw = i["L0_attn_b_conv_w"].reshape(2, HK, DK, 4) if False else i["L0_attn_b_conv_w"]
    vrows = np.concatenate([np.arange(4096 + ov * DV, 4096 + (ov + 1) * DV) for ov in VPERM])
    rows = np.concatenate([np.arange(0, 4096), vrows])
    cwp = cw[rows]
    w["convw"] = np.ascontiguousarray(cwp.reshape(64, 128, 4).transpose(1, 0, 2).reshape(128, 256))
    w["dt_bias"] = np.ascontiguousarray(i["L0_attn_b_dt_bias"][VPERM].reshape(32, 1))
    w["A_log"] = np.ascontiguousarray(i["L0_attn_b_A_log"][VPERM].reshape(32, 1))
    w["normw"] = np.ascontiguousarray(i["L0_attn_b_norm_weight"].reshape(128, 1))
    wrows = np.concatenate([np.arange(ov * DV, (ov + 1) * DV) for ov in VPERM])
    w["w_outB"] = np.ascontiguousarray(i["L0_attn_b_W_out"][wrows])
    return w


WEIGHT_SHAPES = {
    "w_qT": [D, D], "w_kT": [D, D], "w_vT": [D, D], "w_oT": [D, D],
    "b_q": [128, 16], "b_k": [128, 16], "b_v": [128, 16], "b_v_rep": [128, D], "b_o_rep": [128, D],
    "w_gT": [D, DH], "w_uT": [D, DH], "w_dT": [DH, D],
    "w_qkvzT": [D, 12288], "w_baT": [D, 64], "convw": [128, 256],
    "dt_bias": [32, 1], "A_log": [32, 1], "normw": [128, 1], "w_outB": [4096, D],
}
CONST_SHAPES = {
    "c_dmaskT": ([CA, HA * CA], BF16), "c_decq": ([128, HA * CA], BF16),
    "c_deck": ([128, HA], F32), "c_lamc": ([128, HA], F32),
    "c_ones_bf": ([128, 128], BF16), "c_ident_bf": ([128, 128], BF16),
    "c_ubias_i": ([128, 8 * CK], F32), "c_ubias_s": ([128, 8 * CK], F32),
    "c_lbias_s": ([128, 8 * CK], F32),
    "c_sel2": ([32, HK * 128], F32), "c_sel1": ([32, HV * 128], F32),
    "c_ident_f32": ([128, 128], F32),
    "parity": ([128, 1], F32),
}
SB_CONSTS = ["c_dmaskT", "c_decq", "c_deck", "c_lamc", "c_ones_bf", "c_ident_bf",
             "c_strictneg", "c_striuneg", "c_inclbias", "c_upperbias",
             "c_sel2", "c_sel1", "parity", "b_q", "b_k", "dt_bias", "A_log", "normw"]


class Prog:
    def __init__(self, T=4096, dbg=False):
        self.nc = bass.Bass("TRN2", target_bir_lowering=False)
        self.T = T
        self.dbg = dbg
        self.es = contextlib.ExitStack()
        self.scopes = [self.es]
        self.scope_sems = [[]]
        self.d = {}
        self._sem_n = 0
        self._sb_n = 0

    def dram(self, name, shape, dtype, kind=None):
        t = self.nc.dram_tensor(name, shape, dtype) if kind is None else \
            self.nc.dram_tensor(name, shape, dtype, kind=kind)
        self.d[name] = t.ap()

    def scratch(self, name, shape, dtype):
        self.dram(name, shape, dtype, "ExternalOutput" if self.dbg else None)

    def sbuf(self, name, shape, dtype):
        self._sb_n += 1
        return self.scopes[-1].enter_context(
            self.nc.sbuf_tensor(f"{name}_{self._sb_n}", shape, dtype))

    def sem(self):
        self._sem_n += 1
        h = self.nc.alloc_semaphore(f"s{self._sem_n}")
        self.scope_sems[-1].append(h)
        return h

    def sems(self, n):
        return [self.sem() for _ in range(n)]

    @contextlib.contextmanager
    def scope(self):
        es = contextlib.ExitStack()
        self.scopes.append(es)
        self.scope_sems.append([])
        try:
            with es:
                yield
        finally:
            self.scopes.pop()
            freed = self.scope_sems.pop()
            if freed:
                self.nc.clear_and_free_semaphores(freed)


def build(T=4096, dbg=False, upto=99, nocc=False):
    P = Prog(T, dbg)
    nc = P.nc
    NT = T // 128
    NCA = T // CA
    NCG = T // CK
    MAXC = max(T, 2048)

    P.dram("x0", [T, D], F32, "ExternalInput")
    for n, sh in WEIGHT_SHAPES.items():
        P.dram(n, sh, F32, "ExternalInput")
    for n, (sh, dt) in CONST_SHAPES.items():
        P.dram(n, sh, dt, "ExternalInput")
    P.dram("y", [T, D], F32, "ExternalOutput")

    P.scratch("sc_xh", [T, D], BF16)
    P.scratch("sc_x1", [T, D], F32)
    P.scratch("sc_x2", [T, D], F32)
    P.scratch("sc_x3", [T, D], F32)
    P.scratch("sc_q", [D, T], BF16)
    P.scratch("sc_k", [D, T], BF16)
    P.scratch("sc_v", [D, T], BF16)
    P.scratch("sc_kd", [D, T], BF16)
    P.scratch("sc_vt", [D, T], BF16)
    P.scratch("sc_o", [D, T], BF16)
    P.scratch("sc_h", [(DH // 128) * (T // 128), 128, 128], BF16)
    P.scratch("sc_wo_bf", [D, D], BF16)
    P.scratch("sc_wd_bf", [DH, D], BF16)
    P.scratch("sc_wob_bf", [4096, D], BF16)
    P.scratch("sc_qg", [D, T], BF16)
    P.scratch("sc_kg", [D, T], BF16)
    P.scratch("sc_vg", [HV * DV, T], BF16)
    P.scratch("sc_z", [HV * DV, T], BF16)
    P.scratch("sc_xgn", [HV * DV, T], BF16)
    P.scratch("sc_ba", [64, T], F32)
    P.scratch("sc_gba", [96, T], F32)
    P.scratch("sc_cm", [5, NCG // 2, 128, 32], F32)
    P.scratch("sc_X", [NCG * 16, 128, 256], BF16)
    P.scratch("sc_kd2", [NCG * 16, 128, 128], BF16)
    P.scratch("sc_kdn", [NCG * 16, 128, 128], BF16)
    P.scratch("sc_AT", [(NCG // 8) * 16, 128, 512], BF16)
    P.scratch("sc_qdT", [HV * (NCG // 8), 128, 512], BF16)
    P.dram("cc1_in", [HA * 128, 128], F32)
    P.dram("cc1_out", [2 * HA * 128, 128], F32)
    P.dram("cc2_in", [DH, 4], F32)
    P.dram("cc2_out", [2 * DH, 4], F32)
    P.dram("cc3_in", [HV * 128, 128], F32)
    P.dram("cc3_out", [2 * HV * 128, 128], F32)
    GROUPS = [[0, 1], [2, 3], [4, 5], [6, 7]]
    dd = P.d

    cs = {}
    for n in ["c_deck", "c_lamc", "c_ones_bf", "c_ident_bf", "c_ident_f32", "parity",
              "c_ubias_i", "c_ubias_s", "c_lbias_s",
              "b_q", "b_k", "b_v", "dt_bias", "A_log", "normw"]:
        if n in CONST_SHAPES:
            sh, dt = CONST_SHAPES[n]
        elif n == "b_v":
            sh, dt = [128, 16], F32
        else:
            sh, dt = WEIGHT_SHAPES[n], F32
        cs[n] = P.sbuf("sb_" + n, sh, dt)
    eps_ln = P.sbuf("eps_ln", [128, 1], F32)
    eps_l2 = P.sbuf("eps_l2", [128, 1], F32)
    eps_rms = P.sbuf("eps_rms", [128, 1], F32)


    pb = [P.es.enter_context(nc.psum_tensor(f"pb{i}", [128, 512], F32)) for i in range(8)]

    s_pre = P.sem()

    def phase_consts():
        s = P.sem()
        names = ["c_deck", "c_lamc", "c_ones_bf", "c_ident_bf", "c_ident_f32", "parity",
                 "c_ubias_i", "c_ubias_s", "c_lbias_s",
                 "b_q", "b_k", "b_v", "dt_bias", "A_log", "normw"]
        with nc.Block() as b:
            @b.sync
            def _(sync):
                for n in names:
                    sync.dma_start(cs[n][:, :], dd[n][:, :]).then_inc(s, 16)
                sync.wait_ge(s, 16 * len(names))

            @b.vector
            def _(ve):
                ve.memset(eps_ln[:, :], LN_EPS)
                ve.memset(eps_l2[:, :], L2_EPS)
                ve.memset(eps_rms[:, :], RMS_EPS)

            @b.gpsimd
            def _(gp):
                gp.dma_start(dd["sc_wo_bf"][:, :], dd["w_oT"][:, :]).then_inc(s_pre, 16)
                gp.dma_start(dd["sc_wd_bf"][:, :], dd["w_dT"][:, :]).then_inc(s_pre, 16)
                gp.dma_start(dd["sc_wob_bf"][:, :], dd["w_outB"][:, :]).then_inc(s_pre, 16)


    def phase_ln_standalone(srcname):
        lda = P.sems(2)
        sts = P.sems(2)
        ves, ses = P.sem(), P.sem()
        xt = P.sbuf("xt", [128, 2 * D], F32)
        xht = P.sbuf("xht", [128, 2 * D], BF16)
        stats = P.sbuf("stats", [128, 64], F32)
        aggr = P.sbuf("aggr", [128, 8], F32)
        with nc.Block() as b:
            @b.sync
            def _(sync):
                for i in range(NT):
                    bu = i % 2
                    if i >= 2:
                        sync.wait_ge(ves, 2 * (i - 1))
                    sync.dma_start(xt[:, bu * D:(bu + 1) * D],
                                   dd[srcname][i * 128:(i + 1) * 128, :]).then_inc(lda[bu], 16)

            @b.vector
            def _(ve):
                for i in range(NT):
                    bu = i % 2
                    ve.wait_ge(lda[bu], 16 * (i // 2 + 1))
                    for j in range(4):
                        ve.bn_stats(stats[:, bu * 32 + 6 * j:bu * 32 + 6 * j + 6],
                                    xt[:, bu * D + 512 * j:bu * D + 512 * j + 512])
                    ve.drain()
                    ve.bn_aggr(aggr[:, bu * 4:bu * 4 + 2], stats[:, bu * 32:bu * 32 + 24]).then_inc(ves, 1)
                    ve.wait_ge(ses, i + 1)
                    ve.reciprocal(aggr[:, bu * 4 + 3:bu * 4 + 4], aggr[:, bu * 4 + 2:bu * 4 + 3])
                    ve.drain()
                    if i >= 2:
                        ve.wait_ge(sts[bu], 16 * (i // 2))
                    ve.tensor_scalar(xht[:, bu * D:(bu + 1) * D], xt[:, bu * D:(bu + 1) * D],
                                     aggr[:, bu * 4:bu * 4 + 1], aggr[:, bu * 4 + 3:bu * 4 + 4],
                                     op0=ALU.subtract, op1=ALU.mult).then_inc(ves, 1)

            @b.scalar
            def _(sc):
                for i in range(NT):
                    bu = i % 2
                    sc.wait_ge(ves, 2 * i + 1)
                    sc.activation(aggr[:, bu * 4 + 2:bu * 4 + 3], aggr[:, bu * 4 + 1:bu * 4 + 2],
                                  AF.Sqrt, bias=eps_ln[:, 0:1], scale=1.0).then_inc(ses, 1)

            @b.gpsimd
            def _(gp):
                for i in range(NT):
                    bu = i % 2
                    gp.wait_ge(ves, 2 * (i + 1))
                    gp.dma_start(dd["sc_xh"][i * 128:(i + 1) * 128, :],
                                 xht[:, bu * D:(bu + 1) * D]).then_inc(sts[bu], 16)
                gp.wait_ge(sts[0], 16 * ((NT + 1) // 2))
                gp.wait_ge(sts[1], 16 * (NT // 2))

    def phase_xh_load(xh_fm):
        s = P.sem()
        with nc.Block() as b:
            @b.sync
            def _(sync):
                for t in range(16):
                    sync.dma_start_transpose(xh_fm[:, t * T:(t + 1) * T],
                                             dd["sc_xh"][:, t * 128:(t + 1) * 128]).then_inc(s, 16)
                sync.wait_ge(s, 256)

    def phase_fm_proj(xh_fm, wbuf, evb, wname, dout, dstname, bias_col=None, bias_off=0, row0=0):
        ndt = dout // 128
        nst = max(T // 512, 1)
        halves = 2 if nst >= 2 else 1
        nh = nst // halves
        ldw = P.sems(4)
        sts = P.sems(2)
        pes, ves = P.sem(), P.sem()
        wd = dd[wname]
        dst = dd[dstname]
        with nc.Block() as b:
            @b.gpsimd
            def _(gp):
                for t in range(ndt):
                    wb = t % 4
                    if t >= 4:
                        gp.wait_ge(pes, halves * (t - 3))
                    srcw = wd[:, t * 128:(t + 1) * 128].rearrange("(a p) m -> p a m", p=128)
                    dstw = wbuf[:, wb * 2048:(wb + 1) * 2048].rearrange("p (a m) -> p a m", a=16)
                    gp.dma_start(dstw, srcw).then_inc(ldw[wb], 16)

            @b.tensor
            def _(te):
                for t in range(ndt):
                    wb = t % 4
                    te.wait_ge(ldw[wb], 16 * (t // 4 + 1))
                    for half in range(halves):
                        g = halves * t + half
                        if g >= halves:
                            te.wait_ge(ves, g - halves + 1)
                        for k in range(16):
                            for s_ in range(nh):
                                si = half * nh + s_
                                mm = te.matmul(pb[half * 4 + s_][:, 0:512],
                                               wbuf[:, wb * 2048 + k * 128:wb * 2048 + (k + 1) * 128],
                                               xh_fm[:, k * T + si * 512:k * T + si * 512 + 512],
                                               start=(k == 0), stop=(k == 15))
                        mm.then_inc(pes, 1)

            @b.vector
            def _(ve):
                for t in range(ndt):
                    eb = (t % 2) * MAXC
                    for half in range(halves):
                        g = halves * t + half
                        ve.wait_ge(pes, g + 1)
                        if half == 0 and t >= 2:
                            ve.wait_ge(sts[t % 2], 16 * (t // 2))
                        for s_ in range(nh):
                            si = half * nh + s_
                            op = ve.tensor_scalar(evb[:, eb + si * 512:eb + (si + 1) * 512],
                                                  pb[half * 4 + s_][:, 0:512],
                                                  cs[bias_col][:, bias_off + t:bias_off + t + 1] if bias_col else 0.0,
                                                  None, op0=ALU.add)
                        op.then_inc(ves, 1)

            @b.sync
            def _(sync):
                for t in range(ndt):
                    eb = (t % 2) * MAXC
                    sync.wait_ge(ves, halves * (t + 1))
                    sync.dma_start(dst[row0 + t * 128:row0 + (t + 1) * 128, :],
                                   evb[:, eb:eb + T]).then_inc(sts[t % 2], 16)
                sync.wait_ge(sts[0], 16 * ((ndt + 1) // 2))
                sync.wait_ge(sts[1], 16 * (ndt // 2))

    def phase_lightning():
        NC = NCA
        nh2 = HA // 2
        dmask = P.sbuf("la_dmask", [CA, HA * CA], BF16)
        decq = P.sbuf("la_decq", [128, HA * CA], BF16)
        qf = [P.sbuf(f"la_qf{s}", [128, T], BF16) for s in range(2)]
        kf = [P.sbuf(f"la_kf{s}", [128, T], BF16) for s in range(2)]
        vf = [P.sbuf(f"la_vf{s}", [128, T], BF16) for s in range(2)]
        vt = [P.sbuf(f"la_vt{s}", [128, T], BF16) for s in range(2)]
        ktm = [P.sbuf(f"la_ktm{s}", [128, T], BF16) for s in range(2)]
        scT = [P.sbuf(f"la_scT{s}", [128, 128], BF16) for s in range(2)]
        qdc = [P.sbuf(f"la_qdc{s}", [128, 2 * 128], BF16) for s in range(2)]
        S32 = [P.sbuf(f"la_S32{s}", [128, 128], F32) for s in range(2)]
        Sbf = [P.sbuf(f"la_Sbf{s}", [128, 128], BF16) for s in range(2)]
        Sini = [P.sbuf(f"la_Sini{s}", [128, 128], F32) for s in range(2)]
        ob = [P.sbuf(f"la_ob{s}", [128, T], BF16) for s in range(2)]
        pT = [pb[6][:, :].bitcast(BF16), pb[7][:, :].bitcast(BF16)]

        sc_load = P.sem()

        def pass1():
            lda = [P.sems(1)[0], P.sems(1)[0]]
            peT = P.sems(2); peS = P.sems(2); peO = P.sems(2); peKV = P.sems(2)
            veK = P.sems(2); veS = P.sems(2); veV = P.sems(2); veO = P.sems(2); veKV = P.sems(2)
            sK = P.sems(2); sS = P.sems(2); sO = P.sems(2)
            with nc.Block() as b:
                @b.sync
                def _(sync):
                    for hi in range(nh2):
                        for s in range(2):
                            h = 2 * hi + s
                            if hi >= 1:
                                sync.wait_ge(veKV[s], hi * NC)
                            sync.dma_start(qf[s][:, :], dd["sc_q"][h * 128:(h + 1) * 128, :]).then_inc(lda[s], 16)
                            sync.dma_start(kf[s][:, :], dd["sc_k"][h * 128:(h + 1) * 128, :]).then_inc(lda[s], 16)
                            sync.dma_start(vf[s][:, :], dd["sc_v"][h * 128:(h + 1) * 128, :]).then_inc(lda[s], 16)

                @b.tensor
                def _(te):
                    for hi in range(nh2):
                        for c in range(NC):
                            for s in range(2):
                                i = hi * NC + c
                                if c == 0:
                                    te.wait_ge(lda[s], 48 * (hi + 1))
                                if i > 0:
                                    te.wait_ge(veK[s], i)
                                te.transpose(pT[s][:, 0:128], vf[s][:, c * 128:(c + 1) * 128],
                                             cs["c_ident_bf"][:, :]).then_inc(peT[s], 1)
                                if i > 0:
                                    te.wait_ge(veS[s], i)
                                te.matmul(pb[3 * s][:, 0:128], kf[s][:, c * 128:(c + 1) * 128],
                                          qf[s][:, c * 128:(c + 1) * 128],
                                          start=True, stop=True).then_inc(peS[s], 1)
                                te.wait_ge(veV[s], i + 1)
                                te.transpose(pT[s][:, 128:256], kf[s][:, c * 128:(c + 1) * 128],
                                             cs["c_ident_bf"][:, :]).then_inc(peT[s], 1)
                                te.wait_ge(veS[s], i + 1)
                                if i > 0:
                                    te.wait_ge(veO[s], i)
                                te.matmul(pb[3 * s + 1][:, 0:128], vt[s][:, c * 128:(c + 1) * 128],
                                          scT[s][:, 0:128], start=True, stop=True).then_inc(peO[s], 1)
                                te.wait_ge(veK[s], i + 1)
                                if i > 0:
                                    te.wait_ge(veKV[s], i)
                                te.matmul(pb[3 * s + 2][:, 0:128], ktm[s][:, c * 128:(c + 1) * 128],
                                          vt[s][:, c * 128:(c + 1) * 128],
                                          start=True, stop=True).then_inc(peKV[s], 1)

                @b.vector
                def _(ve):
                    ve.wait_ge(sc_load, 32)
                    for hi in range(nh2):
                        for c in range(NC):
                            for s in range(2):
                                h = 2 * hi + s
                                i = hi * NC + c
                                if c == 0 and hi >= 1:
                                    ve.wait_ge(sK[s], 32 * hi)
                                ve.wait_ge(peT[s], 2 * i + 1)
                                ve.tensor_copy(vt[s][:, c * 128:(c + 1) * 128], pT[s][:, 0:128]).then_inc(veV[s], 1)
                                ve.wait_ge(peT[s], 2 * i + 2)
                                ve.tensor_scalar(ktm[s][:, c * 128:(c + 1) * 128], pT[s][:, 128:256],
                                                 cs["c_deck"][:, h:h + 1], None, op0=ALU.mult).then_inc(veK[s], 1)
                                ve.wait_ge(peS[s], i + 1)
                                ve.tensor_tensor(scT[s][:, 0:128], pb[3 * s][:, 0:128],
                                                 dmask[:, h * 128:(h + 1) * 128],
                                                 op=ALU.mult).then_inc(veS[s], 1)
                                ve.wait_ge(peKV[s], i + 1)
                                if c == 0:
                                    if hi >= 1:
                                        ve.wait_ge(sS[s], 16 * hi)
                                    ve.tensor_copy(S32[s][:, :], pb[3 * s + 2][:, 0:128]).then_inc(veKV[s], 1)
                                else:
                                    ve.drain()
                                    ve.scalar_tensor_tensor(S32[s][:, :], S32[s][:, :],
                                                            cs["c_lamc"][:, h:h + 1], pb[3 * s + 2][:, 0:128],
                                                            op0=ALU.mult, op1=ALU.add).then_inc(veKV[s], 1)

                @b.scalar
                def _(sc):
                    sc.dma_start(dmask[:, :], dd["c_dmaskT"][:, :]).then_inc(sc_load, 16)
                    sc.dma_start(decq[:, :], dd["c_decq"][:, :]).then_inc(sc_load, 16)
                    for hi in range(nh2):
                        for c in range(NC):
                            for s in range(2):
                                h = 2 * hi + s
                                i = hi * NC + c
                                if c == 0 and hi >= 1:
                                    sc.wait_ge(sO[s], 16 * hi)
                                sc.wait_ge(peO[s], i + 1)
                                sc.copy(ob[s][:, c * 128:(c + 1) * 128],
                                        pb[3 * s + 1][:, 0:128]).then_inc(veO[s], 1)

                @b.gpsimd
                def _(gp):
                    for hi in range(nh2):
                        for s in range(2):
                            h = 2 * hi + s
                            gp.wait_ge(veK[s], (hi + 1) * NC)
                            gp.dma_start(dd["sc_kd"][h * 128:(h + 1) * 128, :], ktm[s][:, :]).then_inc(sK[s], 16)
                            gp.wait_ge(veV[s], (hi + 1) * NC)
                            gp.dma_start(dd["sc_vt"][h * 128:(h + 1) * 128, :], vt[s][:, :]).then_inc(sK[s], 16)
                            gp.wait_ge(veKV[s], (hi + 1) * NC)
                            gp.dma_start(dd["cc1_in"][h * 128:(h + 1) * 128, :], S32[s][:, :]).then_inc(sS[s], 16)
                            gp.wait_ge(veO[s], (hi + 1) * NC)
                            gp.dma_start(dd["sc_o"][h * 128:(h + 1) * 128, :], ob[s][:, :]).then_inc(sO[s], 16)
                    for s in range(2):
                        gp.wait_ge(sK[s], 32 * nh2)
                        gp.wait_ge(sS[s], 16 * nh2)
                        gp.wait_ge(sO[s], 16 * nh2)

        def xchg1():
            s = P.sem()
            with nc.Block() as b:
                @b.gpsimd
                def _(gp):
                    if nocc:
                        gp.dma_start(dd["cc1_out"][0:HA * 128, :], dd["cc1_in"][:, :]).then_inc(s, 16)
                        gp.wait_ge(s, 16)
                    else:
                        gp.collective_compute("AllGather", ALU.bypass,
                                              ins=[dd["cc1_in"][:, :].opt()],
                                              outs=[dd["cc1_out"][:, :].opt()],
                                              replica_groups=GROUPS).then_inc(s)
                        gp.wait_ge(s, 1)

        def pass2():
            lda = [P.sems(1)[0], P.sems(1)[0]]
            ldS = P.sems(2)
            peI = P.sems(2); peKV = P.sems(2)
            veQ = P.sems(2); veO = P.sems(2); veKV = P.sems(2); veSb = P.sems(2)
            sO = P.sems(2)
            with nc.Block() as b:
                @b.sync
                def _(sync):
                    for hi in range(nh2):
                        for s in range(2):
                            h = 2 * hi + s
                            if hi >= 1:
                                sync.wait_ge(veKV[s], hi * NC)
                                sync.wait_ge(sO[s], 16 * hi)
                            sync.dma_start(qf[s][:, :], dd["sc_q"][h * 128:(h + 1) * 128, :]).then_inc(lda[s], 16)
                            sync.dma_start(ktm[s][:, :], dd["sc_kd"][h * 128:(h + 1) * 128, :]).then_inc(lda[s], 16)
                            sync.dma_start(vt[s][:, :], dd["sc_vt"][h * 128:(h + 1) * 128, :]).then_inc(lda[s], 16)
                            sync.dma_start(ob[s][:, :], dd["sc_o"][h * 128:(h + 1) * 128, :]).then_inc(lda[s], 16)

                @b.scalar
                def _(sc):
                    for hi in range(nh2):
                        for s in range(2):
                            h = 2 * hi + s
                            if hi >= 1:
                                sc.wait_ge(veSb[s], (hi - 1) * (NC + 1) + 1)
                            sc.dma_start(Sini[s][:, :], dd["cc1_out"][h * 128:(h + 1) * 128, :]).then_inc(ldS[s], 16)

                @b.vector
                def _(ve):
                    for hi in range(nh2):
                        for c in range(NC):
                            for s in range(2):
                                h = 2 * hi + s
                                i = hi * NC + c
                                if c == 0:
                                    ve.wait_ge(ldS[s], 16 * (hi + 1))
                                    ve.tensor_scalar(S32[s][:, :], Sini[s][:, :], cs["parity"][:, 0:1],
                                                     None, op0=ALU.mult)
                                    ve.drain()
                                    ve.tensor_copy(Sbf[s][:, :], S32[s][:, :]).then_inc(veSb[s], 1)
                                ve.wait_ge(lda[s], 64 * (hi + 1))
                                if i > 0:
                                    ve.wait_ge(peI[s], i)
                                ve.tensor_tensor(qdc[s][:, (c % 2) * 128:(c % 2) * 128 + 128],
                                                 qf[s][:, c * 128:(c + 1) * 128],
                                                 decq[:, h * 128:(h + 1) * 128],
                                                 op=ALU.mult).then_inc(veQ[s], 1)
                                ve.wait_ge(peI[s], i + 1)
                                ve.tensor_tensor(ob[s][:, c * 128:(c + 1) * 128],
                                                 pb[3 * s + 1][:, 0:128],
                                                 ob[s][:, c * 128:(c + 1) * 128],
                                                 op=ALU.add).then_inc(veO[s], 1)
                                ve.wait_ge(peKV[s], i + 1)
                                ve.drain()
                                ve.scalar_tensor_tensor(S32[s][:, :], S32[s][:, :],
                                                        cs["c_lamc"][:, h:h + 1], pb[3 * s + 2][:, 0:128],
                                                        op0=ALU.mult, op1=ALU.add).then_inc(veKV[s], 1)
                                ve.drain()
                                ve.tensor_copy(Sbf[s][:, :], S32[s][:, :]).then_inc(veSb[s], 1)

                @b.tensor
                def _(te):
                    for hi in range(nh2):
                        for c in range(NC):
                            for s in range(2):
                                i = hi * NC + c
                                te.wait_ge(veSb[s], i + hi + 1)
                                te.wait_ge(veQ[s], i + 1)
                                if i > 0:
                                    te.wait_ge(veO[s], i)
                                te.matmul(pb[3 * s + 1][:, 0:128], Sbf[s][:, :],
                                          qdc[s][:, (c % 2) * 128:(c % 2) * 128 + 128],
                                          start=True, stop=True).then_inc(peI[s], 1)
                                if i > 0:
                                    te.wait_ge(veKV[s], i)
                                te.matmul(pb[3 * s + 2][:, 0:128], ktm[s][:, c * 128:(c + 1) * 128],
                                          vt[s][:, c * 128:(c + 1) * 128],
                                          start=True, stop=True).then_inc(peKV[s], 1)

                @b.gpsimd
                def _(gp):
                    for hi in range(nh2):
                        for s in range(2):
                            h = 2 * hi + s
                            gp.wait_ge(veO[s], (hi + 1) * NC)
                            gp.dma_start(dd["sc_o"][h * 128:(h + 1) * 128, :], ob[s][:, :]).then_inc(sO[s], 16)
                    gp.wait_ge(sO[0], 16 * nh2)
                    gp.wait_ge(sO[1], 16 * nh2)

        pass1()
        xchg1()
        pass2()

    def phase_tm_proj_ln(tag, stat_name, nk, w_bf_name, bias_rep_name, xold_name, xnew_name,
                         final_out=False, stat_tiled=False):
        ldw = P.sems(2)
        lds = P.sems(2)
        ldx = P.sems(4)
        ldbr = P.sem()
        pes, vesP, vesL, ses = P.sem(), P.sem(), P.sem(), P.sem()
        sto = P.sems(2)
        KG = 4
        npc = nk // KG
        wpc = P.sbuf(f"wpc_{tag}", [128, 2 * KG * 2048], BF16)
        xnw = P.sbuf(f"xnw_{tag}", [128, 2 * 2 * D], F32)
        xht = P.sbuf(f"xht_{tag}", [128, 2 * D], BF16)
        stats = P.sbuf(f"stats_{tag}", [128, 64], F32)
        aggr = P.sbuf(f"aggr_{tag}", [128, 8], F32)
        stb = P.sbuf(f"stb_{tag}", [128, 2 * nk * 256], BF16)
        brep = P.sbuf(f"brep_{tag}", [128, D], F32) if bias_rep_name else None
        xold = P.sbuf(f"xold_{tag}", [128, 2 * 2 * D], F32)
        wsrc = dd[w_bf_name]
        ssrc = dd[stat_name]
        NFL = NT // 2
        per = 1 if final_out else 2
        with nc.Block() as b:
            @b.sync
            def _(sync):
                sync.wait_ge(s_pre, 48)
                if bias_rep_name:
                    sync.dma_start(brep[:, :], dd[bias_rep_name][:, :]).then_inc(ldbr, 16)
                npt = 0
                for fl in range(NFL):
                    slot = fl % 2
                    if fl >= 2:
                        sync.wait_ge(pes, npc * (fl - 1))
                    ssl = stb[:, slot * nk * 256:(slot + 1) * nk * 256].rearrange(
                        "p (k c) -> p k c", k=nk)
                    if stat_tiled:
                        for tt in range(2):
                            t = 2 * fl + tt
                            sync.dma_start(ssl[:, :, tt * 128:(tt + 1) * 128],
                                           ssrc[t:nk * NT:NT, :, :].rearrange("k p c -> p k c")) \
                                .then_inc(lds[slot], 16)
                    else:
                        sync.dma_start(ssl, ssrc[:, 2 * fl * 128:(2 * fl + 2) * 128]
                                       .rearrange("(k p) c -> p k c", p=128)).then_inc(lds[slot], 16)
                    for tt in range(2):
                        t = 2 * fl + tt
                        if fl >= 2:
                            sync.wait_ge(vesL, 4 * (fl - 1))
                        sync.dma_start(xold[:, (slot * 2 + tt) * D:(slot * 2 + tt + 1) * D],
                                       dd[xold_name][t * 128:(t + 1) * 128, :]).then_inc(ldx[slot * 2 + tt], 16)
                    for pc in range(npc):
                        wslot = npt % 2
                        if npt >= 2:
                            sync.wait_ge(pes, npt - 1)
                        wsl = wpc[:, wslot * KG * 2048:(wslot + 1) * KG * 2048].rearrange("p (a m) -> p a m", a=KG)
                        sync.dma_start(wsl, wsrc[pc * KG * 128:(pc + 1) * KG * 128, :]
                                       .rearrange("(a p) m -> p a m", p=128)).then_inc(ldw[wslot], 16)
                        npt += 1

            @b.tensor
            def _(te):
                stper = 2 if stat_tiled else 1
                npt = 0
                for fl in range(NFL):
                    slot = fl % 2
                    te.wait_ge(lds[slot], 16 * stper * (fl // 2 + 1))
                    for pc in range(npc):
                        wslot = npt % 2
                        te.wait_ge(ldw[wslot], 16 * (npt // 2 + 1))
                        if pc == 0 and fl >= 1:
                            te.wait_ge(vesP, 2 * fl)
                        for kk in range(KG):
                            k = pc * KG + kk
                            for tt in range(2):
                                for q in range(4):
                                    mm = te.matmul(pb[tt * 4 + q][:, 0:512],
                                                   stb[:, slot * nk * 256 + k * 256 + tt * 128:
                                                       slot * nk * 256 + k * 256 + (tt + 1) * 128],
                                                   wpc[:, wslot * KG * 2048 + kk * 2048 + q * 512:
                                                       wslot * KG * 2048 + kk * 2048 + (q + 1) * 512],
                                                   start=(k == 0), stop=(k == nk - 1))
                        mm.then_inc(pes, 1)
                        npt += 1

            @b.vector
            def _(ve):
                if bias_rep_name:
                    ve.wait_ge(ldbr, 16)
                for fl in range(NFL):
                    slot = fl % 2
                    ve.wait_ge(pes, npc * (fl + 1))
                    ve.wait_ge(ldx[slot * 2], 16 * (fl // 2 + 1))
                    ve.wait_ge(ldx[slot * 2 + 1], 16 * (fl // 2 + 1))
                    for tt in range(2):
                        t = 2 * fl + tt
                        bu = slot * 2 + tt
                        if fl >= 2:
                            ve.wait_ge(vesL, 4 * (fl - 1))
                        for q in range(4):
                            op = ve.tensor_tensor(xnw[:, bu * D + q * 512:bu * D + (q + 1) * 512],
                                                  pb[tt * 4 + q][:, 0:512],
                                                  xold[:, bu * D + q * 512:bu * D + (q + 1) * 512],
                                                  op=ALU.add)
                        op.then_inc(vesP, 1)
                    for tt in range(2):
                        t = 2 * fl + tt
                        bu = slot * 2 + tt
                        xb = bu % 2
                        if bias_rep_name:
                            ve.drain()
                            for q in range(4):
                                ve.tensor_tensor(xnw[:, bu * D + q * 512:bu * D + (q + 1) * 512],
                                                 xnw[:, bu * D + q * 512:bu * D + (q + 1) * 512],
                                                 brep[:, q * 512:(q + 1) * 512], op=ALU.add)
                        ve.drain()
                        if final_out:
                            ve.tensor_copy(aggr[:, 4 * xb:4 * xb + 1], xnw[:, bu * D:bu * D + 1]).then_inc(vesL, 2)
                            continue
                        for j in range(4):
                            ve.bn_stats(stats[:, xb * 32 + 6 * j:xb * 32 + 6 * j + 6],
                                        xnw[:, bu * D + 512 * j:bu * D + 512 * j + 512])
                        ve.drain()
                        ve.bn_aggr(aggr[:, xb * 4:xb * 4 + 2], stats[:, xb * 32:xb * 32 + 24]).then_inc(vesL, 1)
                        ve.wait_ge(ses, t + 1)
                        ve.reciprocal(aggr[:, xb * 4 + 3:xb * 4 + 4], aggr[:, xb * 4 + 2:xb * 4 + 3])
                        ve.drain()
                        if fl >= 1:
                            ve.wait_ge(sto[tt], (16 * per) * fl)
                        ve.tensor_scalar(xht[:, xb * D:(xb + 1) * D], xnw[:, bu * D:(bu + 1) * D],
                                         aggr[:, xb * 4:xb * 4 + 1], aggr[:, xb * 4 + 3:xb * 4 + 4],
                                         op0=ALU.subtract, op1=ALU.mult).then_inc(vesL, 1)

            @b.scalar
            def _(sc):
                if not final_out:
                    for t in range(NT):
                        xb = t % 2
                        sc.wait_ge(vesL, 2 * t + 1)
                        sc.activation(aggr[:, xb * 4 + 2:xb * 4 + 3], aggr[:, xb * 4 + 1:xb * 4 + 2],
                                      AF.Sqrt, bias=eps_ln[:, 0:1], scale=1.0).then_inc(ses, 1)

            @b.gpsimd
            def _(gp):
                for fl in range(NFL):
                    slot = fl % 2
                    for tt in range(2):
                        t = 2 * fl + tt
                        bu = slot * 2 + tt
                        xb = bu % 2
                        gp.wait_ge(vesL, 4 * fl + 2 * (tt + 1))
                        if final_out:
                            gp.dma_start(dd["y"][t * 128:(t + 1) * 128, :],
                                         xnw[:, bu * D:(bu + 1) * D]).then_inc(sto[tt], 16)
                        else:
                            gp.dma_start(dd[xnew_name][t * 128:(t + 1) * 128, :],
                                         xnw[:, bu * D:(bu + 1) * D]).then_inc(sto[tt], 16)
                            gp.dma_start(dd["sc_xh"][t * 128:(t + 1) * 128, :],
                                         xht[:, xb * D:(xb + 1) * D]).then_inc(sto[tt], 16)
                gp.wait_ge(sto[0], (16 * per) * NFL)
                gp.wait_ge(sto[1], (16 * per) * NFL)

    def phase_ffn_gateup(xh_fm, wbuf):
        NHT = DH // 128
        ldg = P.sems(2)
        ldu = P.sems(2)
        sth = P.sems(2)
        pes, ves, ses = P.sem(), P.sem(), P.sem()
        hstage = P.sbuf("hstage", [128, 2 * 1024], BF16)
        sstage = P.sbuf("sstage", [128, 2 * 1024], BF16)
        ngr = max(T // 1024, 1)
        nstr = min(2, max(T // 512, 1))
        with nc.Block() as b:
            @b.gpsimd
            def _(gp):
                for ht in range(NHT):
                    wb = ht % 2
                    if ht >= 2:
                        gp.wait_ge(pes, ngr * (ht - 1))
                    sg = dd["w_gT"][:, ht * 128:(ht + 1) * 128].rearrange("(a p) m -> p a m", p=128)
                    su = dd["w_uT"][:, ht * 128:(ht + 1) * 128].rearrange("(a p) m -> p a m", p=128)
                    gp.dma_start(wbuf[:, wb * 2048:(wb + 1) * 2048].rearrange("p (a m) -> p a m", a=16),
                                 sg).then_inc(ldg[wb], 16)
                    gp.dma_start(wbuf[:, (2 + wb) * 2048:(3 + wb) * 2048].rearrange("p (a m) -> p a m", a=16),
                                 su).then_inc(ldu[wb], 16)

            @b.tensor
            def _(te):
                for ht in range(NHT):
                    wb = ht % 2
                    te.wait_ge(ldg[wb], 16 * (ht // 2 + 1))
                    te.wait_ge(ldu[wb], 16 * (ht // 2 + 1))
                    for g in range(ngr):
                        gi = ht * ngr + g
                        bb = (gi % 2) * 4
                        if gi >= 2:
                            te.wait_ge(ves, gi - 1)
                        for k in range(16):
                            for s_ in range(nstr):
                                col = (2 * g + s_) * 512
                                mm = te.matmul(pb[bb + s_][:, 0:512],
                                               wbuf[:, wb * 2048 + k * 128:wb * 2048 + (k + 1) * 128],
                                               xh_fm[:, k * T + col:k * T + col + 512],
                                               start=(k == 0), stop=(k == 15))
                                mm2 = te.matmul(pb[bb + 2 + s_][:, 0:512],
                                                wbuf[:, (2 + wb) * 2048 + k * 128:(2 + wb) * 2048 + (k + 1) * 128],
                                                xh_fm[:, k * T + col:k * T + col + 512],
                                                start=(k == 0), stop=(k == 15))
                        mm2.then_inc(pes, 1)

            @b.scalar
            def _(sc):
                for ht in range(NHT):
                    for g in range(ngr):
                        gi = ht * ngr + g
                        bb = (gi % 2) * 4
                        sl = gi % 2
                        sc.wait_ge(pes, gi + 1)
                        if gi >= 2:
                            sc.wait_ge(ves, gi - 1)
                        for s_ in range(nstr):
                            sc.activation(sstage[:, sl * 1024 + s_ * 512:sl * 1024 + (s_ + 1) * 512],
                                          pb[bb + s_][:, 0:512], AF.Silu).then_inc(ses, 1)

            @b.vector
            def _(ve):
                for ht in range(NHT):
                    for g in range(ngr):
                        gi = ht * ngr + g
                        bb = (gi % 2) * 4
                        sl = gi % 2
                        ve.wait_ge(ses, nstr * (gi + 1))
                        if gi >= 2:
                            ve.wait_ge(sth[sl], 16 * (gi // 2))
                        for s_ in range(nstr):
                            op = ve.tensor_tensor(hstage[:, sl * 1024 + s_ * 512:sl * 1024 + (s_ + 1) * 512],
                                                  pb[bb + 2 + s_][:, 0:512],
                                                  sstage[:, sl * 1024 + s_ * 512:sl * 1024 + (s_ + 1) * 512],
                                                  op=ALU.mult)
                        op.then_inc(ves, 1)

            @b.sync
            def _(sync):
                for ht in range(NHT):
                    for g in range(ngr):
                        gi = ht * ngr + g
                        sl = gi % 2
                        nt_g = nstr * 4
                        sync.wait_ge(ves, gi + 1)
                        dst = dd["sc_h"][ht * NT + g * nt_g:ht * NT + (g + 1) * nt_g, :, :] \
                            .rearrange("t p c -> p t c")
                        sync.dma_start(dst, hstage[:, sl * 1024:sl * 1024 + nt_g * 128]
                                       .rearrange("p (t c) -> p t c", c=128)).then_inc(sth[sl], 16)
                sync.wait_ge(sth[0], 16 * ((NHT * ngr + 1) // 2))
                sync.wait_ge(sth[1], 16 * ((NHT * ngr) // 2))

    def phase_gdn_qkvz(xh_fm, wbuf, own3_sb):
        NCT = 96
        nst = T // 512
        spq = min(2, nst)
        nqt = max(nst // spq, 1)
        ldw = P.sems(4)
        ldc = P.sem()
        peB, veB = P.sem(), P.sem()
        peA, veA, veD, veM, peC, seC = P.sem(), P.sem(), P.sem(), P.sem(), P.sem(), P.sem()
        stQ = P.sems(2)
        stH = P.sems(2)
        stB = P.sem()
        ba_st = P.sbuf("ba_st", [64, 2 * 512], F32)
        mixp = P.sbuf("mixp", [128, 2 * (T + 3)], BF16)
        dstat = P.sbuf("dstat", [128, 2 * 4 * 128], BF16)
        cvw = P.sbuf("cvw", [128, 256], F32)
        wba_sb = P.sbuf("wba_sb", [128, 1024], BF16)
        ldc2 = P.sem()
        halo3 = P.sbuf("halo3", [128, 2 * 4], BF16)
        evq = P.sbuf("evq", [128, 2 * T], BF16)
        NCV = 64

        with nc.Block() as b:
            @b.gpsimd
            def _(gp):
                gp.dma_start(wba_sb[:, :].rearrange("p (a m) -> p a m", a=16),
                             dd["w_baT"][:, :].rearrange("(a p) m -> p a m", p=128)).then_inc(ldc2, 16)
                for si in range(nst):
                    sl = si % 2
                    gp.wait_ge(veB, si + 1)
                    gp.dma_start(dd["sc_ba"][0:64, si * 512:(si + 1) * 512],
                                 ba_st[0:64, sl * 512:(sl + 1) * 512]).then_inc(stB, 16)
                for ct in range(NCT):
                    wb = ct % 4
                    if ct >= 4:
                        gp.wait_ge(peA, nqt * (ct - 3))
                    srcw = dd["w_qkvzT"][:, ct * 128:(ct + 1) * 128].rearrange("(a p) m -> p a m", p=128)
                    gp.dma_start(wbuf[:, wb * 2048:(wb + 1) * 2048].rearrange("p (a m) -> p a m", a=16),
                                 srcw).then_inc(ldw[wb], 16)
                    hc = ct - 2
                    if 0 <= hc < NCV:
                        gp.wait_ge(veM, hc + 1)
                        gp.dma_start(dd["cc2_in"][hc * 128:(hc + 1) * 128, 0:4],
                                     halo3[:, (hc % 2) * 4:(hc % 2) * 4 + 4]).then_inc(stH[hc % 2], 16)
                for hc in range(NCT - 2, NCV):
                    gp.wait_ge(veM, hc + 1)
                    gp.dma_start(dd["cc2_in"][hc * 128:(hc + 1) * 128, 0:4],
                                 halo3[:, (hc % 2) * 4:(hc % 2) * 4 + 4]).then_inc(stH[hc % 2], 16)
                gp.wait_ge(stH[0], 16 * (NCV // 2))
                gp.wait_ge(stH[1], 16 * (NCV // 2))
                gp.wait_ge(stB, 16 * nst)

            @b.sync
            def _(sync):
                sync.dma_start(cvw[:, :], dd["convw"][:, :]).then_inc(ldc, 16)
                for ct in range(NCT):
                    sl = ct % 2
                    if ct < NCV:
                        sync.wait_ge(seC, nst * (ct + 1))
                    else:
                        sync.wait_ge(veA, nqt * (ct + 1))
                    dn = "sc_qg" if ct < 16 else ("sc_kg" if ct < 32 else ("sc_vg" if ct < NCV else "sc_z"))
                    r0 = (ct % 16) * 128 if ct < 32 else ((ct - 32) * 128 if ct < NCV else (ct - NCV) * 128)
                    sync.dma_start(dd[dn][r0:r0 + 128, :], evq[:, sl * T:sl * T + T]).then_inc(stQ[sl], 16)
                sync.wait_ge(stQ[0], 16 * (NCT // 2))
                sync.wait_ge(stQ[1], 16 * (NCT // 2))

            @b.tensor
            def _(te):
                te.wait_ge(ldc2, 16)
                for si in range(nst):
                    if si >= 1:
                        te.wait_ge(veB, si)
                    for k in range(16):
                        mm = te.matmul(pb[6][0:64, 0:512], wba_sb[:, k * 64:(k + 1) * 64],
                                       xh_fm[:, k * T + si * 512:k * T + si * 512 + 512],
                                       start=(k == 0), stop=(k == 15))
                    mm.then_inc(peB, 1)
                for ct in range(NCT):
                    wb = ct % 4
                    sl = ct % 2
                    te.wait_ge(ldw[wb], 16 * (ct // 4 + 1))
                    rd = min(nqt, 2)
                    for qt in range(nqt):
                        g = nqt * ct + qt
                        if g >= rd:
                            te.wait_ge(veA, g - rd + 1)
                        for k in range(16):
                            for s_ in range(spq):
                                si = spq * qt + s_
                                mm = te.matmul(pb[(qt % 2) * 2 + s_][:, 0:512],
                                               wbuf[:, wb * 2048 + k * 128:wb * 2048 + (k + 1) * 128],
                                               xh_fm[:, k * T + si * 512:k * T + si * 512 + 512],
                                               start=(k == 0), stop=(k == 15))
                        mm.then_inc(peA, 1)
                    if ct < NCV:
                        te.wait_ge(veM, ct + 1)
                        te.wait_ge(veD, ct + 1)
                        for s_ in range(nst):
                            ci = ct * nst + s_
                            if ci >= 4:
                                te.wait_ge(seC, ci - 3)
                            for j in range(4):
                                mm = te.matmul(pb[4 + ci % 4][:, 0:512],
                                               dstat[:, sl * 512 + j * 128:sl * 512 + (j + 1) * 128],
                                               mixp[:, sl * (T + 3) + s_ * 512 + j:sl * (T + 3) + s_ * 512 + j + 512],
                                               start=(j == 0), stop=(j == 3))
                            mm.then_inc(peC, 1)

            @b.vector
            def _(ve):
                ve.memset(halo3[:, :], 0.0)
                for si in range(nst):
                    sl = si % 2
                    ve.wait_ge(peB, si + 1)
                    if si >= 2:
                        ve.wait_ge(stB, 16 * (si - 1))
                    ve.tensor_copy(ba_st[0:64, sl * 512:(sl + 1) * 512], pb[6][0:64, 0:512]).then_inc(veB, 1)
                for ct in range(NCT):
                    sl = ct % 2
                    if ct < NCV:
                        if ct == 0:
                            ve.wait_ge(ldc, 16)
                        if ct >= 2:
                            ve.wait_ge(peC, nst * (ct - 1))
                        for j in range(4):
                            ve.tensor_scalar(dstat[:, sl * 512 + j * 128:sl * 512 + (j + 1) * 128],
                                             cs["c_ident_bf"][:, :], cvw[:, ct * 4 + j:ct * 4 + j + 1],
                                             None, op0=ALU.mult)
                        ve.drain()
                        ve.nop().then_inc(veD, 1)
                    for qt in range(nqt):
                        g = nqt * ct + qt
                        ve.wait_ge(peA, g + 1)
                        if ct < NCV:
                            if ct >= 2 and qt == 0:
                                ve.wait_ge(peC, nst * (ct - 1))
                            for s_ in range(spq):
                                si = spq * qt + s_
                                op = ve.tensor_copy(
                                    mixp[:, sl * (T + 3) + 3 + si * 512:sl * (T + 3) + 3 + (si + 1) * 512],
                                    pb[(qt % 2) * 2 + s_][:, 0:512])
                            op.then_inc(veA, 1)
                        else:
                            if ct >= 2 and qt == 0:
                                ve.wait_ge(stQ[sl], 16 * ((ct - 2) // 2 + 1))
                            for s_ in range(spq):
                                si = spq * qt + s_
                                op = ve.tensor_copy(evq[:, sl * T + si * 512:sl * T + (si + 1) * 512],
                                                    pb[(qt % 2) * 2 + s_][:, 0:512])
                            op.then_inc(veA, 1)
                    if ct < NCV:
                        ve.drain()
                        if ct >= 2:
                            ve.wait_ge(stH[ct % 2], 16 * (ct // 2))
                        ve.memset(mixp[:, sl * (T + 3):sl * (T + 3) + 3], 0.0)
                        ve.tensor_copy(own3_sb[:, ct * 4:ct * 4 + 3],
                                       mixp[:, sl * (T + 3) + 3:sl * (T + 3) + 6])
                        ve.tensor_copy(halo3[:, sl * 4:sl * 4 + 3],
                                       mixp[:, sl * (T + 3) + T:sl * (T + 3) + T + 3])
                        ve.drain()
                        ve.nop().then_inc(veM, 1)

            @b.scalar
            def _(sc):
                for ct in range(NCV):
                    sl = ct % 2
                    for s_ in range(nst):
                        ci = ct * nst + s_
                        sc.wait_ge(peC, ci + 1)
                        if s_ == 0 and ct >= 2:
                            sc.wait_ge(stQ[sl], 16 * ((ct - 2) // 2 + 1))
                        sc.activation(evq[:, sl * T + s_ * 512:sl * T + (s_ + 1) * 512],
                                      pb[4 + ci % 4][:, 0:512], AF.Silu).then_inc(seC, 1)

    def phase_gdn_fixup(own3_sb):
        s = P.sem()
        lda, ves, ses, sto = P.sem(), P.sem(), P.sem(), P.sem()
        hal = P.sbuf("hal", [128, 4], F32)
        w6 = P.sbuf("w6", [128, 8], BF16)
        acc = P.sbuf("acc", [128, 4], F32)
        fx = P.sbuf("fx", [128, 4], BF16)
        cvw = P.sbuf("cvw2", [128, 256], F32)
        NCV = 64
        with nc.Block() as b:
            @b.gpsimd
            def _(gp):
                gp.collective_compute("AllGather", ALU.bypass,
                                      ins=[dd["cc2_in"][:, :].opt()],
                                      outs=[dd["cc2_out"][:, :].opt()],
                                      replica_groups=GROUPS).then_inc(s)
                gp.wait_ge(s, 1)

        with nc.Block() as b2:
            @b2.sync
            def _(sync):
                sync.dma_start(cvw[:, :], dd["convw"][:, :]).then_inc(lda, 16)
                for ct in range(NCV):
                    if ct >= 1:
                        sync.wait_ge(ves, 2 * ct)
                    sync.dma_start(hal[:, 0:3], dd["cc2_out"][ct * 128:(ct + 1) * 128, 0:3]).then_inc(lda, 16)

            @b2.vector
            def _(ve):
                for ct in range(NCV):
                    ve.wait_ge(lda, 16 * (ct + 2))
                    ve.tensor_scalar(w6[:, 0:3], hal[:, 0:3], cs["parity"][:, 0:1], None,
                                     op0=ALU.mult).then_inc(ves, 1)
                    ve.tensor_copy(w6[:, 3:6], own3_sb[:, ct * 4:ct * 4 + 3])
                    ve.drain()
                    if ct >= 1:
                        ve.wait_ge(ses, ct)
                    ve.tensor_scalar(acc[:, 0:3], w6[:, 0:3], cvw[:, ct * 4:ct * 4 + 1], None, op0=ALU.mult)
                    ve.drain()
                    for j in range(1, 4):
                        ve.scalar_tensor_tensor(acc[:, 0:3], w6[:, j:j + 3], cvw[:, ct * 4 + j:ct * 4 + j + 1],
                                                acc[:, 0:3], op0=ALU.mult, op1=ALU.add)
                        ve.drain()
                    if ct >= 2:
                        ve.wait_ge(sto, 16 * (ct - 1))
                    ve.nop().then_inc(ves, 1)

            @b2.scalar
            def _(sc):
                for ct in range(NCV):
                    sc.wait_ge(ves, 2 * ct + 2)
                    if ct >= 1:
                        sc.wait_ge(sto, 16 * ct)
                    sc.activation(fx[:, 0:3], acc[:, 0:3], AF.Silu).then_inc(ses, 1)

            @b2.gpsimd
            def _(gp):
                for ct in range(NCV):
                    gp.wait_ge(ses, ct + 1)
                    dn = "sc_qg" if ct < 16 else ("sc_kg" if ct < 32 else "sc_vg")
                    r0 = (ct % 16) * 128 if ct < 32 else (ct - 32) * 128
                    gp.dma_start(dd[dn][r0:r0 + 128, 0:3], fx[:, 0:3]).then_inc(sto, 16)
                gp.wait_ge(sto, 16 * NCV)

    def phase_gdn_prep(cm2, egC_sb, egl):
        TC = min(1024, T)
        NTC = T // TC
        NP8 = TC // 128
        CQ = NCG * 16
        lda, pet, vcp, stg, stc = P.sem(), P.sem(), P.sem(), P.sem(), P.sem()
        ba2 = P.sbuf("g3_ba2", [32, 2 * TC], F32)
        cmstage = P.sbuf("g3_cmstage", [128, 5 * 32], F32)
        tmp = P.sbuf("g3_tmp", [32, 8 * TC], F32)
        onec = P.sbuf("g3_onec", [128, 1], F32)
        negc = P.sbuf("g3_negc", [128, 1], F32)
        eac = P.sbuf("g3_eac", [32, 4], F32)

        def tc_(q):
            return tmp[0:32, q * TC:(q + 1) * TC]

        with nc.Block() as b:
            @b.vector
            def _(ve):
                ve.memset(onec[:, :], 1.0)
                ve.memset(negc[:, :], -1.0)
            @b.scalar
            def _(sc):
                sc.activation(eac[:, 0:1], cs["A_log"][0:32, 0:1], AF.Exp)
                sc.drain()
                sc.mul(eac[:, 1:2], eac[:, 0:1], -1.0)

        for i in range(NTC):
            bv = ba2[0:32, 0:TC]
            av = ba2[0:32, TC:2 * TC]
            with nc.Block() as bL:
                @bL.sync
                def _(sync):
                    sync.dma_start(bv, dd["sc_ba"][0:32, i * TC:(i + 1) * TC]).then_inc(lda, 16)
                    sync.dma_start(av, dd["sc_ba"][32:64, i * TC:(i + 1) * TC]).then_inc(lda, 16)
                    sync.wait_ge(lda, 32 * (i + 1))
            with nc.Block() as bS1:
                @bS1.scalar
                def _(sc):
                    sc.activation(tc_(0), bv, AF.Exp, scale=-1.0)
                    sc.activation(tc_(1), bv, AF.Abs)
                    sc.drain()
                    sc.activation(tc_(1), tc_(1), AF.Exp, scale=-1.0)
                    sc.drain()
                    sc.activation(tc_(1), tc_(1), AF.Ln, bias=onec[0:32, 0:1])
            with nc.Block() as bV1:
                @bV1.vector
                def _(ve):
                    ve.tensor_scalar(tc_(0), tc_(0), 1.0, None, op0=ALU.add)
                    ve.drain()
                    ve.reciprocal(tc_(0), tc_(0))
                    ve.tensor_scalar(tc_(3), bv, negc[0:32, 0:1], 0.0, op0=ALU.mult, op1=ALU.max)
                    ve.drain()
                    ve.tensor_tensor(tc_(1), tc_(3), tc_(1), op=ALU.add)
                    ve.drain()
                    ve.tensor_scalar(tc_(1), tc_(1), negc[0:32, 0:1], None, op0=ALU.mult)
                    ve.tensor_scalar(tc_(4), av, cs["dt_bias"][0:32, 0:1], None, op0=ALU.add)
            with nc.Block() as bS2:
                @bS2.scalar
                def _(sc):
                    sc.activation(tc_(2), tc_(4), AF.Abs)
                    sc.drain()
                    sc.activation(tc_(2), tc_(2), AF.Exp, scale=-1.0)
                    sc.drain()
                    sc.activation(tc_(2), tc_(2), AF.Ln, bias=onec[0:32, 0:1])
            with nc.Block() as bV2:
                @bV2.vector
                def _(ve):
                    ve.tensor_scalar(tc_(7), tc_(4), 0.0, None, op0=ALU.max)
                    ve.drain()
                    ve.tensor_tensor(tc_(2), tc_(7), tc_(2), op=ALU.add)
                    ve.drain()
                    ve.tensor_scalar(tc_(2), tc_(2), eac[0:32, 1:2], None, op0=ALU.mult)
                    ve.drain()
                    for n in range(TC // CK):
                        ve.tensor_tensor_scan(tmp[0:32, 3 * TC + n * CK:3 * TC + (n + 1) * CK],
                                              tmp[0:32, 2 * TC + n * CK:2 * TC + (n + 1) * CK],
                                              tmp[0:32, 2 * TC + n * CK:2 * TC + (n + 1) * CK],
                                              0.0, op0=ALU.add, op1=ALU.bypass)
                    ve.drain()
                    ve.tensor_tensor(tc_(4), tc_(3) if False else tmp[0:32, 3 * TC:4 * TC], tc_(1), op=ALU.add)
                    glv = tmp[0:32, 3 * TC:4 * TC].rearrange("p (n c) -> p n c", c=CK)[:, :, CK - 1:CK] \
                        .broadcast_to([32, TC // CK, CK])
                    gv3 = tmp[0:32, 3 * TC:4 * TC].rearrange("p (n c) -> p n c", c=CK)
                    ve.tensor_tensor(tc_(7).rearrange("p (n c) -> p n c", c=CK), glv, gv3, op=ALU.subtract)
            with nc.Block() as bS3:
                @bS3.scalar
                def _(sc):
                    sc.activation(tc_(5), tc_(3), AF.Exp)
                    sc.activation(tc_(7), tc_(7), AF.Exp)
                    sc.activation(egl[0:32, i * NP8 * 2:(i + 1) * NP8 * 2],
                                  tmp[0:32, 3 * TC + CK - 1:4 * TC:CK], AF.Exp)
            with nc.Block() as bV3:
                @bV3.vector
                def _(ve):
                    ve.tensor_tensor(tc_(6), tc_(0), tc_(5), op=ALU.mult)
            with nc.Block() as bC:
                @bC.tensor
                def _(te):
                    for p8 in range(NP8):
                        for qi, q in enumerate([3, 4, 0, 6, 7]):
                            gi = (i * NP8 + p8) * 5 + qi
                            m5, q5 = divmod(gi - 2, 5)
                            if gi >= 2:
                                te.wait_ge(vcp, m5 * 6 + q5 + 1)
                            te.transpose(pb[gi % 2][:, 0:32],
                                         tmp[0:32, q * TC + p8 * 128:q * TC + (p8 + 1) * 128],
                                         cs["c_ident_f32"][0:32, 0:32]).then_inc(pet, 1)

                @bC.vector
                def _(ve):
                    for p8 in range(NP8):
                        m = i * NP8 + p8
                        for qi in range(5):
                            gi = m * 5 + qi
                            ve.wait_ge(pet, gi + 1)
                            if m >= 1 and qi == 0:
                                ve.wait_ge(stc, 16 * m)
                            op = ve.tensor_copy(cmstage[:, qi * 32:(qi + 1) * 32], pb[gi % 2][:, 0:32])
                            op.then_inc(vcp, 1)
                        ve.drain()
                        ve.nop().then_inc(vcp, 1)

                @bC.gpsimd
                def _(gp):
                    for p8 in range(NP8):
                        m = i * NP8 + p8
                        gp.wait_ge(vcp, (m + 1) * 6)
                        gp.dma_start(dd["sc_cm"][:, m, :, :].rearrange("q p c -> p q c"),
                                     cmstage[:, :].rearrange("p (q c) -> p q c", q=5)).then_inc(stc, 16)
                    gp.wait_ge(stc, 16 * (i + 1) * NP8)

            with nc.Block() as bST:
                @bST.sync
                def _(sync):
                    for ri, q in enumerate([3, 4, 5]):
                        sync.dma_start(dd["sc_gba"][32 * ri:32 * (ri + 1), i * TC:(i + 1) * TC],
                                       tc_(q)).then_inc(stg, 16)
                    sync.wait_ge(stg, 48 * (i + 1))

        rbs = P.sem()
        with nc.Block() as bR:
            @bR.sync
            def _(sync):
                for q in range(5):
                    for a in range(2):
                        srcap = dd["sc_cm"][q, :, :, :].rearrange("m (np c) j -> c m np j", np=2)[:, :, :, 16 * a:16 * a + 16]
                        dstap = cm2[a * 64:(a + 1) * 64, q * CQ:(q + 1) * CQ].rearrange(
                            "c (m np j) -> c m np j", np=2, j=16)
                        sync.dma_start(dstap, srcap).then_inc(rbs, 16)
                sync.wait_ge(rbs, 160)

        with nc.Block() as bN:
            @bN.vector
            def _(ve):
                ve.tensor_scalar(cm2[:, 5 * CQ:6 * CQ], cm2[:, 4 * CQ:5 * CQ], negc[:, 0:1],
                                 None, op0=ALU.mult)

    def phase_gdn_l2norm():
        lda = P.sems(2)
        pes, ves, ses = P.sem(), P.sem(), P.sem()
        sto = P.sems(2)
        hb = P.sbuf("l2_hb", [128, 2 * T], BF16)
        sq = P.sbuf("l2_sq", [128, T], BF16)
        rs = P.sbuf("l2_rs", [128, 2 * T], F32)
        epsq = P.sbuf("l2_epsq", [128, 1], F32)
        nst = T // 512
        with nc.Block() as b:
            @b.vector
            def _(ve):
                ve.memset(epsq[:, :], L2_EPS * DK)
        with nc.Block() as b2:
            @b2.sync
            def _(sync):
                for it in range(2 * HK):
                    sl = it % 2
                    dn = "sc_qg" if it < HK else "sc_kg"
                    r0 = (it % HK) * 128
                    if it >= 2:
                        sync.wait_ge(ves, nst * (it - 1))
                        sync.wait_ge(sto[sl], 16 * (it // 2))
                    sync.dma_start(hb[:, sl * T:(sl + 1) * T], dd[dn][r0:r0 + 128, :]).then_inc(lda[sl], 16)

            @b2.scalar
            def _(sc):
                for it in range(2 * HK):
                    sl = it % 2
                    sc.wait_ge(lda[sl], 16 * (it // 2 + 1))
                    if it >= 2:
                        sc.wait_ge(pes, nst * (it - 1))
                    sc.activation(sq[:, :], hb[:, sl * T:(sl + 1) * T], AF.Square)
                    sc.drain()
                    sc.nop().then_inc(ses, 1)
                    for s_ in range(nst):
                        sc.wait_ge(pes, nst * it + s_ + 1)
                        scl = float(DK) if it < HK else 1.0
                        epsap = epsq[:, 0:1] if it < HK else eps_l2[:, 0:1]
                        sc.activation(rs[:, sl * T + s_ * 512:sl * T + (s_ + 1) * 512],
                                      pb[s_ % 2][:, 0:512], AF.Sqrt, scale=scl, bias=epsap).then_inc(ses, 1)

            @b2.tensor
            def _(te):
                for it in range(2 * HK):
                    te.wait_ge(ses, (nst + 1) * it + 1)
                    for s_ in range(nst):
                        if nst * it + s_ >= 2:
                            te.wait_ge(ses, (nst + 1) * it + s_ - 1 + 1 + 1) if False else None
                        if nst * it + s_ >= 2:
                            te.wait_ge(ves, 0) if False else None
                        gi = nst * it + s_
                        if gi >= 2:
                            prev_it, prev_s = divmod(gi - 2, nst)
                            te.wait_ge(ses, (nst + 1) * prev_it + prev_s + 2)
                        te.matmul(pb[s_ % 2][:, 0:512], cs["c_ones_bf"][:, :],
                                  sq[:, s_ * 512:(s_ + 1) * 512], start=True, stop=True).then_inc(pes, 1)

            @b2.vector
            def _(ve):
                for it in range(2 * HK):
                    sl = it % 2
                    for s_ in range(nst):
                        ve.wait_ge(ses, (nst + 1) * it + s_ + 2)
                        ve.reciprocal(rs[:, sl * T + s_ * 512:sl * T + (s_ + 1) * 512],
                                      rs[:, sl * T + s_ * 512:sl * T + (s_ + 1) * 512])
                        ve.drain()
                        if it >= 2 and s_ == 0:
                            ve.wait_ge(sto[sl], 16 * (it // 2))
                        ve.tensor_tensor(hb[:, sl * T + s_ * 512:sl * T + (s_ + 1) * 512],
                                         hb[:, sl * T + s_ * 512:sl * T + (s_ + 1) * 512],
                                         rs[:, sl * T + s_ * 512:sl * T + (s_ + 1) * 512],
                                         op=ALU.mult)
                        ve.drain()
                    ve.nop().then_inc(ves, nst)

            @b2.gpsimd
            def _(gp):
                for it in range(2 * HK):
                    sl = it % 2
                    dn = "sc_qg" if it < HK else "sc_kg"
                    r0 = (it % HK) * 128
                    gp.wait_ge(ves, nst * (it + 1))
                    gp.dma_start(dd[dn][r0:r0 + 128, :], hb[:, sl * T:(sl + 1) * T]).then_inc(sto[sl], 16)
                gp.wait_ge(sto[0], 16 * HK)
                gp.wait_ge(sto[1], 16 * HK)

    def phase_gdn_egc(egC_sb, egl):
        pes, ves = P.sem(), P.sem()
        with nc.Block() as b:
            @b.tensor
            def _(te):
                for vh in range(HV):
                    if vh >= 2:
                        te.wait_ge(ves, vh - 1)
                    te.matmul(pb[vh % 2][:, 0:NCG], cs["c_sel1"][:, vh * 128:(vh + 1) * 128],
                              egl[0:32, 0:NCG], start=True, stop=True).then_inc(pes, 1)

            @b.vector
            def _(ve):
                for vh in range(HV):
                    ve.wait_ge(pes, vh + 1)
                    ve.tensor_copy(egC_sb[:, vh * NCG:(vh + 1) * NCG], pb[vh % 2][:, 0:NCG]).then_inc(ves, 1)

    def phase_gdn_core(cm2, egC_sb, gba_sb):
        CQ = NCG * 16
        NG8 = NCG // 8
        NU = HK * NG8
        lda, ldg = P.sem(), P.sem()
        peG, peQ, veM, seE = P.sem(), P.sem(), P.sem(), P.sem()
        peT, seT, poB, poKg = P.sem(), P.sem(), P.sem(), P.sem()
        peS, veX, seX, stx = P.sem(), P.sem(), P.sem(), P.sem()
        qn = P.sbuf("g4_qn", [128, T], BF16)
        kn = P.sbuf("g4_kn", [128, T], BF16)
        kng = P.sbuf("g4_kng", [128, T], BF16)
        vga = P.sbuf("g4_vga", [128, T], BF16)
        vgb = P.sbuf("g4_vgb", [128, T], BF16)
        gslc2 = P.sbuf("g4_gslc", [32, 2 * 1536], F32)
        d2 = P.sbuf("g4_d2", [128, 512], F32)
        tq = P.sbuf("g4_tq", [128, 512], F32)
        eE2 = P.sbuf("g4_eE", [128, 2 * 1536], BF16)
        ATs2 = P.sbuf("g4_AT", [128, 2 * 512], BF16)
        NTs2 = P.sbuf("g4_NT", [128, 2 * 512], BF16)
        Ns2 = P.sbuf("g4_N", [128, 2 * 512], BF16)
        qds2 = P.sbuf("g4_qds", [128, 2 * 1024], BF16)
        kTvT = P.sbuf("g4_kTvT", [128, 2 * 2048], BF16)
        kd_all = P.sbuf("g4_kd", [128, 1024], BF16)
        kdn_all = P.sbuf("g4_kdn", [128, 1024], BF16)
        X_all = P.sbuf("g4_X", [128, 2 * 2048], BF16)
        Pn2 = P.sbuf("g4_Pn", [128, 2 * 512], BF16)
        Pt_all = P.sbuf("g4_Pt", [128, 5 * 512], BF16)
        pT6 = pb[6][:, :].bitcast(BF16)
        pT7 = pb[7][:, :].bitcast(BF16)

        def uinfo(u):
            return u // NG8, u % NG8, u % 2

        SCHED = [("Pn", 1), ("Pt", 1), ("A0", 0), ("A1", 0),
                 ("Pn", 2), ("Pt", 2), ("A0", 1), ("A1", 1),
                 ("Pn", 3), ("Pt", 3), ("A0", 2), ("A1", 2),
                 ("Pn", 4), ("Pt", 4), ("A0", 3), ("A1", 3),
                 ("Pt", 5), ("A0", 4), ("A1", 4), ("A0", 5), ("A1", 5)]
        PE_W = [[("sx", 1, 7)], [("sx", 1, 9)], [("vx", 1, 12)], [("vx", 0, 1)],
                [("sx", 0, 2)], [("sx", 0, 2)], [("sx", 0, 2), ("vx", 0, 2)],
                [("vx", 0, 3)], [("sx", 0, 4)], [("sx", 0, 4)], [("vx", 0, 4)],
                [("vx", 0, 5)], [("sx", 0, 6)], [("sx", 0, 6)], [("vx", 0, 6)],
                [("vx", 0, 7)], [("sx", 0, 8)], [("vx", 0, 8)], [("vx", 0, 9)],
                [("sx", 0, 9), ("vx", 0, 10)], [("vx", 0, 11)]]
        DVE_DRAIN = {6, 10, 14, 17, 19}

        def xslots(k):
            return (0, 2048) if k % 2 == 0 else (2048, 0)

        with nc.Block() as b:
            @b.sync
            def _(sync):
                for u in range(NU):
                    j, gg, s = uinfo(u)
                    if gg == 0:
                        if j >= 1:
                            sync.wait_ge(peT, 2 * u)
                            sync.wait_ge(peG, 4 * u)
                            sync.wait_ge(veM, 8 * u)
                        sync.dma_start(qn[:, :], dd["sc_qg"][j * 128:(j + 1) * 128, :]).then_inc(lda, 16)
                        sync.dma_start(kn[:, :], dd["sc_kg"][j * 128:(j + 1) * 128, :]).then_inc(lda, 16)
                        sync.dma_start(vga[:, :], dd["sc_vg"][j * 128:(j + 1) * 128, :]).then_inc(lda, 16)
                        sync.dma_start(vgb[:, :], dd["sc_vg"][(j + 16) * 128:(j + 17) * 128, :]).then_inc(lda, 16)
                    if u >= 2:
                        sync.wait_ge(peG, 4 * (u - 1))
                        sync.wait_ge(peQ, 2 * (u - 1))
                    c0 = gg * 512
                    for ri in range(3):
                        sync.dma_start(gslc2[0:32, s * 1536 + ri * 512:s * 1536 + (ri + 1) * 512],
                                       dd["sc_gba"][32 * ri:32 * ri + 32, c0:c0 + 512]).then_inc(ldg, 16)

            @b.tensor
            def _(te):
                for it in range(NU + 1):
                    if it < NU:
                        u = it
                        j, gg, s = uinfo(u)
                        c0 = gg * 512
                        te.wait_ge(ldg, 48 * (u + 1))
                        if gg == 0:
                            te.wait_ge(lda, 64 * (j + 1))
                            te.wait_ge(poKg, j + 1)
                        if u >= 1:
                            te.wait_ge(veM, 8 * u)
                        gsl = gslc2[0:32, s * 1536:s * 1536 + 1536]
                        te.matmul(pb[2][:, 0:512], cs["c_sel2"][:, j * 128:(j + 1) * 128],
                                  gsl[:, 0:512], start=True, stop=True).then_inc(peG, 1)
                        te.matmul(pb[3][:, 0:512], cs["c_sel2"][:, j * 128:(j + 1) * 128],
                                  gsl[:, 512:1024], start=True, stop=True).then_inc(peG, 1)
                        for n8 in range(8):
                            cc = c0 + n8 * 64
                            for tp in range(2):
                                mm = te.matmul(pb[0][tp * 64:(tp + 1) * 64, n8 * 64:(n8 + 1) * 64],
                                               kn[:, cc:cc + 64], kng[:, cc:cc + 64],
                                               start=True, stop=True, tile_position=(0, tp * 64))
                        mm.then_inc(peG, 1)
                        for n8 in range(8):
                            cc = c0 + n8 * 64
                            for tp in range(2):
                                mm = te.matmul(pb[1][tp * 64:(tp + 1) * 64, n8 * 64:(n8 + 1) * 64],
                                               qn[:, cc:cc + 64], qn[:, cc:cc + 64],
                                               start=True, stop=True, tile_position=(0, tp * 64))
                        mm.then_inc(peG, 1)
                    if it >= 1:
                        u = it - 1
                        j, gg, s = uinfo(u)
                        for gidx, (kind, lvl) in enumerate(SCHED):
                            if gidx == 0:
                                te.wait_ge(veM, 8 * u + 6)
                            for (sk, back, off) in PE_W[gidx]:
                                rate = 9 if sk == "sx" else 12
                                tgt = rate * (u - back) + off
                                if tgt > 0:
                                    te.wait_ge(seX if sk == "sx" else veX, tgt)
                            if kind == "Pn" or kind == "Pt":
                                dstb = pb[4] if kind == "Pn" else pb[5]
                                for n8 in range(8):
                                    for a in range(2):
                                        if lvl == 1:
                                            lT = NTs2[a * 64:(a + 1) * 64, s * 512 + n8 * 64:s * 512 + (n8 + 1) * 64]
                                            nN = Ns2[a * 64:(a + 1) * 64, s * 512 + n8 * 64:s * 512 + (n8 + 1) * 64]
                                        else:
                                            lT = Pt_all[a * 64:(a + 1) * 64,
                                                        (lvl - 2) * 512 + n8 * 64:(lvl - 2) * 512 + (n8 + 1) * 64]
                                            ps = (lvl - 2) % 2
                                            nN = Pn2[a * 64:(a + 1) * 64, ps * 512 + n8 * 64:ps * 512 + (n8 + 1) * 64]
                                        l_, r_ = (lT, nN) if kind == "Pn" else (nN, lT)
                                        mm = te.matmul(dstb[a * 64:(a + 1) * 64, n8 * 64:(n8 + 1) * 64],
                                                       l_, r_, start=True, stop=True,
                                                       tile_position=(a * 64, a * 64))
                                mm.then_inc(peS, 1)
                            else:
                                k = lvl
                                wv = 0 if kind == "A0" else 1
                                if gidx == 2:
                                    te.wait_ge(poB, 2 * u + 2)
                                    te.wait_ge(seT, 2 * u + 2)
                                srcb, _dst = xslots(k)
                                for n8 in range(4 * wv, 4 * wv + 4):
                                    bank = pb[6] if (n8 % 4) < 2 else pb[7]
                                    colb = (n8 % 2) * 256
                                    for a in range(2):
                                        if k == 0:
                                            lT = NTs2[a * 64:(a + 1) * 64, s * 512 + n8 * 64:s * 512 + (n8 + 1) * 64]
                                        else:
                                            lT = Pt_all[a * 64:(a + 1) * 64,
                                                        (k - 1) * 512 + n8 * 64:(k - 1) * 512 + (n8 + 1) * 64]
                                        mm = te.matmul(bank[a * 64:(a + 1) * 64, colb:colb + 256],
                                                       lT,
                                                       X_all[a * 64:(a + 1) * 64, srcb + n8 * 256:srcb + (n8 + 1) * 256],
                                                       start=True, stop=True,
                                                       tile_position=(a * 64, a * 64))
                                mm.then_inc(peS, 1)
                    if it < NU:
                        u = it
                        j, gg, s = uinfo(u)
                        gsl = gslc2[0:32, s * 1536:s * 1536 + 1536]
                        te.wait_ge(veM, 8 * u + 3)
                        te.matmul(pb[3][:, 0:512], cs["c_sel1"][:, j * 128:(j + 1) * 128],
                                  gsl[:, 1024:1536], start=True, stop=True).then_inc(peQ, 1)
                        te.wait_ge(veM, 8 * u + 5)
                        te.matmul(pb[2][:, 0:512], cs["c_sel1"][:, (j + 16) * 128:(j + 17) * 128],
                                  gsl[:, 1024:1536], start=True, stop=True).then_inc(peQ, 1)
                    if it < NU:
                        u = it
                        j, gg, s = uinfo(u)
                        c0 = gg * 512
                        if u >= 1:
                            te.wait_ge(veX, 12 * (u - 1) + 12)
                        for n8 in range(8):
                            cc = c0 + n8 * 64
                            pbT = pT6 if n8 < 4 else pT7
                            cb = (n8 % 4) * 256
                            te.transpose(pbT[0:64, cb:cb + 128], kn[:, cc:cc + 64],
                                         cs["c_ident_bf"][:, :], tile_position=(0, 0))
                            te.transpose(pbT[64:128, cb:cb + 128], kn[:, cc:cc + 64],
                                         cs["c_ident_bf"][:, :], tile_position=(0, 64))
                            te.transpose(pbT[0:64, cb + 128:cb + 256], vga[:, cc:cc + 64],
                                         cs["c_ident_bf"][:, :], tile_position=(0, 0))
                            tt = te.transpose(pbT[64:128, cb + 128:cb + 256], vgb[:, cc:cc + 64],
                                              cs["c_ident_bf"][:, :], tile_position=(0, 64))
                            if n8 == 3 or n8 == 7:
                                tt.then_inc(peT, 1)

            @b.vector
            def _(ve):
                for it in range(NU + 1):
                    if it < NU:
                        u = it
                        j, gg, s = uinfo(u)
                        c0 = gg * 512
                        g3d = cm2[:, 0 * CQ + (gg * 8) * 16 + j:0 * CQ + (gg * 8 + 8) * 16 + j:16] \
                            .unsqueeze(2).broadcast_to([128, 8, 64])
                        glb3d = cm2[:, 1 * CQ + (gg * 8) * 16 + j:1 * CQ + (gg * 8 + 8) * 16 + j:16] \
                            .unsqueeze(2).broadcast_to([128, 8, 64])
                        d2v = d2[:, :].rearrange("p (n s) -> p n s", n=8)
                        eEs = eE2[:, s * 1536:(s + 1) * 1536]
                        ve.wait_ge(peG, 4 * u + 1)
                        if u >= 1:
                            ve.wait_ge(seE, 3 * u)
                        ve.tensor_tensor(d2v, pb[2][:, 0:512].rearrange("p (n s) -> p n s", n=8),
                                         g3d, op=ALU.subtract)
                        ve.drain()
                        ve.tensor_tensor(tq[:, :], d2[:, :], cs["c_ubias_i"][:, :], op=ALU.add).then_inc(veM, 1)
                        ve.wait_ge(seE, 3 * u + 1)
                        ve.wait_ge(peG, 4 * u + 4)
                        if u >= 2:
                            ve.wait_ge(stx, 96 * (u - 1))
                        ve.tensor_tensor(ATs2[:, s * 512:(s + 1) * 512], eEs[:, 0:512],
                                         pb[1][:, 0:512], op=ALU.mult).then_inc(veM, 1)
                        ve.tensor_tensor(d2v, pb[3][:, 0:512].rearrange("p (n s) -> p n s", n=8),
                                         g3d, op=ALU.subtract)
                        ve.drain()
                        ve.tensor_tensor(tq[:, :], d2[:, :], cs["c_ubias_s"][:, :], op=ALU.add).then_inc(veM, 1)
                        ve.wait_ge(seE, 3 * u + 2)
                        if u >= 1:
                            ve.wait_ge(peS, 21 * (u - 1))
                        ve.tensor_tensor(NTs2[:, s * 512:(s + 1) * 512], eEs[:, 512:1024],
                                         pb[0][:, 0:512], op=ALU.mult).then_inc(veM, 1)
                        ve.tensor_tensor(d2v, glb3d, pb[2][:, 0:512].rearrange("p (n s) -> p n s", n=8),
                                         op=ALU.subtract)
                        ve.drain()
                        ve.tensor_tensor(tq[:, :], d2[:, :], cs["c_lbias_s"][:, :], op=ALU.add).then_inc(veM, 1)
                        ve.wait_ge(seE, 3 * u + 3)
                        ve.tensor_tensor(Ns2[:, s * 512:(s + 1) * 512], eEs[:, 1024:1536],
                                         pb[0][:, 0:512], op=ALU.mult).then_inc(veM, 1)
                    if it >= 1:
                        u = it - 1
                        j, gg, s = uinfo(u)
                        for gidx, (kind, lvl) in enumerate(SCHED):
                            if kind in ("Pn", "Pt"):
                                continue
                            ve.wait_ge(peS, 21 * u + gidx + 1)
                            k = lvl
                            wv = 0 if kind == "A0" else 1
                            if gidx == 2:
                                ve.wait_ge(poB, 2 * u + 2)
                            if gidx in DVE_DRAIN:
                                ve.drain()
                            srcb, dstb = xslots(k)
                            cA = 4 * wv * 256
                            ve.tensor_tensor(X_all[:, dstb + cA:dstb + cA + 512],
                                             pb[6][:, 0:512],
                                             X_all[:, srcb + cA:srcb + cA + 512], op=ALU.add)
                            ve.tensor_tensor(X_all[:, dstb + cA + 512:dstb + cA + 1024],
                                             pb[7][:, 0:512],
                                             X_all[:, srcb + cA + 512:srcb + cA + 1024],
                                             op=ALU.add).then_inc(veX, 1)
                    if it < NU:
                        u = it
                        j, gg, s = uinfo(u)
                        c0 = gg * 512
                        ve.wait_ge(peQ, 2 * u + 1)
                        ve.tensor_tensor(qds2[:, s * 1024:s * 1024 + 512], qn[:, c0:c0 + 512],
                                         pb[3][:, 0:512], op=ALU.mult).then_inc(veM, 1)
                        ve.wait_ge(peQ, 2 * u + 2)
                        ve.tensor_tensor(qds2[:, s * 1024 + 512:s * 1024 + 1024], qn[:, c0:c0 + 512],
                                         pb[2][:, 0:512], op=ALU.mult).then_inc(veM, 1)
                    if it < NU:
                        u = it
                        j, gg, s = uinfo(u)
                        ve.wait_ge(seT, 2 * u + 2)
                        if u >= 1:
                            ve.wait_ge(stx, 96 * u)
                        for n8 in range(8):
                            col = (gg * 8 + n8) * 16 + j
                            kv = kTvT[:, s * 2048 + n8 * 256:s * 2048 + n8 * 256 + 128]
                            vv = kTvT[:, s * 2048 + n8 * 256 + 128:s * 2048 + (n8 + 1) * 256]
                            ve.tensor_scalar(kd_all[:, n8 * 128:(n8 + 1) * 128], kv,
                                             cm2[:, 4 * CQ + col:4 * CQ + col + 1], None, op0=ALU.mult)
                            op = ve.tensor_scalar(kdn_all[:, n8 * 128:(n8 + 1) * 128], kv,
                                                  cm2[:, 5 * CQ + col:5 * CQ + col + 1], None, op0=ALU.mult)
                            if n8 == 7:
                                op.then_inc(poB, 1)
                        for n8 in range(8):
                            col = (gg * 8 + n8) * 16 + j
                            kv = kTvT[:, s * 2048 + n8 * 256:s * 2048 + n8 * 256 + 128]
                            vv = kTvT[:, s * 2048 + n8 * 256 + 128:s * 2048 + (n8 + 1) * 256]
                            ve.tensor_scalar(X_all[:, n8 * 256:n8 * 256 + 128], vv,
                                             cm2[:, 2 * CQ + col:2 * CQ + col + 1], None, op0=ALU.mult)
                            op = ve.tensor_scalar(X_all[:, n8 * 256 + 128:(n8 + 1) * 256], kv,
                                                  cm2[:, 3 * CQ + col:3 * CQ + col + 1], None, op0=ALU.mult)
                            if n8 == 7:
                                op.then_inc(poB, 1)

            @b.scalar
            def _(sc):
                for u in range(NU):
                    j, gg, s = uinfo(u)
                    eEs = eE2[:, s * 1536:(s + 1) * 1536]
                    for e in range(3):
                        sc.wait_ge(veM, 8 * u + 2 * e + 1)
                        sc.activation(eEs[:, e * 512:(e + 1) * 512], tq[:, :], AF.Exp).then_inc(seE, 1)
                    if u >= 1:
                        up = u - 1
                        for gidx, (kind, lvl) in enumerate(SCHED):
                            if kind == "Pn":
                                sc.wait_ge(peS, 21 * up + gidx + 1)
                                ps = (lvl - 1) % 2
                                sc.copy(Pn2[:, ps * 512:(ps + 1) * 512],
                                        pb[4][:, 0:512]).then_inc(seX, 1)
                            elif kind == "Pt":
                                sc.wait_ge(peS, 21 * up + gidx + 1)
                                sc.copy(Pt_all[:, (lvl - 1) * 512:lvl * 512],
                                        pb[5][:, 0:512]).then_inc(seX, 1)
                    sc.wait_ge(peT, 2 * u + 1)
                    if u >= 2:
                        sc.wait_ge(poB, 2 * (u - 1))
                    sc.copy(kTvT[:, s * 2048:s * 2048 + 1024], pT6[:, 0:1024]).then_inc(seT, 1)
                    sc.wait_ge(peT, 2 * u + 2)
                    sc.copy(kTvT[:, s * 2048 + 1024:s * 2048 + 2048], pT7[:, 0:1024]).then_inc(seT, 1)
                up = NU - 1
                for gidx, (kind, lvl) in enumerate(SCHED):
                    if kind == "Pn":
                        sc.wait_ge(peS, 21 * up + gidx + 1)
                        ps = (lvl - 1) % 2
                        sc.copy(Pn2[:, ps * 512:(ps + 1) * 512],
                                pb[4][:, 0:512]).then_inc(seX, 1)
                    elif kind == "Pt":
                        sc.wait_ge(peS, 21 * up + gidx + 1)
                        sc.copy(Pt_all[:, (lvl - 1) * 512:lvl * 512],
                                pb[5][:, 0:512]).then_inc(seX, 1)

            @b.gpsimd
            def _(gp):
                for it in range(NU + 1):
                    if it < NU and it % NG8 == 0:
                        j = it // NG8
                        gp.wait_ge(lda, 64 * (j + 1))
                        if j >= 1:
                            gp.wait_ge(peG, 4 * it)
                        gp.tensor_scalar(kng[:, :], kn[:, :], -1.0, None, op0=ALU.mult).then_inc(poKg, 1)
                    if it >= 1:
                        u = it - 1
                        j, gg, s = uinfo(u)
                        gp.wait_ge(veX, 12 * u + 12)
                        gp.wait_ge(veM, 8 * u + 8)
                        r0 = gg * 128 + j
                        gp.dma_start(dd["sc_kd2"][r0:r0 + 113:16, :, :].rearrange("n p c -> p n c"),
                                     kd_all[:, :].rearrange("p (n c) -> p n c", c=128)).then_inc(stx, 16)
                        gp.dma_start(dd["sc_kdn"][r0:r0 + 113:16, :, :].rearrange("n p c -> p n c"),
                                     kdn_all[:, :].rearrange("p (n c) -> p n c", c=128)).then_inc(stx, 16)
                        gp.dma_start(dd["sc_X"][r0:r0 + 113:16, :, :].rearrange("n p c -> p n c"),
                                     X_all[:, 0:2048].rearrange("p (n c) -> p n c", c=256)).then_inc(stx, 16)
                        gp.dma_start(dd["sc_AT"][gg * 16 + j, :, :],
                                     ATs2[:, s * 512:(s + 1) * 512]).then_inc(stx, 16)
                        gp.dma_start(dd["sc_qdT"][j * NG8 + gg, :, :],
                                     qds2[:, s * 1024:s * 1024 + 512]).then_inc(stx, 16)
                        gp.dma_start(dd["sc_qdT"][(j + 16) * NG8 + gg, :, :],
                                     qds2[:, s * 1024 + 512:s * 1024 + 1024]).then_inc(stx, 16)
                gp.wait_ge(stx, 96 * NU)

    def phase_gdn_scan(S32, Sbf, egC_sb, pass2):
        NG8 = NCG // 8
        NSL = T // 256
        L2 = 2
        lda = [P.sems(2) for _ in range(L2)]
        ldz = P.sems(L2)
        ldS = P.sems(L2)
        peW = P.sems(L2)
        veW = P.sems(L2)
        peP = P.sems(L2)
        veS = P.sems(L2)
        seS = P.sems(L2)
        peA = P.sems(L2)
        veQm = P.sems(L2)
        peO = P.sems(L2)
        veO = P.sems(L2)
        seZ = P.sems(L2)
        veN = P.sems(L2)
        sqd = P.sems(L2)
        peR = P.sems(L2)
        seR = P.sems(L2)
        veF = P.sems(L2)
        veR8 = P.sems(L2)
        stS = P.sems(L2)
        stG = P.sems(L2)
        Xs = [P.sbuf(f"g5_X{L}", [128, 2 * 256], BF16) for L in range(L2)]
        kds = [P.sbuf(f"g5_kd{L}", [128, 2 * 128], BF16) for L in range(L2)]
        kdns = [P.sbuf(f"g5_kdn{L}", [128, 2 * 128], BF16) for L in range(L2)]
        wmt = [P.sbuf(f"g5_wmt{L}", [128, 2 * 128], BF16) for L in range(L2)]
        Sini = [P.sbuf(f"g5_Sini{L}", [128, 128], F32) for L in range(L2)]
        if pass2:
            ATs = [P.sbuf(f"g5_ATs{L}", [128, 2 * 512], BF16) for L in range(L2)]
            qdTs = [P.sbuf(f"g5_qdT{L}", [128, 4 * 512], BF16) for L in range(L2)]
            qm = [P.sbuf(f"g5_qm{L}", [128, 2 * 64], BF16) for L in range(L2)]
            zsl = [P.sbuf(f"g5_z{L}", [128, 2 * T], BF16) for L in range(L2)]
            xg = [P.sbuf(f"g5_xg{L}", [128, 2 * T], BF16) for L in range(L2)]
            rs5 = [P.sbuf(f"g5_rs{L}", [128, 2 * 256], F32) for L in range(L2)]
            sqz = [P.sbuf(f"g5_sq{L}", [128, T], BF16) for L in range(L2)]

        def ld_counts():
            cnt = [0, 0]
            table = []
            for n in range(NCG):
                sl = n % 2
                cnt[sl] += 3
                if pass2 and n % 8 == 0:
                    cnt[sl] += 3
                table.append((sl, cnt[sl]))
            return table, cnt
        LDT, LDJ = ld_counts()

        with nc.Block() as bP:
            @bP.sync
            def _(sync):
                for jp in range(8):
                    for n in range(NCG):
                        for L in range(L2):
                            j = 2 * jp + L
                            sl, cn = LDT[n]
                            if pass2 and n == 0:
                                for a in range(2):
                                    vh = j + 16 * a
                                    if 2 * jp + a > 0:
                                        sync.wait_ge(veN[L], 2 * jp + a)
                                    sync.dma_start(Sini[L][:, :],
                                                   dd["cc3_out"][vh * 128:(vh + 1) * 128, :]).then_inc(ldS[L], 16)
                            if n >= 2 or jp > 0:
                                tgt = max(0, 2 * NCG * jp + 2 * (n - 1))
                                sync.wait_ge(veS[L], tgt)
                                sync.wait_ge(peW[L], tgt)
                                sync.wait_ge(peP[L], tgt)
                                if pass2:
                                    sync.wait_ge(peA[L], tgt)
                                    sync.wait_ge(peO[L], tgt)
                                    sync.wait_ge(veQm[L], tgt)
                            sync.dma_start(Xs[L][:, sl * 256:(sl + 1) * 256],
                                           dd["sc_X"][n * 16 + j, :, :]).then_inc(lda[L][sl], 16)
                            sync.dma_start(kds[L][:, sl * 128:(sl + 1) * 128],
                                           dd["sc_kd2"][n * 16 + j, :, :]).then_inc(lda[L][sl], 16)
                            sync.dma_start(kdns[L][:, sl * 128:(sl + 1) * 128],
                                           dd["sc_kdn"][n * 16 + j, :, :]).then_inc(lda[L][sl], 16)
                            if pass2 and n % 8 == 0:
                                sync.dma_start(ATs[L][:, (n // 8 % 2) * 512:(n // 8 % 2 + 1) * 512],
                                               dd["sc_AT"][(n // 8) * 16 + j, :, :]).then_inc(lda[L][sl], 16)
                                for a in range(2):
                                    vh = j + 16 * a
                                    sync.dma_start(qdTs[L][:, (2 * (n // 8 % 2) + a) * 512:
                                                        (2 * (n // 8 % 2) + a + 1) * 512],
                                                   dd["sc_qdT"][vh * NG8 + n // 8, :, :]).then_inc(lda[L][sl], 16)

            if pass2:
                @bP.scalar
                def _(sc):
                    for jp in range(8):
                        for L in range(L2):
                            j = 2 * jp + L
                            for a in range(2):
                                vh = j + 16 * a
                                if jp > 0:
                                    sc.wait_ge(veF[L], 2 * (jp - 1) + a + 1)
                                sc.dma_start(zsl[L][:, a * T:(a + 1) * T],
                                             dd["sc_z"][vh * 128:(vh + 1) * 128, :]).then_inc(ldz[L], 16)
                            sc.wait_ge(ldz[L], 32 * (jp + 1))
                            for a in range(2):
                                sc.activation(zsl[L][:, a * T:(a + 1) * T], zsl[L][:, a * T:(a + 1) * T],
                                              AF.Silu).then_inc(seZ[L], 1)
                        for n in range(NCG):
                            for L in range(L2):
                                for a in range(2):
                                    i2 = (jp * NCG + n) * 2 + a
                                    sc.wait_ge(peW[L], i2 + 1)
                                    sc.copy(wmt[L][:, a * 128:(a + 1) * 128],
                                            pb[a][:, L * 128:(L + 1) * 128]).then_inc(veW[L], 1)
                                    sc.wait_ge(veS[L], i2 + 1)
                                    sc.copy(Sbf[:, (2 * jp + L + 16 * a) * 128:(2 * jp + L + 16 * a + 1) * 128],
                                            S32[:, (2 * jp + L + 16 * a) * 128:(2 * jp + L + 16 * a + 1) * 128]) \
                                        .then_inc(seS[L], 1)
                        for L in range(L2):
                            j = 2 * jp + L
                            for a in range(2):
                                vh = j + 16 * a
                                sc.wait_ge(veO[L], 2 * NCG * jp + 2 * (NCG - 1) + a + 1)
                                if 2 * jp + a > 0:
                                    sc.wait_ge(peR[L], NSL * (2 * jp + a))
                                sc.activation(sqz[L][:, :], xg[L][:, a * T:(a + 1) * T], AF.Square)
                                sc.drain()
                                sc.nop().then_inc(sqd[L], 1)
                                for s_ in range(NSL):
                                    si8 = (2 * jp + a) * NSL + s_
                                    sc.wait_ge(peR[L], si8 + 1)
                                    if si8 >= 2:
                                        sc.wait_ge(veR8[L], si8 - 1)
                                    sc.activation(rs5[L][:, (si8 % 2) * 256:(si8 % 2) * 256 + 256],
                                                  pb[2 * (si8 % 2) + L][:, 256:512],
                                                  AF.Ln, scale=1.0 / 128.0,
                                                  bias=eps_rms[:, 0:1])
                                    sc.drain()
                                    sc.activation(rs5[L][:, (si8 % 2) * 256:(si8 % 2) * 256 + 256],
                                                  rs5[L][:, (si8 % 2) * 256:(si8 % 2) * 256 + 256],
                                                  AF.Exp, scale=-0.5).then_inc(seR[L], 1)
            else:
                @bP.scalar
                def _(sc):
                    for jp in range(8):
                        for n in range(NCG):
                            for L in range(L2):
                                for a in range(2):
                                    i2 = (jp * NCG + n) * 2 + a
                                    sc.wait_ge(peW[L], i2 + 1)
                                    sc.copy(wmt[L][:, a * 128:(a + 1) * 128],
                                            pb[a][:, L * 128:(L + 1) * 128]).then_inc(veW[L], 1)
                                    sc.wait_ge(veS[L], i2 + 1)
                                    sc.copy(Sbf[:, (2 * jp + L + 16 * a) * 128:(2 * jp + L + 16 * a + 1) * 128],
                                            S32[:, (2 * jp + L + 16 * a) * 128:(2 * jp + L + 16 * a + 1) * 128]) \
                                        .then_inc(seS[L], 1)

            @bP.vector
            def _(ve):
                for jp in range(8):
                    for L in range(L2):
                        j = 2 * jp + L
                        for a in range(2):
                            vh = j + 16 * a
                            if pass2:
                                ve.wait_ge(ldS[L], 16 * (2 * jp + a + 1))
                                ve.tensor_scalar(S32[:, vh * 128:(vh + 1) * 128], Sini[L][:, :],
                                                 cs["parity"][:, 0:1], None, op0=ALU.mult)
                            else:
                                ve.memset(S32[:, vh * 128:(vh + 1) * 128], 0.0)
                            ve.drain()
                            ve.tensor_copy(Sbf[:, vh * 128:(vh + 1) * 128],
                                           S32[:, vh * 128:(vh + 1) * 128]).then_inc(veN[L], 1)
                    for n in range(NCG):
                        for L in range(L2):
                            j = 2 * jp + L
                            sl = n % 2
                            for a in range(2):
                                vh = j + 16 * a
                                i2 = (jp * NCG + n) * 2 + a
                                if pass2:
                                    ve.wait_ge(peA[L], i2 + 1)
                                    ve.tensor_tensor(qm[L][:, a * 64:(a + 1) * 64],
                                                     qdTs[L][:, (2 * (n // 8 % 2) + a) * 512 + (n % 8) * 64:
                                                          (2 * (n // 8 % 2) + a) * 512 + (n % 8 + 1) * 64],
                                                     pb[4 + a][:, L * 64:(L + 1) * 64],
                                                     op=ALU.subtract).then_inc(veQm[L], 1)
                                ve.wait_ge(peP[L], i2 + 1)
                                if pass2:
                                    ve.wait_ge(peO[L], i2 + 1)
                                ve.drain()
                                ve.scalar_tensor_tensor(S32[:, vh * 128:(vh + 1) * 128],
                                                        S32[:, vh * 128:(vh + 1) * 128],
                                                        egC_sb[:, vh * NCG + n:vh * NCG + n + 1],
                                                        pb[2 + a][:, L * 128:(L + 1) * 128],
                                                        op0=ALU.mult, op1=ALU.subtract).then_inc(veS[L], 1)
                                if pass2:
                                    ve.wait_ge(seZ[L], 2 * jp + a + 1)
                                    if n == 0 and a == 0 and jp > 0:
                                        ve.wait_ge(stG[L], 32 * jp)
                                    ve.tensor_tensor(xg[L][:, a * T + n * 64:a * T + (n + 1) * 64],
                                                     pb[6 + a][:, L * 64:(L + 1) * 64],
                                                     zsl[L][:, a * T + n * 64:a * T + (n + 1) * 64],
                                                     op=ALU.mult).then_inc(veO[L], 1)
                    if pass2:
                        for L in range(L2):
                            for a in range(2):
                                base = (2 * jp + a) * NSL
                                for s_ in range(NSL):
                                    si8 = base + s_
                                    sslot = (si8 % 2) * 256
                                    ve.wait_ge(seR[L], si8 + 1)
                                    ve.drain()
                                    ve.scalar_tensor_tensor(xg[L][:, a * T + s_ * 256:a * T + (s_ + 1) * 256],
                                                            xg[L][:, a * T + s_ * 256:a * T + (s_ + 1) * 256],
                                                            cs["normw"][:, 0:1],
                                                            rs5[L][:, sslot:sslot + 256],
                                                            op0=ALU.mult, op1=ALU.mult)
                                    ve.drain()
                                    ve.nop().then_inc(veR8[L], 1)
                                ve.nop().then_inc(veF[L], 1)

            @bP.tensor
            def _(te):
                for jp in range(8):
                    for n in range(NCG):
                        for L in range(L2):
                            j = 2 * jp + L
                            sl, cn = LDT[n]
                            for a in range(2):
                                vh = j + 16 * a
                                i2 = (jp * NCG + n) * 2 + a
                                if a == 0:
                                    te.wait_ge(lda[L][sl], 16 * (jp * LDJ[sl] + cn))
                                if i2 >= 2:
                                    te.wait_ge(veW[L], i2 - 1)
                                te.matmul(pb[a][:, L * 128:(L + 1) * 128],
                                          Xs[L][a * 64:a * 64 + 64, sl * 256 + 128:sl * 256 + 256],
                                          kds[L][a * 64:a * 64 + 64, sl * 128:(sl + 1) * 128],
                                          start=True, stop=True, tile_position=(a * 64, 0)).then_inc(peW[L], 1)
                                if i2 >= 2:
                                    te.wait_ge(veS[L], i2 - 1)
                                te.wait_ge(veN[L], 2 * jp + a + 1)
                                te.matmul(pb[2 + a][:, L * 128:(L + 1) * 128],
                                          kdns[L][a * 64:a * 64 + 64, sl * 128:(sl + 1) * 128],
                                          Xs[L][a * 64:a * 64 + 64, sl * 256:sl * 256 + 128],
                                          start=True, stop=False, tile_position=(a * 64, 0))
                                te.wait_ge(veW[L], i2 + 1)
                                if i2 >= 2:
                                    te.wait_ge(seS[L], i2 - 1)
                                te.matmul(pb[2 + a][:, L * 128:(L + 1) * 128], wmt[L][:, a * 128:(a + 1) * 128],
                                          Sbf[:, vh * 128:(vh + 1) * 128],
                                          start=False, stop=True, tile_position=(0, 0)).then_inc(peP[L], 1)
                                if pass2:
                                    if i2 >= 2:
                                        te.wait_ge(veQm[L], i2 - 1)
                                    te.matmul(pb[4 + a][:, L * 64:(L + 1) * 64],
                                              Xs[L][a * 64:a * 64 + 64, sl * 256 + 128:sl * 256 + 256],
                                              ATs[L][a * 64:a * 64 + 64, (n // 8 % 2) * 512 + (n % 8) * 64:
                                                  (n // 8 % 2) * 512 + (n % 8 + 1) * 64],
                                              start=True, stop=True, tile_position=(a * 64, 0)).then_inc(peA[L], 1)
                                    if i2 >= 2:
                                        te.wait_ge(veO[L], i2 - 1)
                                    te.matmul(pb[6 + a][:, L * 64:(L + 1) * 64],
                                              Xs[L][a * 64:a * 64 + 64, sl * 256:sl * 256 + 128],
                                              ATs[L][a * 64:a * 64 + 64, (n // 8 % 2) * 512 + (n % 8) * 64:
                                                  (n // 8 % 2) * 512 + (n % 8 + 1) * 64],
                                              start=True, stop=False, tile_position=(a * 64, 0))
                                    te.wait_ge(veQm[L], i2 + 1)
                                    te.matmul(pb[6 + a][:, L * 64:(L + 1) * 64],
                                              Sbf[:, vh * 128:(vh + 1) * 128],
                                              qm[L][:, a * 64:(a + 1) * 64],
                                              start=False, stop=True, tile_position=(0, 0)).then_inc(peO[L], 1)
                    if pass2:
                        for L in range(L2):
                            for a in range(2):
                                te.wait_ge(sqd[L], 2 * jp + a + 1)
                                for s_ in range(NSL):
                                    si8 = (2 * jp + a) * NSL + s_
                                    if si8 >= 2:
                                        te.wait_ge(seR[L], si8 - 1)
                                    te.matmul(pb[2 * (si8 % 2) + L][:, 256:512], cs["c_ones_bf"][:, :],
                                              sqz[L][:, s_ * 256:(s_ + 1) * 256],
                                              start=True, stop=True).then_inc(peR[L], 1)

            @bP.gpsimd
            def _(gp):
                for jp in range(8):
                    for L in range(L2):
                        j = 2 * jp + L
                        if not pass2:
                            for a in range(2):
                                vh = j + 16 * a
                                gp.wait_ge(veS[L], 2 * NCG * jp + 2 * (NCG - 1) + a + 1)
                                gp.dma_start(dd["cc3_in"][vh * 128:(vh + 1) * 128, :],
                                             S32[:, vh * 128:(vh + 1) * 128]).then_inc(stS[L], 16)
                        else:
                            for a in range(2):
                                vh = j + 16 * a
                                gp.wait_ge(veF[L], 2 * jp + a + 1)
                                gp.dma_start(dd["sc_xgn"][vh * 128:(vh + 1) * 128, :],
                                             xg[L][:, a * T:(a + 1) * T]).then_inc(stG[L], 16)
                for L in range(L2):
                    if not pass2:
                        gp.wait_ge(stS[L], 32 * 8)
                    else:
                        gp.wait_ge(stG[L], 32 * 8)

    def phase_gdn_xchg3():
        s = P.sem()
        with nc.Block() as b:
            @b.gpsimd
            def _(gp):
                gp.collective_compute("AllGather", ALU.bypass,
                                      ins=[dd["cc3_in"][:, :].opt()],
                                      outs=[dd["cc3_out"][:, :].opt()],
                                      replica_groups=GROUPS).then_inc(s)
                gp.wait_ge(s, 1)

    def seg0():
        phase_consts()
    def seg1():
        phase_ln_standalone("x0")
    def seg2():
        xh_fm = P.sbuf("xh_fm", [128, 16 * T], BF16)
        wbuf = P.sbuf("wbuf", [128, 4 * 2048], BF16)
        evb = P.sbuf("evb", [128, 2 * MAXC], BF16)
        phase_xh_load(xh_fm)
        phase_fm_proj(xh_fm, wbuf, evb, "w_qT", D, "sc_q", "b_q")
        phase_fm_proj(xh_fm, wbuf, evb, "w_kT", D, "sc_k", "b_k")
        phase_fm_proj(xh_fm, wbuf, evb, "w_vT", D, "sc_v", "b_v")
    def seg3():
        phase_lightning()
    def seg4():
        phase_tm_proj_ln("oproj", "sc_o", 16, "sc_wo_bf", "b_o_rep", "x0", "sc_x1")
    def seg5():
        xh_fm = P.sbuf("xh_fm", [128, 16 * T], BF16)
        wbuf = P.sbuf("wbuf", [128, 4 * 2048], BF16)
        phase_xh_load(xh_fm)
        phase_ffn_gateup(xh_fm, wbuf)
    def seg6():
        phase_tm_proj_ln("ffn1d", "sc_h", 64, "sc_wd_bf", None, "sc_x1", "sc_x2",
                         stat_tiled=True)
    def seg7():
        own3 = P.sbuf("gdn_own3", [128, 256], BF16)
        with P.scope():
            xh_fm = P.sbuf("xh_fm", [128, 16 * T], BF16)
            wbuf = P.sbuf("wbuf", [128, 4 * 2048], BF16)
            phase_xh_load(xh_fm)
            phase_gdn_qkvz(xh_fm, wbuf, own3)
        with P.scope():
            phase_gdn_fixup(own3)
        egC_sb = P.sbuf("gdn_egC", [128, HV * NCG], F32)
        with P.scope():
            sel1 = P.sbuf("gdn_sel1", [32, HV * 128], F32)
            sel2 = P.sbuf("gdn_sel2", [32, HK * 128], F32)
            cs["c_sel1"] = sel1
            cs["c_sel2"] = sel2
            egl = P.sbuf("gdn_egl", [32, NCG], F32)
            cm2 = P.sbuf("gdn_cm2", [128, 6 * NCG * 16], F32)
            sls = P.sem()
            with nc.Block() as bsel:
                @bsel.sync
                def _(sync):
                    sync.dma_start(sel1[:, :], dd["c_sel1"][:, :]).then_inc(sls, 16)
                    sync.dma_start(sel2[:, :], dd["c_sel2"][:, :]).then_inc(sls, 16)
                    sync.wait_ge(sls, 32)
            with P.scope():
                phase_gdn_prep(cm2, egC_sb, egl)
            with P.scope():
                phase_gdn_l2norm()
            with P.scope():
                phase_gdn_egc(egC_sb, egl)
            with P.scope():
                phase_gdn_core(cm2, egC_sb, None)
        S32 = P.sbuf("gdn_S32", [128, HV * 128], F32)
        Sbf = P.sbuf("gdn_Sbf", [128, HV * 128], BF16)
        with P.scope():
            phase_gdn_scan(S32, Sbf, egC_sb, pass2=False)
        phase_gdn_xchg3()
        with P.scope():
            phase_gdn_scan(S32, Sbf, egC_sb, pass2=True)

    def seg7b():
        phase_tm_proj_ln("bproj", "sc_xgn", 32, "sc_wob_bf", None, "sc_x2", "sc_x3")
    def seg8():
        xh_fm = P.sbuf("xh_fm", [128, 16 * T], BF16)
        wbuf = P.sbuf("wbuf", [128, 4 * 2048], BF16)
        phase_xh_load(xh_fm)
        phase_ffn_gateup(xh_fm, wbuf)
    def seg9():
        phase_tm_proj_ln("ffn2d", "sc_h", 64, "sc_wd_bf", None, "sc_x3", "y",
                         final_out=True, stat_tiled=True)
    for i, seg in enumerate([seg0, seg1, seg2, seg3, seg4, seg5, seg6, seg7, seg7b, seg8, seg9]):
        if i < upto:
            with P.scope():
                seg()
    P.es.close()
    return P


def kernel(**inputs):
    x = np.asarray(inputs["L0_attn_a_x"])
    ins = {k: np.asarray(v) for k, v in inputs.items()}
    consts = host_constants()
    w = host_weights(ins)
    P = build(4096, dbg=False)
    in_maps = []
    for c in range(8):
        b, p = c // 2, c % 2
        m = {"x0": np.ascontiguousarray(x[b, p * 4096:(p + 1) * 4096, :])}
        m.update(w)
        m.update(consts)
        m["parity"] = np.full((128, 1), float(p), np.float32)
        in_maps.append(m)
    res = run_bass_kernel_spmd(P.nc, in_maps, core_ids=list(range(8)))
    out = np.empty((4, 8192, 2048), np.float32)
    for c in range(8):
        b, p = c // 2, c % 2
        out[b, p * 4096:(p + 1) * 4096, :] = res.results[c]["y"]
    return out


# version 11
# speedup vs baseline: 1.0456x; 1.0456x over previous
import contextlib
import math

import numpy as np
import ml_dtypes

import concourse.bass as bass
import concourse.mybir as mybir
from concourse.bass_utils import run_bass_kernel_spmd

F32 = mybir.dt.float32
BF16 = mybir.dt.bfloat16
AF = mybir.ActivationFunctionType
ALU = mybir.AluOpType
BF = ml_dtypes.bfloat16

D = 2048
DH = 8192
HA = 16
CA = 128
HK, HV, DK, DV = 16, 32, 128, 128
CK = 64
NEG = -1e30
LN_EPS = 1e-5
L2_EPS = 1e-6
RMS_EPS = 1e-6
SCALE_DK = DK ** -0.5


def _slopes16():
    start = 2.0 ** (-2.0 ** (-(math.log2(16) - 3)))
    return np.array([start ** (i + 1) for i in range(16)], dtype=np.float32)


def host_constants():
    c = {}
    lam = np.exp(-_slopes16()).astype(np.float64)
    pos = np.arange(CA, dtype=np.float64)
    dm = np.where(pos[None, :, None] >= pos[None, None, :],
                  lam[:, None, None] ** (pos[None, :, None] - pos[None, None, :]),
                  0.0)
    dmT = dm.transpose(2, 0, 1)
    c["c_dmaskT"] = np.ascontiguousarray(dmT.reshape(CA, HA * CA)).astype(BF)
    dq = lam[:, None] ** (pos[None, :] + 1.0)
    c["c_decq"] = np.ascontiguousarray(np.broadcast_to(dq[None], (128, HA, CA)).reshape(128, HA * CA)).astype(BF)
    dk_ = lam[:, None] ** (CA - 1.0 - pos[None, :])
    c["c_deck"] = np.ascontiguousarray(dk_.T.astype(np.float32))
    c["c_lamc"] = np.ascontiguousarray(np.broadcast_to((lam ** CA).astype(np.float32)[None], (128, HA)).copy())
    c["c_ones_bf"] = np.ones((128, 128), np.float32).astype(BF)
    c["c_ident_bf"] = np.eye(128, dtype=np.float32).astype(BF)
    tril_s = np.tril(np.ones((CK, CK), np.float32), -1)
    tril_i = np.tril(np.ones((CK, CK), np.float32))
    c["c_ubias_i"] = np.ascontiguousarray(np.tile(np.where(tril_i.T > 0, 0.0, NEG).astype(np.float32), (2, 8)))
    c["c_ubias_s"] = np.ascontiguousarray(np.tile(np.where(tril_s.T > 0, 0.0, NEG).astype(np.float32), (2, 8)))
    c["c_lbias_s"] = np.ascontiguousarray(np.tile(np.where(tril_s > 0, 0.0, NEG).astype(np.float32), (2, 8)))
    b2 = np.zeros((32, HK, 128), np.float32)
    for j in range(HK):
        b2[j, j, 0:64] = 1.0
        b2[j + 16, j, 64:128] = 1.0
    c["c_sel2"] = np.ascontiguousarray(b2.reshape(32, HK * 128))
    c["c_ident_f32"] = np.eye(128, dtype=np.float32)
    b1 = np.zeros((32, HV, 128), np.float32)
    for h in range(HV):
        b1[h, h, :] = 1.0
    c["c_sel1"] = np.ascontiguousarray(b1.reshape(32, HV * 128))
    return c


def host_weights(i):
    w = {}
    wq = i["L0_attn_a_w_qkv"]
    w["w_qT"] = np.ascontiguousarray(wq[0:D].T)
    w["w_kT"] = np.ascontiguousarray(wq[D:2 * D].T)
    w["w_vT"] = np.ascontiguousarray(wq[2 * D:3 * D].T)
    bq = i["L0_attn_a_b_qkv"]
    w["b_q"] = np.ascontiguousarray(bq[0:D].reshape(16, 128).T)
    w["b_k"] = np.ascontiguousarray(bq[D:2 * D].reshape(16, 128).T)
    w["b_v_rep"] = np.ascontiguousarray(np.broadcast_to(bq[2 * D:3 * D][None, :], (128, D)).copy())
    w["b_v"] = np.ascontiguousarray(bq[2 * D:3 * D].reshape(16, 128).T)
    w["w_oT"] = np.ascontiguousarray(i["L0_attn_a_w_out"].T)
    w["b_o_rep"] = np.ascontiguousarray(np.broadcast_to(i["L0_attn_a_b_out"][None, :], (128, D)).copy())
    w["w_gT"] = np.ascontiguousarray(i["L0_ffn_w_gate"].T)
    w["w_uT"] = np.ascontiguousarray(i["L0_ffn_w_up"].T)
    w["w_dT"] = np.ascontiguousarray(i["L0_ffn_w_down"].T)
    VPERM = [2 * j for j in range(16)] + [2 * j + 1 for j in range(16)]
    Wqkvz = i["L0_attn_b_W_qkvz"]
    blk = 2 * DK + 2 * 2 * DV
    cols = []
    for h in range(HK):
        cols.append(np.arange(h * blk, h * blk + DK))
    for h in range(HK):
        cols.append(np.arange(h * blk + DK, h * blk + 2 * DK))
    for dv in VPERM:
        h, r = dv // 2, dv % 2
        cols.append(np.arange(h * blk + 2 * DK + r * DV, h * blk + 2 * DK + (r + 1) * DV))
    for dv in VPERM:
        h, r = dv // 2, dv % 2
        cols.append(np.arange(h * blk + 2 * DK + 2 * DV + r * DV, h * blk + 2 * DK + 2 * DV + (r + 1) * DV))
    w["w_qkvzT"] = np.ascontiguousarray(Wqkvz[:, np.concatenate(cols)])
    Wba = i["L0_attn_b_W_ba"]
    bcols = [(ov // 2) * 4 + (ov % 2) for ov in VPERM]
    acols = [(ov // 2) * 4 + 2 + (ov % 2) for ov in VPERM]
    w["w_baT"] = np.ascontiguousarray(Wba[:, bcols + acols])
    cw = i["L0_attn_b_conv_w"].reshape(2, HK, DK, 4) if False else i["L0_attn_b_conv_w"]
    vrows = np.concatenate([np.arange(4096 + ov * DV, 4096 + (ov + 1) * DV) for ov in VPERM])
    rows = np.concatenate([np.arange(0, 4096), vrows])
    cwp = cw[rows]
    w["convw"] = np.ascontiguousarray(cwp.reshape(64, 128, 4).transpose(1, 0, 2).reshape(128, 256))
    w["dt_bias"] = np.ascontiguousarray(i["L0_attn_b_dt_bias"][VPERM].reshape(32, 1))
    w["A_log"] = np.ascontiguousarray(i["L0_attn_b_A_log"][VPERM].reshape(32, 1))
    w["normw"] = np.ascontiguousarray(i["L0_attn_b_norm_weight"].reshape(128, 1))
    wrows = np.concatenate([np.arange(ov * DV, (ov + 1) * DV) for ov in VPERM])
    w["w_outB"] = np.ascontiguousarray(i["L0_attn_b_W_out"][wrows])
    return w


WEIGHT_SHAPES = {
    "w_qT": [D, D], "w_kT": [D, D], "w_vT": [D, D], "w_oT": [D, D],
    "b_q": [128, 16], "b_k": [128, 16], "b_v": [128, 16], "b_v_rep": [128, D], "b_o_rep": [128, D],
    "w_gT": [D, DH], "w_uT": [D, DH], "w_dT": [DH, D],
    "w_qkvzT": [D, 12288], "w_baT": [D, 64], "convw": [128, 256],
    "dt_bias": [32, 1], "A_log": [32, 1], "normw": [128, 1], "w_outB": [4096, D],
}
CONST_SHAPES = {
    "c_dmaskT": ([CA, HA * CA], BF16), "c_decq": ([128, HA * CA], BF16),
    "c_deck": ([128, HA], F32), "c_lamc": ([128, HA], F32),
    "c_ones_bf": ([128, 128], BF16), "c_ident_bf": ([128, 128], BF16),
    "c_ubias_i": ([128, 8 * CK], F32), "c_ubias_s": ([128, 8 * CK], F32),
    "c_lbias_s": ([128, 8 * CK], F32),
    "c_sel2": ([32, HK * 128], F32), "c_sel1": ([32, HV * 128], F32),
    "c_ident_f32": ([128, 128], F32),
    "parity": ([128, 1], F32),
}
SB_CONSTS = ["c_dmaskT", "c_decq", "c_deck", "c_lamc", "c_ones_bf", "c_ident_bf",
             "c_strictneg", "c_striuneg", "c_inclbias", "c_upperbias",
             "c_sel2", "c_sel1", "parity", "b_q", "b_k", "dt_bias", "A_log", "normw"]


class Prog:
    def __init__(self, T=4096, dbg=False):
        self.nc = bass.Bass("TRN2", target_bir_lowering=False)
        self.T = T
        self.dbg = dbg
        self.es = contextlib.ExitStack()
        self.scopes = [self.es]
        self.scope_sems = [[]]
        self.d = {}
        self._sem_n = 0
        self._sb_n = 0

    def dram(self, name, shape, dtype, kind=None):
        t = self.nc.dram_tensor(name, shape, dtype) if kind is None else \
            self.nc.dram_tensor(name, shape, dtype, kind=kind)
        self.d[name] = t.ap()

    def scratch(self, name, shape, dtype):
        self.dram(name, shape, dtype, "ExternalOutput" if self.dbg else None)

    def sbuf(self, name, shape, dtype):
        self._sb_n += 1
        return self.scopes[-1].enter_context(
            self.nc.sbuf_tensor(f"{name}_{self._sb_n}", shape, dtype))

    def sem(self):
        self._sem_n += 1
        h = self.nc.alloc_semaphore(f"s{self._sem_n}")
        self.scope_sems[-1].append(h)
        return h

    def sems(self, n):
        return [self.sem() for _ in range(n)]

    @contextlib.contextmanager
    def scope(self):
        es = contextlib.ExitStack()
        self.scopes.append(es)
        self.scope_sems.append([])
        try:
            with es:
                yield
        finally:
            self.scopes.pop()
            freed = self.scope_sems.pop()
            if freed:
                self.nc.clear_and_free_semaphores(freed)


def build(T=4096, dbg=False, upto=99, nocc=False):
    P = Prog(T, dbg)
    nc = P.nc
    NT = T // 128
    NCA = T // CA
    NCG = T // CK
    MAXC = max(T, 2048)

    P.dram("x0", [T, D], F32, "ExternalInput")
    for n, sh in WEIGHT_SHAPES.items():
        P.dram(n, sh, F32, "ExternalInput")
    for n, (sh, dt) in CONST_SHAPES.items():
        P.dram(n, sh, dt, "ExternalInput")
    P.dram("y", [T, D], F32, "ExternalOutput")

    P.scratch("sc_xh", [T, D], BF16)
    P.scratch("sc_x1", [T, D], F32)
    P.scratch("sc_x2", [T, D], F32)
    P.scratch("sc_x3", [T, D], F32)
    P.scratch("sc_q", [D, T], BF16)
    P.scratch("sc_k", [D, T], BF16)
    P.scratch("sc_v", [D, T], BF16)
    P.scratch("sc_kd", [D, T], BF16)
    P.scratch("sc_vt", [D, T], BF16)
    P.scratch("sc_o", [D, T], BF16)
    P.scratch("sc_h", [(DH // 128) * (T // 128), 128, 128], BF16)
    P.scratch("sc_wo_bf", [D, D], BF16)
    P.scratch("sc_wd_bf", [DH, D], BF16)
    P.scratch("sc_wob_bf", [4096, D], BF16)
    P.scratch("sc_qg", [D, T], BF16)
    P.scratch("sc_kg", [D, T], BF16)
    P.scratch("sc_vg", [HV * DV, T], BF16)
    P.scratch("sc_z", [HV * DV, T], BF16)
    P.scratch("sc_xgn", [HV * DV, T], BF16)
    P.scratch("sc_ba", [64, T], F32)
    P.scratch("sc_gba", [96, T], F32)
    P.scratch("sc_cm", [5, NCG // 2, 128, 32], F32)
    P.scratch("sc_X", [NCG * 16, 128, 256], BF16)
    P.scratch("sc_kd2", [NCG * 16, 128, 128], BF16)
    P.scratch("sc_kdn", [NCG * 16, 128, 128], BF16)
    P.scratch("sc_AT", [(NCG // 8) * 16, 128, 512], BF16)
    P.scratch("sc_qdT", [HV * (NCG // 8), 128, 512], BF16)
    P.dram("cc1_in", [HA * 128, 128], F32)
    P.dram("cc1_out", [2 * HA * 128, 128], F32)
    P.dram("cc2_in", [DH, 4], F32)
    P.dram("cc2_out", [2 * DH, 4], F32)
    P.dram("cc3_in", [HV * 128, 128], F32)
    P.dram("cc3_out", [2 * HV * 128, 128], F32)
    GROUPS = [[0, 1], [2, 3], [4, 5], [6, 7]]
    dd = P.d

    cs = {}
    for n in ["c_deck", "c_lamc", "c_ones_bf", "c_ident_bf", "c_ident_f32", "parity",
              "c_ubias_i", "c_ubias_s", "c_lbias_s",
              "b_q", "b_k", "b_v", "dt_bias", "A_log", "normw"]:
        if n in CONST_SHAPES:
            sh, dt = CONST_SHAPES[n]
        elif n == "b_v":
            sh, dt = [128, 16], F32
        else:
            sh, dt = WEIGHT_SHAPES[n], F32
        cs[n] = P.sbuf("sb_" + n, sh, dt)
    eps_ln = P.sbuf("eps_ln", [128, 1], F32)
    eps_l2 = P.sbuf("eps_l2", [128, 1], F32)
    eps_rms = P.sbuf("eps_rms", [128, 1], F32)


    pb = [P.es.enter_context(nc.psum_tensor(f"pb{i}", [128, 512], F32)) for i in range(8)]

    s_pre = P.sem()

    def phase_consts():
        s = P.sem()
        names = ["c_deck", "c_lamc", "c_ones_bf", "c_ident_bf", "c_ident_f32", "parity",
                 "c_ubias_i", "c_ubias_s", "c_lbias_s",
                 "b_q", "b_k", "b_v", "dt_bias", "A_log", "normw"]
        with nc.Block() as b:
            @b.sync
            def _(sync):
                for n in names:
                    sync.dma_start(cs[n][:, :], dd[n][:, :]).then_inc(s, 16)
                sync.wait_ge(s, 16 * len(names))

            @b.vector
            def _(ve):
                ve.memset(eps_ln[:, :], LN_EPS)
                ve.memset(eps_l2[:, :], L2_EPS)
                ve.memset(eps_rms[:, :], RMS_EPS)

            @b.gpsimd
            def _(gp):
                gp.dma_start(dd["sc_wo_bf"][:, :], dd["w_oT"][:, :]).then_inc(s_pre, 16)
                gp.dma_start(dd["sc_wd_bf"][:, :], dd["w_dT"][:, :]).then_inc(s_pre, 16)
                gp.dma_start(dd["sc_wob_bf"][:, :], dd["w_outB"][:, :]).then_inc(s_pre, 16)


    def phase_ln_standalone(srcname):
        lda = P.sems(2)
        sts = P.sems(2)
        ves, ses = P.sem(), P.sem()
        xt = P.sbuf("xt", [128, 2 * D], F32)
        xht = P.sbuf("xht", [128, 2 * D], BF16)
        stats = P.sbuf("stats", [128, 64], F32)
        aggr = P.sbuf("aggr", [128, 8], F32)
        with nc.Block() as b:
            @b.sync
            def _(sync):
                for i in range(NT):
                    bu = i % 2
                    if i >= 2:
                        sync.wait_ge(ves, 2 * (i - 1))
                    sync.dma_start(xt[:, bu * D:(bu + 1) * D],
                                   dd[srcname][i * 128:(i + 1) * 128, :]).then_inc(lda[bu], 16)

            @b.vector
            def _(ve):
                for i in range(NT):
                    bu = i % 2
                    ve.wait_ge(lda[bu], 16 * (i // 2 + 1))
                    for j in range(4):
                        ve.bn_stats(stats[:, bu * 32 + 6 * j:bu * 32 + 6 * j + 6],
                                    xt[:, bu * D + 512 * j:bu * D + 512 * j + 512])
                    ve.drain()
                    ve.bn_aggr(aggr[:, bu * 4:bu * 4 + 2], stats[:, bu * 32:bu * 32 + 24]).then_inc(ves, 1)
                    ve.wait_ge(ses, i + 1)
                    ve.reciprocal(aggr[:, bu * 4 + 3:bu * 4 + 4], aggr[:, bu * 4 + 2:bu * 4 + 3])
                    ve.drain()
                    if i >= 2:
                        ve.wait_ge(sts[bu], 16 * (i // 2))
                    ve.tensor_scalar(xht[:, bu * D:(bu + 1) * D], xt[:, bu * D:(bu + 1) * D],
                                     aggr[:, bu * 4:bu * 4 + 1], aggr[:, bu * 4 + 3:bu * 4 + 4],
                                     op0=ALU.subtract, op1=ALU.mult).then_inc(ves, 1)

            @b.scalar
            def _(sc):
                for i in range(NT):
                    bu = i % 2
                    sc.wait_ge(ves, 2 * i + 1)
                    sc.activation(aggr[:, bu * 4 + 2:bu * 4 + 3], aggr[:, bu * 4 + 1:bu * 4 + 2],
                                  AF.Sqrt, bias=eps_ln[:, 0:1], scale=1.0).then_inc(ses, 1)

            @b.gpsimd
            def _(gp):
                for i in range(NT):
                    bu = i % 2
                    gp.wait_ge(ves, 2 * (i + 1))
                    gp.dma_start(dd["sc_xh"][i * 128:(i + 1) * 128, :],
                                 xht[:, bu * D:(bu + 1) * D]).then_inc(sts[bu], 16)
                gp.wait_ge(sts[0], 16 * ((NT + 1) // 2))
                gp.wait_ge(sts[1], 16 * (NT // 2))

    def phase_xh_load(xh_fm):
        s = P.sem()
        with nc.Block() as b:
            @b.sync
            def _(sync):
                for t in range(16):
                    sync.dma_start_transpose(xh_fm[:, t * T:(t + 1) * T],
                                             dd["sc_xh"][:, t * 128:(t + 1) * 128]).then_inc(s, 16)
                sync.wait_ge(s, 256)

    def phase_fm_proj(xh_fm, wbuf, evb, wname, dout, dstname, bias_col=None, bias_off=0, row0=0):
        ndt = dout // 128
        nst = max(T // 512, 1)
        halves = 2 if nst >= 2 else 1
        nh = nst // halves
        ldw = P.sems(4)
        sts = P.sems(2)
        pes, ves = P.sem(), P.sem()
        wd = dd[wname]
        dst = dd[dstname]
        with nc.Block() as b:
            @b.gpsimd
            def _(gp):
                for t in range(ndt):
                    wb = t % 4
                    if t >= 4:
                        gp.wait_ge(pes, halves * (t - 3))
                    srcw = wd[:, t * 128:(t + 1) * 128].rearrange("(a p) m -> p a m", p=128)
                    dstw = wbuf[:, wb * 2048:(wb + 1) * 2048].rearrange("p (a m) -> p a m", a=16)
                    gp.dma_start(dstw, srcw).then_inc(ldw[wb], 16)

            @b.tensor
            def _(te):
                for t in range(ndt):
                    wb = t % 4
                    te.wait_ge(ldw[wb], 16 * (t // 4 + 1))
                    for half in range(halves):
                        g = halves * t + half
                        if g >= halves:
                            te.wait_ge(ves, g - halves + 1)
                        for k in range(16):
                            for s_ in range(nh):
                                si = half * nh + s_
                                mm = te.matmul(pb[half * 4 + s_][:, 0:512],
                                               wbuf[:, wb * 2048 + k * 128:wb * 2048 + (k + 1) * 128],
                                               xh_fm[:, k * T + si * 512:k * T + si * 512 + 512],
                                               start=(k == 0), stop=(k == 15))
                        mm.then_inc(pes, 1)

            @b.vector
            def _(ve):
                for t in range(ndt):
                    eb = (t % 2) * MAXC
                    for half in range(halves):
                        g = halves * t + half
                        ve.wait_ge(pes, g + 1)
                        if half == 0 and t >= 2:
                            ve.wait_ge(sts[t % 2], 16 * (t // 2))
                        for s_ in range(nh):
                            si = half * nh + s_
                            op = ve.tensor_scalar(evb[:, eb + si * 512:eb + (si + 1) * 512],
                                                  pb[half * 4 + s_][:, 0:512],
                                                  cs[bias_col][:, bias_off + t:bias_off + t + 1] if bias_col else 0.0,
                                                  None, op0=ALU.add)
                        op.then_inc(ves, 1)

            @b.sync
            def _(sync):
                for t in range(ndt):
                    eb = (t % 2) * MAXC
                    sync.wait_ge(ves, halves * (t + 1))
                    sync.dma_start(dst[row0 + t * 128:row0 + (t + 1) * 128, :],
                                   evb[:, eb:eb + T]).then_inc(sts[t % 2], 16)
                sync.wait_ge(sts[0], 16 * ((ndt + 1) // 2))
                sync.wait_ge(sts[1], 16 * (ndt // 2))

    def phase_lightning():
        NC = NCA
        nh2 = HA // 2
        dmask = P.sbuf("la_dmask", [CA, HA * CA], BF16)
        decq = P.sbuf("la_decq", [128, HA * CA], BF16)
        qf = [P.sbuf(f"la_qf{s}", [128, T], BF16) for s in range(2)]
        kf = [P.sbuf(f"la_kf{s}", [128, T], BF16) for s in range(2)]
        vf = [P.sbuf(f"la_vf{s}", [128, T], BF16) for s in range(2)]
        vt = [P.sbuf(f"la_vt{s}", [128, T], BF16) for s in range(2)]
        ktm = [P.sbuf(f"la_ktm{s}", [128, T], BF16) for s in range(2)]
        scT = [P.sbuf(f"la_scT{s}", [128, 128], BF16) for s in range(2)]
        qdc = [P.sbuf(f"la_qdc{s}", [128, 2 * 128], BF16) for s in range(2)]
        S32 = [P.sbuf(f"la_S32{s}", [128, 128], F32) for s in range(2)]
        Sbf = [P.sbuf(f"la_Sbf{s}", [128, 128], BF16) for s in range(2)]
        Sini = [P.sbuf(f"la_Sini{s}", [128, 128], F32) for s in range(2)]
        ob = [P.sbuf(f"la_ob{s}", [128, T], BF16) for s in range(2)]
        pT = [pb[6][:, :].bitcast(BF16), pb[7][:, :].bitcast(BF16)]

        sc_load = P.sem()

        def pass1():
            lda = [P.sems(1)[0], P.sems(1)[0]]
            peT = P.sems(2); peS = P.sems(2); peO = P.sems(2); peKV = P.sems(2)
            veK = P.sems(2); veS = P.sems(2); veV = P.sems(2); veO = P.sems(2); veKV = P.sems(2)
            sK = P.sems(2); sS = P.sems(2); sO = P.sems(2)
            with nc.Block() as b:
                @b.sync
                def _(sync):
                    for hi in range(nh2):
                        for s in range(2):
                            h = 2 * hi + s
                            if hi >= 1:
                                sync.wait_ge(veKV[s], hi * NC)
                            sync.dma_start(qf[s][:, :], dd["sc_q"][h * 128:(h + 1) * 128, :]).then_inc(lda[s], 16)
                            sync.dma_start(kf[s][:, :], dd["sc_k"][h * 128:(h + 1) * 128, :]).then_inc(lda[s], 16)
                            sync.dma_start(vf[s][:, :], dd["sc_v"][h * 128:(h + 1) * 128, :]).then_inc(lda[s], 16)

                @b.tensor
                def _(te):
                    for hi in range(nh2):
                        for c in range(NC):
                            for s in range(2):
                                i = hi * NC + c
                                if c == 0:
                                    te.wait_ge(lda[s], 48 * (hi + 1))
                                if i > 0:
                                    te.wait_ge(veK[s], i)
                                te.transpose(pT[s][:, 0:128], vf[s][:, c * 128:(c + 1) * 128],
                                             cs["c_ident_bf"][:, :]).then_inc(peT[s], 1)
                                if i > 0:
                                    te.wait_ge(veS[s], i)
                                te.matmul(pb[3 * s][:, 0:128], kf[s][:, c * 128:(c + 1) * 128],
                                          qf[s][:, c * 128:(c + 1) * 128],
                                          start=True, stop=True).then_inc(peS[s], 1)
                                te.wait_ge(veV[s], i + 1)
                                te.transpose(pT[s][:, 128:256], kf[s][:, c * 128:(c + 1) * 128],
                                             cs["c_ident_bf"][:, :]).then_inc(peT[s], 1)
                                te.wait_ge(veS[s], i + 1)
                                if i > 0:
                                    te.wait_ge(veO[s], i)
                                te.matmul(pb[3 * s + 1][:, 0:128], vt[s][:, c * 128:(c + 1) * 128],
                                          scT[s][:, 0:128], start=True, stop=True).then_inc(peO[s], 1)
                                te.wait_ge(veK[s], i + 1)
                                if i > 0:
                                    te.wait_ge(veKV[s], i)
                                te.matmul(pb[3 * s + 2][:, 0:128], ktm[s][:, c * 128:(c + 1) * 128],
                                          vt[s][:, c * 128:(c + 1) * 128],
                                          start=True, stop=True).then_inc(peKV[s], 1)

                @b.vector
                def _(ve):
                    ve.wait_ge(sc_load, 32)
                    for hi in range(nh2):
                        for c in range(NC):
                            for s in range(2):
                                h = 2 * hi + s
                                i = hi * NC + c
                                if c == 0 and hi >= 1:
                                    ve.wait_ge(sK[s], 32 * hi)
                                ve.wait_ge(peT[s], 2 * i + 1)
                                ve.tensor_copy(vt[s][:, c * 128:(c + 1) * 128], pT[s][:, 0:128]).then_inc(veV[s], 1)
                                ve.wait_ge(peT[s], 2 * i + 2)
                                ve.tensor_scalar(ktm[s][:, c * 128:(c + 1) * 128], pT[s][:, 128:256],
                                                 cs["c_deck"][:, h:h + 1], None, op0=ALU.mult).then_inc(veK[s], 1)
                                ve.wait_ge(peS[s], i + 1)
                                ve.tensor_tensor(scT[s][:, 0:128], pb[3 * s][:, 0:128],
                                                 dmask[:, h * 128:(h + 1) * 128],
                                                 op=ALU.mult).then_inc(veS[s], 1)
                                ve.wait_ge(peKV[s], i + 1)
                                if c == 0:
                                    if hi >= 1:
                                        ve.wait_ge(sS[s], 16 * hi)
                                    ve.tensor_copy(S32[s][:, :], pb[3 * s + 2][:, 0:128]).then_inc(veKV[s], 1)
                                else:
                                    ve.drain()
                                    ve.scalar_tensor_tensor(S32[s][:, :], S32[s][:, :],
                                                            cs["c_lamc"][:, h:h + 1], pb[3 * s + 2][:, 0:128],
                                                            op0=ALU.mult, op1=ALU.add).then_inc(veKV[s], 1)

                @b.scalar
                def _(sc):
                    sc.dma_start(dmask[:, :], dd["c_dmaskT"][:, :]).then_inc(sc_load, 16)
                    sc.dma_start(decq[:, :], dd["c_decq"][:, :]).then_inc(sc_load, 16)
                    for hi in range(nh2):
                        for c in range(NC):
                            for s in range(2):
                                h = 2 * hi + s
                                i = hi * NC + c
                                if c == 0 and hi >= 1:
                                    sc.wait_ge(sO[s], 16 * hi)
                                sc.wait_ge(peO[s], i + 1)
                                sc.copy(ob[s][:, c * 128:(c + 1) * 128],
                                        pb[3 * s + 1][:, 0:128]).then_inc(veO[s], 1)

                @b.gpsimd
                def _(gp):
                    for hi in range(nh2):
                        for s in range(2):
                            h = 2 * hi + s
                            gp.wait_ge(veK[s], (hi + 1) * NC)
                            gp.dma_start(dd["sc_kd"][h * 128:(h + 1) * 128, :], ktm[s][:, :]).then_inc(sK[s], 16)
                            gp.wait_ge(veV[s], (hi + 1) * NC)
                            gp.dma_start(dd["sc_vt"][h * 128:(h + 1) * 128, :], vt[s][:, :]).then_inc(sK[s], 16)
                            gp.wait_ge(veKV[s], (hi + 1) * NC)
                            gp.dma_start(dd["cc1_in"][h * 128:(h + 1) * 128, :], S32[s][:, :]).then_inc(sS[s], 16)
                            gp.wait_ge(veO[s], (hi + 1) * NC)
                            gp.dma_start(dd["sc_o"][h * 128:(h + 1) * 128, :], ob[s][:, :]).then_inc(sO[s], 16)
                    for s in range(2):
                        gp.wait_ge(sK[s], 32 * nh2)
                        gp.wait_ge(sS[s], 16 * nh2)
                        gp.wait_ge(sO[s], 16 * nh2)

        def xchg1():
            s = P.sem()
            with nc.Block() as b:
                @b.gpsimd
                def _(gp):
                    if nocc:
                        gp.dma_start(dd["cc1_out"][0:HA * 128, :], dd["cc1_in"][:, :]).then_inc(s, 16)
                        gp.wait_ge(s, 16)
                    else:
                        gp.collective_compute("AllGather", ALU.bypass,
                                              ins=[dd["cc1_in"][:, :].opt()],
                                              outs=[dd["cc1_out"][:, :].opt()],
                                              replica_groups=GROUPS).then_inc(s)
                        gp.wait_ge(s, 1)

        def pass2():
            lda = [P.sems(1)[0], P.sems(1)[0]]
            ldS = P.sems(2)
            peI = P.sems(2); peKV = P.sems(2)
            veQ = P.sems(2); veO = P.sems(2); veKV = P.sems(2); veSb = P.sems(2)
            sO = P.sems(2)
            with nc.Block() as b:
                @b.sync
                def _(sync):
                    for hi in range(nh2):
                        for s in range(2):
                            h = 2 * hi + s
                            if hi >= 1:
                                sync.wait_ge(veKV[s], hi * NC)
                                sync.wait_ge(sO[s], 16 * hi)
                            sync.dma_start(qf[s][:, :], dd["sc_q"][h * 128:(h + 1) * 128, :]).then_inc(lda[s], 16)
                            sync.dma_start(ktm[s][:, :], dd["sc_kd"][h * 128:(h + 1) * 128, :]).then_inc(lda[s], 16)
                            sync.dma_start(vt[s][:, :], dd["sc_vt"][h * 128:(h + 1) * 128, :]).then_inc(lda[s], 16)
                            sync.dma_start(ob[s][:, :], dd["sc_o"][h * 128:(h + 1) * 128, :]).then_inc(lda[s], 16)

                @b.scalar
                def _(sc):
                    for hi in range(nh2):
                        for s in range(2):
                            h = 2 * hi + s
                            if hi >= 1:
                                sc.wait_ge(veSb[s], (hi - 1) * (NC + 1) + 1)
                            sc.dma_start(Sini[s][:, :], dd["cc1_out"][h * 128:(h + 1) * 128, :]).then_inc(ldS[s], 16)

                @b.vector
                def _(ve):
                    for hi in range(nh2):
                        for c in range(NC):
                            for s in range(2):
                                h = 2 * hi + s
                                i = hi * NC + c
                                if c == 0:
                                    ve.wait_ge(ldS[s], 16 * (hi + 1))
                                    ve.tensor_scalar(S32[s][:, :], Sini[s][:, :], cs["parity"][:, 0:1],
                                                     None, op0=ALU.mult)
                                    ve.drain()
                                    ve.tensor_copy(Sbf[s][:, :], S32[s][:, :]).then_inc(veSb[s], 1)
                                ve.wait_ge(lda[s], 64 * (hi + 1))
                                if i > 0:
                                    ve.wait_ge(peI[s], i)
                                ve.tensor_tensor(qdc[s][:, (c % 2) * 128:(c % 2) * 128 + 128],
                                                 qf[s][:, c * 128:(c + 1) * 128],
                                                 decq[:, h * 128:(h + 1) * 128],
                                                 op=ALU.mult).then_inc(veQ[s], 1)
                                ve.wait_ge(peI[s], i + 1)
                                ve.tensor_tensor(ob[s][:, c * 128:(c + 1) * 128],
                                                 pb[3 * s + 1][:, 0:128],
                                                 ob[s][:, c * 128:(c + 1) * 128],
                                                 op=ALU.add).then_inc(veO[s], 1)
                                ve.wait_ge(peKV[s], i + 1)
                                ve.drain()
                                ve.scalar_tensor_tensor(S32[s][:, :], S32[s][:, :],
                                                        cs["c_lamc"][:, h:h + 1], pb[3 * s + 2][:, 0:128],
                                                        op0=ALU.mult, op1=ALU.add).then_inc(veKV[s], 1)
                                ve.drain()
                                ve.tensor_copy(Sbf[s][:, :], S32[s][:, :]).then_inc(veSb[s], 1)

                @b.tensor
                def _(te):
                    for hi in range(nh2):
                        for c in range(NC):
                            for s in range(2):
                                i = hi * NC + c
                                te.wait_ge(veSb[s], i + hi + 1)
                                te.wait_ge(veQ[s], i + 1)
                                if i > 0:
                                    te.wait_ge(veO[s], i)
                                te.matmul(pb[3 * s + 1][:, 0:128], Sbf[s][:, :],
                                          qdc[s][:, (c % 2) * 128:(c % 2) * 128 + 128],
                                          start=True, stop=True).then_inc(peI[s], 1)
                                if i > 0:
                                    te.wait_ge(veKV[s], i)
                                te.matmul(pb[3 * s + 2][:, 0:128], ktm[s][:, c * 128:(c + 1) * 128],
                                          vt[s][:, c * 128:(c + 1) * 128],
                                          start=True, stop=True).then_inc(peKV[s], 1)

                @b.gpsimd
                def _(gp):
                    for hi in range(nh2):
                        for s in range(2):
                            h = 2 * hi + s
                            gp.wait_ge(veO[s], (hi + 1) * NC)
                            gp.dma_start(dd["sc_o"][h * 128:(h + 1) * 128, :], ob[s][:, :]).then_inc(sO[s], 16)
                    gp.wait_ge(sO[0], 16 * nh2)
                    gp.wait_ge(sO[1], 16 * nh2)

        pass1()
        xchg1()
        pass2()

    def phase_tm_proj_ln(tag, stat_name, nk, w_bf_name, bias_rep_name, xold_name, xnew_name,
                         final_out=False, stat_tiled=False):
        ldw = P.sems(2)
        lds = P.sems(2)
        ldx = P.sems(4)
        ldbr = P.sem()
        pes, vesP, vesL, ses = P.sem(), P.sem(), P.sem(), P.sem()
        sto = P.sems(2)
        KG = 4
        npc = nk // KG
        wpc = P.sbuf(f"wpc_{tag}", [128, 2 * KG * 2048], BF16)
        xnw = P.sbuf(f"xnw_{tag}", [128, 2 * 2 * D], F32)
        xht = P.sbuf(f"xht_{tag}", [128, 2 * D], BF16)
        stats = P.sbuf(f"stats_{tag}", [128, 64], F32)
        aggr = P.sbuf(f"aggr_{tag}", [128, 8], F32)
        stb = P.sbuf(f"stb_{tag}", [128, 2 * nk * 256], BF16)
        brep = P.sbuf(f"brep_{tag}", [128, D], F32) if bias_rep_name else None
        xold = P.sbuf(f"xold_{tag}", [128, 2 * 2 * D], F32)
        wsrc = dd[w_bf_name]
        ssrc = dd[stat_name]
        NFL = NT // 2
        per = 1 if final_out else 2
        with nc.Block() as b:
            @b.sync
            def _(sync):
                sync.wait_ge(s_pre, 48)
                if bias_rep_name:
                    sync.dma_start(brep[:, :], dd[bias_rep_name][:, :]).then_inc(ldbr, 16)
                npt = 0
                for fl in range(NFL):
                    slot = fl % 2
                    if fl >= 2:
                        sync.wait_ge(pes, npc * (fl - 1))
                    ssl = stb[:, slot * nk * 256:(slot + 1) * nk * 256].rearrange(
                        "p (k c) -> p k c", k=nk)
                    if stat_tiled:
                        for tt in range(2):
                            t = 2 * fl + tt
                            sync.dma_start(ssl[:, :, tt * 128:(tt + 1) * 128],
                                           ssrc[t:nk * NT:NT, :, :].rearrange("k p c -> p k c")) \
                                .then_inc(lds[slot], 16)
                    else:
                        sync.dma_start(ssl, ssrc[:, 2 * fl * 128:(2 * fl + 2) * 128]
                                       .rearrange("(k p) c -> p k c", p=128)).then_inc(lds[slot], 16)
                    for tt in range(2):
                        t = 2 * fl + tt
                        if fl >= 2:
                            sync.wait_ge(vesL, 4 * (fl - 1))
                        sync.dma_start(xold[:, (slot * 2 + tt) * D:(slot * 2 + tt + 1) * D],
                                       dd[xold_name][t * 128:(t + 1) * 128, :]).then_inc(ldx[slot * 2 + tt], 16)
                    for pc in range(npc):
                        wslot = npt % 2
                        if npt >= 2:
                            sync.wait_ge(pes, npt - 1)
                        wsl = wpc[:, wslot * KG * 2048:(wslot + 1) * KG * 2048].rearrange("p (a m) -> p a m", a=KG)
                        sync.dma_start(wsl, wsrc[pc * KG * 128:(pc + 1) * KG * 128, :]
                                       .rearrange("(a p) m -> p a m", p=128)).then_inc(ldw[wslot], 16)
                        npt += 1

            @b.tensor
            def _(te):
                stper = 2 if stat_tiled else 1
                npt = 0
                for fl in range(NFL):
                    slot = fl % 2
                    te.wait_ge(lds[slot], 16 * stper * (fl // 2 + 1))
                    for pc in range(npc):
                        wslot = npt % 2
                        te.wait_ge(ldw[wslot], 16 * (npt // 2 + 1))
                        if pc == 0 and fl >= 1:
                            te.wait_ge(vesP, 2 * fl)
                        for kk in range(KG):
                            k = pc * KG + kk
                            for tt in range(2):
                                for q in range(4):
                                    mm = te.matmul(pb[tt * 4 + q][:, 0:512],
                                                   stb[:, slot * nk * 256 + k * 256 + tt * 128:
                                                       slot * nk * 256 + k * 256 + (tt + 1) * 128],
                                                   wpc[:, wslot * KG * 2048 + kk * 2048 + q * 512:
                                                       wslot * KG * 2048 + kk * 2048 + (q + 1) * 512],
                                                   start=(k == 0), stop=(k == nk - 1))
                        mm.then_inc(pes, 1)
                        npt += 1

            @b.vector
            def _(ve):
                if bias_rep_name:
                    ve.wait_ge(ldbr, 16)
                for fl in range(NFL):
                    slot = fl % 2
                    ve.wait_ge(pes, npc * (fl + 1))
                    ve.wait_ge(ldx[slot * 2], 16 * (fl // 2 + 1))
                    ve.wait_ge(ldx[slot * 2 + 1], 16 * (fl // 2 + 1))
                    for tt in range(2):
                        t = 2 * fl + tt
                        bu = slot * 2 + tt
                        if fl >= 2:
                            ve.wait_ge(vesL, 4 * (fl - 1))
                        for q in range(4):
                            op = ve.tensor_tensor(xnw[:, bu * D + q * 512:bu * D + (q + 1) * 512],
                                                  pb[tt * 4 + q][:, 0:512],
                                                  xold[:, bu * D + q * 512:bu * D + (q + 1) * 512],
                                                  op=ALU.add)
                        op.then_inc(vesP, 1)
                    for tt in range(2):
                        t = 2 * fl + tt
                        bu = slot * 2 + tt
                        xb = bu % 2
                        if bias_rep_name:
                            ve.drain()
                            for q in range(4):
                                ve.tensor_tensor(xnw[:, bu * D + q * 512:bu * D + (q + 1) * 512],
                                                 xnw[:, bu * D + q * 512:bu * D + (q + 1) * 512],
                                                 brep[:, q * 512:(q + 1) * 512], op=ALU.add)
                        ve.drain()
                        if final_out:
                            ve.tensor_copy(aggr[:, 4 * xb:4 * xb + 1], xnw[:, bu * D:bu * D + 1]).then_inc(vesL, 2)
                            continue
                        for j in range(4):
                            ve.bn_stats(stats[:, xb * 32 + 6 * j:xb * 32 + 6 * j + 6],
                                        xnw[:, bu * D + 512 * j:bu * D + 512 * j + 512])
                        ve.drain()
                        ve.bn_aggr(aggr[:, xb * 4:xb * 4 + 2], stats[:, xb * 32:xb * 32 + 24]).then_inc(vesL, 1)
                        ve.wait_ge(ses, t + 1)
                        ve.reciprocal(aggr[:, xb * 4 + 3:xb * 4 + 4], aggr[:, xb * 4 + 2:xb * 4 + 3])
                        ve.drain()
                        if fl >= 1:
                            ve.wait_ge(sto[tt], (16 * per) * fl)
                        ve.tensor_scalar(xht[:, xb * D:(xb + 1) * D], xnw[:, bu * D:(bu + 1) * D],
                                         aggr[:, xb * 4:xb * 4 + 1], aggr[:, xb * 4 + 3:xb * 4 + 4],
                                         op0=ALU.subtract, op1=ALU.mult).then_inc(vesL, 1)

            @b.scalar
            def _(sc):
                if not final_out:
                    for t in range(NT):
                        xb = t % 2
                        sc.wait_ge(vesL, 2 * t + 1)
                        sc.activation(aggr[:, xb * 4 + 2:xb * 4 + 3], aggr[:, xb * 4 + 1:xb * 4 + 2],
                                      AF.Sqrt, bias=eps_ln[:, 0:1], scale=1.0).then_inc(ses, 1)

            @b.gpsimd
            def _(gp):
                for fl in range(NFL):
                    slot = fl % 2
                    for tt in range(2):
                        t = 2 * fl + tt
                        bu = slot * 2 + tt
                        xb = bu % 2
                        gp.wait_ge(vesL, 4 * fl + 2 * (tt + 1))
                        if final_out:
                            gp.dma_start(dd["y"][t * 128:(t + 1) * 128, :],
                                         xnw[:, bu * D:(bu + 1) * D]).then_inc(sto[tt], 16)
                        else:
                            gp.dma_start(dd[xnew_name][t * 128:(t + 1) * 128, :],
                                         xnw[:, bu * D:(bu + 1) * D]).then_inc(sto[tt], 16)
                            gp.dma_start(dd["sc_xh"][t * 128:(t + 1) * 128, :],
                                         xht[:, xb * D:(xb + 1) * D]).then_inc(sto[tt], 16)
                gp.wait_ge(sto[0], (16 * per) * NFL)
                gp.wait_ge(sto[1], (16 * per) * NFL)

    def phase_ffn_gateup(xh_fm, wbuf):
        NHT = DH // 128
        ldg = P.sems(2)
        ldu = P.sems(2)
        sth = P.sems(2)
        pes, ves, ses = P.sem(), P.sem(), P.sem()
        hstage = P.sbuf("hstage", [128, 2 * 1024], BF16)
        sstage = P.sbuf("sstage", [128, 2 * 1024], BF16)
        ngr = max(T // 1024, 1)
        nstr = min(2, max(T // 512, 1))
        with nc.Block() as b:
            @b.gpsimd
            def _(gp):
                for ht in range(NHT):
                    wb = ht % 2
                    if ht >= 2:
                        gp.wait_ge(pes, ngr * (ht - 1))
                    sg = dd["w_gT"][:, ht * 128:(ht + 1) * 128].rearrange("(a p) m -> p a m", p=128)
                    su = dd["w_uT"][:, ht * 128:(ht + 1) * 128].rearrange("(a p) m -> p a m", p=128)
                    gp.dma_start(wbuf[:, wb * 2048:(wb + 1) * 2048].rearrange("p (a m) -> p a m", a=16),
                                 sg).then_inc(ldg[wb], 16)
                    gp.dma_start(wbuf[:, (2 + wb) * 2048:(3 + wb) * 2048].rearrange("p (a m) -> p a m", a=16),
                                 su).then_inc(ldu[wb], 16)

            @b.tensor
            def _(te):
                for ht in range(NHT):
                    wb = ht % 2
                    te.wait_ge(ldg[wb], 16 * (ht // 2 + 1))
                    te.wait_ge(ldu[wb], 16 * (ht // 2 + 1))
                    for g in range(ngr):
                        gi = ht * ngr + g
                        bb = (gi % 2) * 4
                        if gi >= 2:
                            te.wait_ge(ves, gi - 1)
                        for k in range(16):
                            for s_ in range(nstr):
                                col = (2 * g + s_) * 512
                                mm = te.matmul(pb[bb + s_][:, 0:512],
                                               wbuf[:, wb * 2048 + k * 128:wb * 2048 + (k + 1) * 128],
                                               xh_fm[:, k * T + col:k * T + col + 512],
                                               start=(k == 0), stop=(k == 15))
                                mm2 = te.matmul(pb[bb + 2 + s_][:, 0:512],
                                                wbuf[:, (2 + wb) * 2048 + k * 128:(2 + wb) * 2048 + (k + 1) * 128],
                                                xh_fm[:, k * T + col:k * T + col + 512],
                                                start=(k == 0), stop=(k == 15))
                        mm2.then_inc(pes, 1)

            @b.scalar
            def _(sc):
                for ht in range(NHT):
                    for g in range(ngr):
                        gi = ht * ngr + g
                        bb = (gi % 2) * 4
                        sl = gi % 2
                        sc.wait_ge(pes, gi + 1)
                        if gi >= 2:
                            sc.wait_ge(ves, gi - 1)
                        for s_ in range(nstr):
                            sc.activation(sstage[:, sl * 1024 + s_ * 512:sl * 1024 + (s_ + 1) * 512],
                                          pb[bb + s_][:, 0:512], AF.Silu).then_inc(ses, 1)

            @b.vector
            def _(ve):
                for ht in range(NHT):
                    for g in range(ngr):
                        gi = ht * ngr + g
                        bb = (gi % 2) * 4
                        sl = gi % 2
                        ve.wait_ge(ses, nstr * (gi + 1))
                        if gi >= 2:
                            ve.wait_ge(sth[sl], 16 * (gi // 2))
                        for s_ in range(nstr):
                            op = ve.tensor_tensor(hstage[:, sl * 1024 + s_ * 512:sl * 1024 + (s_ + 1) * 512],
                                                  pb[bb + 2 + s_][:, 0:512],
                                                  sstage[:, sl * 1024 + s_ * 512:sl * 1024 + (s_ + 1) * 512],
                                                  op=ALU.mult)
                        op.then_inc(ves, 1)

            @b.sync
            def _(sync):
                for ht in range(NHT):
                    for g in range(ngr):
                        gi = ht * ngr + g
                        sl = gi % 2
                        nt_g = nstr * 4
                        sync.wait_ge(ves, gi + 1)
                        dst = dd["sc_h"][ht * NT + g * nt_g:ht * NT + (g + 1) * nt_g, :, :] \
                            .rearrange("t p c -> p t c")
                        sync.dma_start(dst, hstage[:, sl * 1024:sl * 1024 + nt_g * 128]
                                       .rearrange("p (t c) -> p t c", c=128)).then_inc(sth[sl], 16)
                sync.wait_ge(sth[0], 16 * ((NHT * ngr + 1) // 2))
                sync.wait_ge(sth[1], 16 * ((NHT * ngr) // 2))

    def phase_gdn_qkvz(xh_fm, wbuf, own3_sb):
        NCT = 96
        nst = T // 512
        spq = min(2, nst)
        nqt = max(nst // spq, 1)
        ldw = P.sems(4)
        ldc = P.sem()
        peB, veB = P.sem(), P.sem()
        peA, veA, veD, veM, peC, seC = P.sem(), P.sem(), P.sem(), P.sem(), P.sem(), P.sem()
        stQ = P.sems(2)
        stH = P.sems(2)
        stB = P.sem()
        ba_st = P.sbuf("ba_st", [64, 2 * 512], F32)
        mixp = P.sbuf("mixp", [128, 2 * (T + 3)], BF16)
        dstat = P.sbuf("dstat", [128, 2 * 4 * 128], BF16)
        cvw = P.sbuf("cvw", [128, 256], F32)
        wba_sb = P.sbuf("wba_sb", [128, 1024], BF16)
        ldc2 = P.sem()
        halo3 = P.sbuf("halo3", [128, 2 * 4], BF16)
        evq = P.sbuf("evq", [128, 2 * T], BF16)
        NCV = 64

        with nc.Block() as b:
            @b.gpsimd
            def _(gp):
                gp.dma_start(wba_sb[:, :].rearrange("p (a m) -> p a m", a=16),
                             dd["w_baT"][:, :].rearrange("(a p) m -> p a m", p=128)).then_inc(ldc2, 16)
                for si in range(nst):
                    sl = si % 2
                    gp.wait_ge(veB, si + 1)
                    gp.dma_start(dd["sc_ba"][0:64, si * 512:(si + 1) * 512],
                                 ba_st[0:64, sl * 512:(sl + 1) * 512]).then_inc(stB, 16)
                for ct in range(NCT):
                    wb = ct % 4
                    if ct >= 4:
                        gp.wait_ge(peA, nqt * (ct - 3))
                    srcw = dd["w_qkvzT"][:, ct * 128:(ct + 1) * 128].rearrange("(a p) m -> p a m", p=128)
                    gp.dma_start(wbuf[:, wb * 2048:(wb + 1) * 2048].rearrange("p (a m) -> p a m", a=16),
                                 srcw).then_inc(ldw[wb], 16)
                    hc = ct - 2
                    if 0 <= hc < NCV:
                        gp.wait_ge(veM, hc + 1)
                        gp.dma_start(dd["cc2_in"][hc * 128:(hc + 1) * 128, 0:4],
                                     halo3[:, (hc % 2) * 4:(hc % 2) * 4 + 4]).then_inc(stH[hc % 2], 16)
                for hc in range(NCT - 2, NCV):
                    gp.wait_ge(veM, hc + 1)
                    gp.dma_start(dd["cc2_in"][hc * 128:(hc + 1) * 128, 0:4],
                                 halo3[:, (hc % 2) * 4:(hc % 2) * 4 + 4]).then_inc(stH[hc % 2], 16)
                gp.wait_ge(stH[0], 16 * (NCV // 2))
                gp.wait_ge(stH[1], 16 * (NCV // 2))
                gp.wait_ge(stB, 16 * nst)

            @b.sync
            def _(sync):
                sync.dma_start(cvw[:, :], dd["convw"][:, :]).then_inc(ldc, 16)
                for ct in range(NCT):
                    sl = ct % 2
                    if ct < NCV:
                        sync.wait_ge(seC, nst * (ct + 1))
                    else:
                        sync.wait_ge(veA, nqt * (ct + 1))
                    dn = "sc_qg" if ct < 16 else ("sc_kg" if ct < 32 else ("sc_vg" if ct < NCV else "sc_z"))
                    r0 = (ct % 16) * 128 if ct < 32 else ((ct - 32) * 128 if ct < NCV else (ct - NCV) * 128)
                    sync.dma_start(dd[dn][r0:r0 + 128, :], evq[:, sl * T:sl * T + T]).then_inc(stQ[sl], 16)
                sync.wait_ge(stQ[0], 16 * (NCT // 2))
                sync.wait_ge(stQ[1], 16 * (NCT // 2))

            @b.tensor
            def _(te):
                te.wait_ge(ldc2, 16)
                for si in range(nst):
                    if si >= 1:
                        te.wait_ge(veB, si)
                    for k in range(16):
                        mm = te.matmul(pb[6][0:64, 0:512], wba_sb[:, k * 64:(k + 1) * 64],
                                       xh_fm[:, k * T + si * 512:k * T + si * 512 + 512],
                                       start=(k == 0), stop=(k == 15))
                    mm.then_inc(peB, 1)
                for ct in range(NCT):
                    wb = ct % 4
                    sl = ct % 2
                    te.wait_ge(ldw[wb], 16 * (ct // 4 + 1))
                    rd = min(nqt, 2)
                    for qt in range(nqt):
                        g = nqt * ct + qt
                        if g >= rd:
                            te.wait_ge(veA, g - rd + 1)
                        for k in range(16):
                            for s_ in range(spq):
                                si = spq * qt + s_
                                mm = te.matmul(pb[(qt % 2) * 2 + s_][:, 0:512],
                                               wbuf[:, wb * 2048 + k * 128:wb * 2048 + (k + 1) * 128],
                                               xh_fm[:, k * T + si * 512:k * T + si * 512 + 512],
                                               start=(k == 0), stop=(k == 15))
                        mm.then_inc(peA, 1)
                    if ct < NCV:
                        te.wait_ge(veM, ct + 1)
                        te.wait_ge(veD, ct + 1)
                        for s_ in range(nst):
                            ci = ct * nst + s_
                            if ci >= 4:
                                te.wait_ge(seC, ci - 3)
                            for j in range(4):
                                mm = te.matmul(pb[4 + ci % 4][:, 0:512],
                                               dstat[:, sl * 512 + j * 128:sl * 512 + (j + 1) * 128],
                                               mixp[:, sl * (T + 3) + s_ * 512 + j:sl * (T + 3) + s_ * 512 + j + 512],
                                               start=(j == 0), stop=(j == 3))
                            mm.then_inc(peC, 1)

            @b.vector
            def _(ve):
                ve.memset(halo3[:, :], 0.0)
                for si in range(nst):
                    sl = si % 2
                    ve.wait_ge(peB, si + 1)
                    if si >= 2:
                        ve.wait_ge(stB, 16 * (si - 1))
                    ve.tensor_copy(ba_st[0:64, sl * 512:(sl + 1) * 512], pb[6][0:64, 0:512]).then_inc(veB, 1)
                for ct in range(NCT):
                    sl = ct % 2
                    if ct < NCV:
                        if ct == 0:
                            ve.wait_ge(ldc, 16)
                        if ct >= 2:
                            ve.wait_ge(peC, nst * (ct - 1))
                        for j in range(4):
                            ve.tensor_scalar(dstat[:, sl * 512 + j * 128:sl * 512 + (j + 1) * 128],
                                             cs["c_ident_bf"][:, :], cvw[:, ct * 4 + j:ct * 4 + j + 1],
                                             None, op0=ALU.mult)
                        ve.drain()
                        ve.nop().then_inc(veD, 1)
                    for qt in range(nqt):
                        g = nqt * ct + qt
                        ve.wait_ge(peA, g + 1)
                        if ct < NCV:
                            if ct >= 2 and qt == 0:
                                ve.wait_ge(peC, nst * (ct - 1))
                            for s_ in range(spq):
                                si = spq * qt + s_
                                op = ve.tensor_copy(
                                    mixp[:, sl * (T + 3) + 3 + si * 512:sl * (T + 3) + 3 + (si + 1) * 512],
                                    pb[(qt % 2) * 2 + s_][:, 0:512])
                            op.then_inc(veA, 1)
                        else:
                            if ct >= 2 and qt == 0:
                                ve.wait_ge(stQ[sl], 16 * ((ct - 2) // 2 + 1))
                            for s_ in range(spq):
                                si = spq * qt + s_
                                op = ve.tensor_copy(evq[:, sl * T + si * 512:sl * T + (si + 1) * 512],
                                                    pb[(qt % 2) * 2 + s_][:, 0:512])
                            op.then_inc(veA, 1)
                    if ct < NCV:
                        ve.drain()
                        if ct >= 2:
                            ve.wait_ge(stH[ct % 2], 16 * (ct // 2))
                        ve.memset(mixp[:, sl * (T + 3):sl * (T + 3) + 3], 0.0)
                        ve.tensor_copy(own3_sb[:, ct * 4:ct * 4 + 3],
                                       mixp[:, sl * (T + 3) + 3:sl * (T + 3) + 6])
                        ve.tensor_copy(halo3[:, sl * 4:sl * 4 + 3],
                                       mixp[:, sl * (T + 3) + T:sl * (T + 3) + T + 3])
                        ve.drain()
                        ve.nop().then_inc(veM, 1)

            @b.scalar
            def _(sc):
                for ct in range(NCV):
                    sl = ct % 2
                    for s_ in range(nst):
                        ci = ct * nst + s_
                        sc.wait_ge(peC, ci + 1)
                        if s_ == 0 and ct >= 2:
                            sc.wait_ge(stQ[sl], 16 * ((ct - 2) // 2 + 1))
                        sc.activation(evq[:, sl * T + s_ * 512:sl * T + (s_ + 1) * 512],
                                      pb[4 + ci % 4][:, 0:512], AF.Silu).then_inc(seC, 1)

    def phase_gdn_fixup(own3_sb):
        s = P.sem()
        lda, ves, ses, sto = P.sem(), P.sem(), P.sem(), P.sem()
        halB = P.sbuf("fx_hal", [128, 64 * 3], F32)
        w6B = P.sbuf("fx_w6", [128, 64 * 6], F32)
        accB = P.sbuf("fx_acc", [128, 64 * 3], F32)
        tmpB = P.sbuf("fx_tmp", [128, 64 * 3], F32)
        fxB = P.sbuf("fx_fx", [128, 64 * 3], BF16)
        cvw = P.sbuf("cvw2", [128, 256], F32)
        with nc.Block() as b:
            @b.gpsimd
            def _(gp):
                gp.collective_compute("AllGather", ALU.bypass,
                                      ins=[dd["cc2_in"][:, :].opt()],
                                      outs=[dd["cc2_out"][:, :].opt()],
                                      replica_groups=GROUPS).then_inc(s)
                gp.wait_ge(s, 1)

        with nc.Block() as b2:
            @b2.sync
            def _(sync):
                sync.dma_start(cvw[:, :], dd["convw"][:, :]).then_inc(lda, 16)
                sync.dma_start(halB[:, :].rearrange("p (c w) -> p c w", w=3),
                               dd["cc2_out"][0:8192, 0:3].rearrange("(c p) w -> p c w", p=128)) \
                    .then_inc(lda, 16)

            @b2.vector
            def _(ve):
                h3 = halB[:, :].rearrange("p (c w) -> p c w", w=3)
                w6v = w6B[:, :].rearrange("p (c w) -> p c w", w=6)
                av = accB[:, :].rearrange("p (c w) -> p c w", w=3)
                tv = tmpB[:, :].rearrange("p (c w) -> p c w", w=3)
                cv3 = cvw[:, :].rearrange("p (c j) -> p c j", j=4)
                ve.wait_ge(lda, 32)
                ve.tensor_scalar(w6v[:, :, 0:3], h3, cs["parity"][:, 0:1], None, op0=ALU.mult)
                ve.tensor_copy(w6v[:, :, 3:6],
                               own3_sb[:, :].rearrange("p (c w) -> p c w", w=4)[:, :, 0:3])
                ve.drain()
                ve.tensor_tensor(av, w6v[:, :, 0:3],
                                 cv3[:, :, 0:1].broadcast_to([128, 64, 3]), op=ALU.mult)
                for jj in range(1, 4):
                    ve.tensor_tensor(tv, w6v[:, :, jj:jj + 3],
                                     cv3[:, :, jj:jj + 1].broadcast_to([128, 64, 3]), op=ALU.mult)
                    ve.drain()
                    ve.tensor_tensor(av, av, tv, op=ALU.add)
                    ve.drain()
                ve.nop().then_inc(ves, 1)

            @b2.scalar
            def _(sc):
                sc.wait_ge(ves, 1)
                sc.activation(fxB[:, :], accB[:, :], AF.Silu).then_inc(ses, 1)

            @b2.gpsimd
            def _(gp):
                gp.wait_ge(ses, 1)
                f3 = fxB[:, :].rearrange("p (c w) -> p c w", w=3)
                gp.dma_start(dd["sc_qg"][0:2048, 0:3].rearrange("(c p) w -> p c w", p=128),
                             f3[:, 0:16, :]).then_inc(sto, 16)
                gp.dma_start(dd["sc_kg"][0:2048, 0:3].rearrange("(c p) w -> p c w", p=128),
                             f3[:, 16:32, :]).then_inc(sto, 16)
                gp.dma_start(dd["sc_vg"][0:4096, 0:3].rearrange("(c p) w -> p c w", p=128),
                             f3[:, 32:64, :]).then_inc(sto, 16)
                gp.wait_ge(sto, 48)

    def phase_gdn_prep(cm2, egC_sb, egl):
        TC = min(1024, T)
        NTC = T // TC
        NP8 = TC // 128
        CQ = NCG * 16
        lda, pet, vcp, stg, stc = P.sem(), P.sem(), P.sem(), P.sem(), P.sem()
        ba2 = P.sbuf("g3_ba2", [32, 2 * TC], F32)
        cmstage = P.sbuf("g3_cmstage", [128, 5 * 32], F32)
        tmp = P.sbuf("g3_tmp", [32, 8 * TC], F32)
        onec = P.sbuf("g3_onec", [128, 1], F32)
        negc = P.sbuf("g3_negc", [128, 1], F32)
        eac = P.sbuf("g3_eac", [32, 4], F32)

        def tc_(q):
            return tmp[0:32, q * TC:(q + 1) * TC]

        with nc.Block() as b:
            @b.vector
            def _(ve):
                ve.memset(onec[:, :], 1.0)
                ve.memset(negc[:, :], -1.0)
            @b.scalar
            def _(sc):
                sc.activation(eac[:, 0:1], cs["A_log"][0:32, 0:1], AF.Exp)
                sc.drain()
                sc.mul(eac[:, 1:2], eac[:, 0:1], -1.0)

        for i in range(NTC):
            bv = ba2[0:32, 0:TC]
            av = ba2[0:32, TC:2 * TC]
            with nc.Block() as bL:
                @bL.sync
                def _(sync):
                    sync.dma_start(bv, dd["sc_ba"][0:32, i * TC:(i + 1) * TC]).then_inc(lda, 16)
                    sync.dma_start(av, dd["sc_ba"][32:64, i * TC:(i + 1) * TC]).then_inc(lda, 16)
                    sync.wait_ge(lda, 32 * (i + 1))
            with nc.Block() as bS1:
                @bS1.scalar
                def _(sc):
                    sc.activation(tc_(0), bv, AF.Exp, scale=-1.0)
                    sc.activation(tc_(1), bv, AF.Abs)
                    sc.drain()
                    sc.activation(tc_(1), tc_(1), AF.Exp, scale=-1.0)
                    sc.drain()
                    sc.activation(tc_(1), tc_(1), AF.Ln, bias=onec[0:32, 0:1])
            with nc.Block() as bV1:
                @bV1.vector
                def _(ve):
                    ve.tensor_scalar(tc_(0), tc_(0), 1.0, None, op0=ALU.add)
                    ve.drain()
                    ve.reciprocal(tc_(0), tc_(0))
                    ve.tensor_scalar(tc_(3), bv, negc[0:32, 0:1], 0.0, op0=ALU.mult, op1=ALU.max)
                    ve.drain()
                    ve.tensor_tensor(tc_(1), tc_(3), tc_(1), op=ALU.add)
                    ve.drain()
                    ve.tensor_scalar(tc_(1), tc_(1), negc[0:32, 0:1], None, op0=ALU.mult)
                    ve.tensor_scalar(tc_(4), av, cs["dt_bias"][0:32, 0:1], None, op0=ALU.add)
            with nc.Block() as bS2:
                @bS2.scalar
                def _(sc):
                    sc.activation(tc_(2), tc_(4), AF.Abs)
                    sc.drain()
                    sc.activation(tc_(2), tc_(2), AF.Exp, scale=-1.0)
                    sc.drain()
                    sc.activation(tc_(2), tc_(2), AF.Ln, bias=onec[0:32, 0:1])
            with nc.Block() as bV2:
                @bV2.vector
                def _(ve):
                    ve.tensor_scalar(tc_(7), tc_(4), 0.0, None, op0=ALU.max)
                    ve.drain()
                    ve.tensor_tensor(tc_(2), tc_(7), tc_(2), op=ALU.add)
                    ve.drain()
                    ve.tensor_scalar(tc_(2), tc_(2), eac[0:32, 1:2], None, op0=ALU.mult)
                    ve.drain()
                    for n in range(TC // CK):
                        ve.tensor_tensor_scan(tmp[0:32, 3 * TC + n * CK:3 * TC + (n + 1) * CK],
                                              tmp[0:32, 2 * TC + n * CK:2 * TC + (n + 1) * CK],
                                              tmp[0:32, 2 * TC + n * CK:2 * TC + (n + 1) * CK],
                                              0.0, op0=ALU.add, op1=ALU.bypass)
                    ve.drain()
                    ve.tensor_tensor(tc_(4), tc_(3) if False else tmp[0:32, 3 * TC:4 * TC], tc_(1), op=ALU.add)
                    glv = tmp[0:32, 3 * TC:4 * TC].rearrange("p (n c) -> p n c", c=CK)[:, :, CK - 1:CK] \
                        .broadcast_to([32, TC // CK, CK])
                    gv3 = tmp[0:32, 3 * TC:4 * TC].rearrange("p (n c) -> p n c", c=CK)
                    ve.tensor_tensor(tc_(7).rearrange("p (n c) -> p n c", c=CK), glv, gv3, op=ALU.subtract)
            with nc.Block() as bS3:
                @bS3.scalar
                def _(sc):
                    sc.activation(tc_(5), tc_(3), AF.Exp)
                    sc.activation(tc_(7), tc_(7), AF.Exp)
                    sc.activation(egl[0:32, i * NP8 * 2:(i + 1) * NP8 * 2],
                                  tmp[0:32, 3 * TC + CK - 1:4 * TC:CK], AF.Exp)
            with nc.Block() as bV3:
                @bV3.vector
                def _(ve):
                    ve.tensor_tensor(tc_(6), tc_(0), tc_(5), op=ALU.mult)
            with nc.Block() as bC:
                @bC.tensor
                def _(te):
                    for p8 in range(NP8):
                        for qi, q in enumerate([3, 4, 0, 6, 7]):
                            gi = (i * NP8 + p8) * 5 + qi
                            m5, q5 = divmod(gi - 2, 5)
                            if gi >= 2:
                                te.wait_ge(vcp, m5 * 6 + q5 + 1)
                            te.transpose(pb[gi % 2][:, 0:32],
                                         tmp[0:32, q * TC + p8 * 128:q * TC + (p8 + 1) * 128],
                                         cs["c_ident_f32"][0:32, 0:32]).then_inc(pet, 1)

                @bC.vector
                def _(ve):
                    for p8 in range(NP8):
                        m = i * NP8 + p8
                        for qi in range(5):
                            gi = m * 5 + qi
                            ve.wait_ge(pet, gi + 1)
                            if m >= 1 and qi == 0:
                                ve.wait_ge(stc, 16 * m)
                            op = ve.tensor_copy(cmstage[:, qi * 32:(qi + 1) * 32], pb[gi % 2][:, 0:32])
                            op.then_inc(vcp, 1)
                        ve.drain()
                        ve.nop().then_inc(vcp, 1)

                @bC.gpsimd
                def _(gp):
                    for p8 in range(NP8):
                        m = i * NP8 + p8
                        gp.wait_ge(vcp, (m + 1) * 6)
                        gp.dma_start(dd["sc_cm"][:, m, :, :].rearrange("q p c -> p q c"),
                                     cmstage[:, :].rearrange("p (q c) -> p q c", q=5)).then_inc(stc, 16)
                    gp.wait_ge(stc, 16 * (i + 1) * NP8)

            with nc.Block() as bST:
                @bST.sync
                def _(sync):
                    for ri, q in enumerate([3, 4, 5]):
                        sync.dma_start(dd["sc_gba"][32 * ri:32 * (ri + 1), i * TC:(i + 1) * TC],
                                       tc_(q)).then_inc(stg, 16)
                    sync.wait_ge(stg, 48 * (i + 1))

        rbs = P.sem()
        with nc.Block() as bR:
            @bR.sync
            def _(sync):
                for q in range(5):
                    for a in range(2):
                        srcap = dd["sc_cm"][q, :, :, :].rearrange("m (np c) j -> c m np j", np=2)[:, :, :, 16 * a:16 * a + 16]
                        dstap = cm2[a * 64:(a + 1) * 64, q * CQ:(q + 1) * CQ].rearrange(
                            "c (m np j) -> c m np j", np=2, j=16)
                        sync.dma_start(dstap, srcap).then_inc(rbs, 16)
                sync.wait_ge(rbs, 160)

        with nc.Block() as bN:
            @bN.vector
            def _(ve):
                ve.tensor_scalar(cm2[:, 5 * CQ:6 * CQ], cm2[:, 4 * CQ:5 * CQ], negc[:, 0:1],
                                 None, op0=ALU.mult)

    def phase_gdn_l2norm():
        lda = P.sems(2)
        pes, ves, ses = P.sem(), P.sem(), P.sem()
        sto = P.sems(2)
        hb = P.sbuf("l2_hb", [128, 2 * T], BF16)
        sq = P.sbuf("l2_sq", [128, T], BF16)
        rs = P.sbuf("l2_rs", [128, 2 * T], F32)
        epsq = P.sbuf("l2_epsq", [128, 1], F32)
        nst = T // 512
        with nc.Block() as b:
            @b.vector
            def _(ve):
                ve.memset(epsq[:, :], L2_EPS * DK)
        with nc.Block() as b2:
            @b2.sync
            def _(sync):
                for it in range(2 * HK):
                    sl = it % 2
                    dn = "sc_qg" if it < HK else "sc_kg"
                    r0 = (it % HK) * 128
                    if it >= 2:
                        sync.wait_ge(ves, nst * (it - 1))
                        sync.wait_ge(sto[sl], 16 * (it // 2))
                    sync.dma_start(hb[:, sl * T:(sl + 1) * T], dd[dn][r0:r0 + 128, :]).then_inc(lda[sl], 16)

            @b2.scalar
            def _(sc):
                for it in range(2 * HK):
                    sl = it % 2
                    sc.wait_ge(lda[sl], 16 * (it // 2 + 1))
                    if it >= 2:
                        sc.wait_ge(pes, nst * (it - 1))
                    sc.activation(sq[:, :], hb[:, sl * T:(sl + 1) * T], AF.Square)
                    sc.drain()
                    sc.nop().then_inc(ses, 1)
                    for s_ in range(nst):
                        sc.wait_ge(pes, nst * it + s_ + 1)
                        scl = float(DK) if it < HK else 1.0
                        epsap = epsq[:, 0:1] if it < HK else eps_l2[:, 0:1]
                        sc.activation(rs[:, sl * T + s_ * 512:sl * T + (s_ + 1) * 512],
                                      pb[s_ % 2][:, 0:512], AF.Sqrt, scale=scl, bias=epsap).then_inc(ses, 1)

            @b2.tensor
            def _(te):
                for it in range(2 * HK):
                    te.wait_ge(ses, (nst + 1) * it + 1)
                    for s_ in range(nst):
                        if nst * it + s_ >= 2:
                            te.wait_ge(ses, (nst + 1) * it + s_ - 1 + 1 + 1) if False else None
                        if nst * it + s_ >= 2:
                            te.wait_ge(ves, 0) if False else None
                        gi = nst * it + s_
                        if gi >= 2:
                            prev_it, prev_s = divmod(gi - 2, nst)
                            te.wait_ge(ses, (nst + 1) * prev_it + prev_s + 2)
                        te.matmul(pb[s_ % 2][:, 0:512], cs["c_ones_bf"][:, :],
                                  sq[:, s_ * 512:(s_ + 1) * 512], start=True, stop=True).then_inc(pes, 1)

            @b2.vector
            def _(ve):
                for it in range(2 * HK):
                    sl = it % 2
                    for s_ in range(nst):
                        ve.wait_ge(ses, (nst + 1) * it + s_ + 2)
                        ve.reciprocal(rs[:, sl * T + s_ * 512:sl * T + (s_ + 1) * 512],
                                      rs[:, sl * T + s_ * 512:sl * T + (s_ + 1) * 512])
                        ve.drain()
                        if it >= 2 and s_ == 0:
                            ve.wait_ge(sto[sl], 16 * (it // 2))
                        ve.tensor_tensor(hb[:, sl * T + s_ * 512:sl * T + (s_ + 1) * 512],
                                         hb[:, sl * T + s_ * 512:sl * T + (s_ + 1) * 512],
                                         rs[:, sl * T + s_ * 512:sl * T + (s_ + 1) * 512],
                                         op=ALU.mult)
                        ve.drain()
                    ve.nop().then_inc(ves, nst)

            @b2.gpsimd
            def _(gp):
                for it in range(2 * HK):
                    sl = it % 2
                    dn = "sc_qg" if it < HK else "sc_kg"
                    r0 = (it % HK) * 128
                    gp.wait_ge(ves, nst * (it + 1))
                    gp.dma_start(dd[dn][r0:r0 + 128, :], hb[:, sl * T:(sl + 1) * T]).then_inc(sto[sl], 16)
                gp.wait_ge(sto[0], 16 * HK)
                gp.wait_ge(sto[1], 16 * HK)

    def phase_gdn_egc(egC_sb, egl):
        pes, ves = P.sem(), P.sem()
        with nc.Block() as b:
            @b.tensor
            def _(te):
                for vh in range(HV):
                    if vh >= 2:
                        te.wait_ge(ves, vh - 1)
                    te.matmul(pb[vh % 2][:, 0:NCG], cs["c_sel1"][:, vh * 128:(vh + 1) * 128],
                              egl[0:32, 0:NCG], start=True, stop=True).then_inc(pes, 1)

            @b.vector
            def _(ve):
                for vh in range(HV):
                    ve.wait_ge(pes, vh + 1)
                    ve.tensor_copy(egC_sb[:, vh * NCG:(vh + 1) * NCG], pb[vh % 2][:, 0:NCG]).then_inc(ves, 1)

    def phase_gdn_core(cm2, egC_sb, gba_sb):
        CQ = NCG * 16
        NG8 = NCG // 8
        NU = HK * NG8
        lda, ldg = P.sem(), P.sem()
        peG, peQ, veM, seE = P.sem(), P.sem(), P.sem(), P.sem()
        peT, seT, poB, poKg = P.sem(), P.sem(), P.sem(), P.sem()
        peS, veX, seX, stx = P.sem(), P.sem(), P.sem(), P.sem()
        qn = P.sbuf("g4_qn", [128, T], BF16)
        kn = P.sbuf("g4_kn", [128, T], BF16)
        kng = P.sbuf("g4_kng", [128, T], BF16)
        vga = P.sbuf("g4_vga", [128, T], BF16)
        vgb = P.sbuf("g4_vgb", [128, T], BF16)
        gslc2 = P.sbuf("g4_gslc", [32, 2 * 1536], F32)
        d2 = P.sbuf("g4_d2", [128, 512], F32)
        tq = P.sbuf("g4_tq", [128, 512], F32)
        eE2 = P.sbuf("g4_eE", [128, 2 * 1536], BF16)
        ATs2 = P.sbuf("g4_AT", [128, 2 * 512], BF16)
        NTs2 = P.sbuf("g4_NT", [128, 2 * 512], BF16)
        Ns2 = P.sbuf("g4_N", [128, 2 * 512], BF16)
        qds2 = P.sbuf("g4_qds", [128, 2 * 1024], BF16)
        kTvT = P.sbuf("g4_kTvT", [128, 2 * 2048], BF16)
        kd_all = P.sbuf("g4_kd", [128, 1024], BF16)
        kdn_all = P.sbuf("g4_kdn", [128, 1024], BF16)
        X_all = P.sbuf("g4_X", [128, 2 * 2048], BF16)
        Pn2 = P.sbuf("g4_Pn", [128, 2 * 512], BF16)
        Pt_all = P.sbuf("g4_Pt", [128, 5 * 512], BF16)
        pT6 = pb[6][:, :].bitcast(BF16)
        pT7 = pb[7][:, :].bitcast(BF16)

        def uinfo(u):
            return u // NG8, u % NG8, u % 2

        SCHED = [("Pn", 1), ("Pt", 1), ("A0", 0), ("A1", 0),
                 ("Pn", 2), ("Pt", 2), ("A0", 1), ("A1", 1),
                 ("Pn", 3), ("Pt", 3), ("A0", 2), ("A1", 2),
                 ("Pn", 4), ("Pt", 4), ("A0", 3), ("A1", 3),
                 ("Pt", 5), ("A0", 4), ("A1", 4), ("A0", 5), ("A1", 5)]
        PE_W = [[("sx", 1, 7)], [("sx", 1, 9)], [("vx", 1, 12)], [("vx", 0, 1)],
                [("sx", 0, 2)], [("sx", 0, 2)], [("sx", 0, 2), ("vx", 0, 2)],
                [("vx", 0, 3)], [("sx", 0, 4)], [("sx", 0, 4)], [("vx", 0, 4)],
                [("vx", 0, 5)], [("sx", 0, 6)], [("sx", 0, 6)], [("vx", 0, 6)],
                [("vx", 0, 7)], [("sx", 0, 8)], [("vx", 0, 8)], [("vx", 0, 9)],
                [("sx", 0, 9), ("vx", 0, 10)], [("vx", 0, 11)]]
        DVE_DRAIN = {6, 10, 14, 17, 19}

        def xslots(k):
            return (0, 2048) if k % 2 == 0 else (2048, 0)

        with nc.Block() as b:
            @b.sync
            def _(sync):
                for u in range(NU):
                    j, gg, s = uinfo(u)
                    if gg == 0:
                        if j >= 1:
                            sync.wait_ge(peT, 2 * u)
                            sync.wait_ge(peG, 4 * u)
                            sync.wait_ge(veM, 8 * u)
                        sync.dma_start(qn[:, :], dd["sc_qg"][j * 128:(j + 1) * 128, :]).then_inc(lda, 16)
                        sync.dma_start(kn[:, :], dd["sc_kg"][j * 128:(j + 1) * 128, :]).then_inc(lda, 16)
                        sync.dma_start(vga[:, :], dd["sc_vg"][j * 128:(j + 1) * 128, :]).then_inc(lda, 16)
                        sync.dma_start(vgb[:, :], dd["sc_vg"][(j + 16) * 128:(j + 17) * 128, :]).then_inc(lda, 16)
                    if u >= 2:
                        sync.wait_ge(peG, 4 * (u - 1))
                        sync.wait_ge(peQ, 2 * (u - 1))
                    c0 = gg * 512
                    for ri in range(3):
                        sync.dma_start(gslc2[0:32, s * 1536 + ri * 512:s * 1536 + (ri + 1) * 512],
                                       dd["sc_gba"][32 * ri:32 * ri + 32, c0:c0 + 512]).then_inc(ldg, 16)

            @b.tensor
            def _(te):
                for it in range(NU + 1):
                    if it < NU:
                        u = it
                        j, gg, s = uinfo(u)
                        c0 = gg * 512
                        te.wait_ge(ldg, 48 * (u + 1))
                        if gg == 0:
                            te.wait_ge(lda, 64 * (j + 1))
                            te.wait_ge(poKg, j + 1)
                        if u >= 1:
                            te.wait_ge(veM, 8 * u)
                        gsl = gslc2[0:32, s * 1536:s * 1536 + 1536]
                        te.matmul(pb[2][:, 0:512], cs["c_sel2"][:, j * 128:(j + 1) * 128],
                                  gsl[:, 0:512], start=True, stop=True).then_inc(peG, 1)
                        te.matmul(pb[3][:, 0:512], cs["c_sel2"][:, j * 128:(j + 1) * 128],
                                  gsl[:, 512:1024], start=True, stop=True).then_inc(peG, 1)
                        for n8 in range(8):
                            cc = c0 + n8 * 64
                            for tp in range(2):
                                mm = te.matmul(pb[0][tp * 64:(tp + 1) * 64, n8 * 64:(n8 + 1) * 64],
                                               kn[:, cc:cc + 64], kng[:, cc:cc + 64],
                                               start=True, stop=True, tile_position=(0, tp * 64))
                        mm.then_inc(peG, 1)
                        for n8 in range(8):
                            cc = c0 + n8 * 64
                            for tp in range(2):
                                mm = te.matmul(pb[1][tp * 64:(tp + 1) * 64, n8 * 64:(n8 + 1) * 64],
                                               qn[:, cc:cc + 64], qn[:, cc:cc + 64],
                                               start=True, stop=True, tile_position=(0, tp * 64))
                        mm.then_inc(peG, 1)
                    if it >= 1:
                        u = it - 1
                        j, gg, s = uinfo(u)
                        for gidx, (kind, lvl) in enumerate(SCHED):
                            if gidx == 0:
                                te.wait_ge(veM, 8 * u + 6)
                            for (sk, back, off) in PE_W[gidx]:
                                rate = 9 if sk == "sx" else 12
                                tgt = rate * (u - back) + off
                                if tgt > 0:
                                    te.wait_ge(seX if sk == "sx" else veX, tgt)
                            if kind == "Pn" or kind == "Pt":
                                dstb = pb[4] if kind == "Pn" else pb[5]
                                for n8 in range(8):
                                    for a in range(2):
                                        if lvl == 1:
                                            lT = NTs2[a * 64:(a + 1) * 64, s * 512 + n8 * 64:s * 512 + (n8 + 1) * 64]
                                            nN = Ns2[a * 64:(a + 1) * 64, s * 512 + n8 * 64:s * 512 + (n8 + 1) * 64]
                                        else:
                                            lT = Pt_all[a * 64:(a + 1) * 64,
                                                        (lvl - 2) * 512 + n8 * 64:(lvl - 2) * 512 + (n8 + 1) * 64]
                                            ps = (lvl - 2) % 2
                                            nN = Pn2[a * 64:(a + 1) * 64, ps * 512 + n8 * 64:ps * 512 + (n8 + 1) * 64]
                                        l_, r_ = (lT, nN) if kind == "Pn" else (nN, lT)
                                        mm = te.matmul(dstb[a * 64:(a + 1) * 64, n8 * 64:(n8 + 1) * 64],
                                                       l_, r_, start=True, stop=True,
                                                       tile_position=(a * 64, a * 64))
                                mm.then_inc(peS, 1)
                            else:
                                k = lvl
                                wv = 0 if kind == "A0" else 1
                                if gidx == 2:
                                    te.wait_ge(poB, 2 * u + 2)
                                    te.wait_ge(seT, 2 * u + 2)
                                srcb, _dst = xslots(k)
                                for n8 in range(4 * wv, 4 * wv + 4):
                                    bank = pb[6] if (n8 % 4) < 2 else pb[7]
                                    colb = (n8 % 2) * 256
                                    for a in range(2):
                                        if k == 0:
                                            lT = NTs2[a * 64:(a + 1) * 64, s * 512 + n8 * 64:s * 512 + (n8 + 1) * 64]
                                        else:
                                            lT = Pt_all[a * 64:(a + 1) * 64,
                                                        (k - 1) * 512 + n8 * 64:(k - 1) * 512 + (n8 + 1) * 64]
                                        mm = te.matmul(bank[a * 64:(a + 1) * 64, colb:colb + 256],
                                                       lT,
                                                       X_all[a * 64:(a + 1) * 64, srcb + n8 * 256:srcb + (n8 + 1) * 256],
                                                       start=True, stop=True,
                                                       tile_position=(a * 64, a * 64))
                                mm.then_inc(peS, 1)
                    if it < NU:
                        u = it
                        j, gg, s = uinfo(u)
                        gsl = gslc2[0:32, s * 1536:s * 1536 + 1536]
                        te.wait_ge(veM, 8 * u + 3)
                        te.matmul(pb[3][:, 0:512], cs["c_sel1"][:, j * 128:(j + 1) * 128],
                                  gsl[:, 1024:1536], start=True, stop=True).then_inc(peQ, 1)
                        te.wait_ge(veM, 8 * u + 5)
                        te.matmul(pb[2][:, 0:512], cs["c_sel1"][:, (j + 16) * 128:(j + 17) * 128],
                                  gsl[:, 1024:1536], start=True, stop=True).then_inc(peQ, 1)
                    if it < NU:
                        u = it
                        j, gg, s = uinfo(u)
                        c0 = gg * 512
                        if u >= 1:
                            te.wait_ge(veX, 12 * (u - 1) + 12)
                        for n8 in range(8):
                            cc = c0 + n8 * 64
                            pbT = pT6 if n8 < 4 else pT7
                            cb = (n8 % 4) * 256
                            te.transpose(pbT[0:64, cb:cb + 128], kn[:, cc:cc + 64],
                                         cs["c_ident_bf"][:, :], tile_position=(0, 0))
                            te.transpose(pbT[64:128, cb:cb + 128], kn[:, cc:cc + 64],
                                         cs["c_ident_bf"][:, :], tile_position=(0, 64))
                            te.transpose(pbT[0:64, cb + 128:cb + 256], vga[:, cc:cc + 64],
                                         cs["c_ident_bf"][:, :], tile_position=(0, 0))
                            tt = te.transpose(pbT[64:128, cb + 128:cb + 256], vgb[:, cc:cc + 64],
                                              cs["c_ident_bf"][:, :], tile_position=(0, 64))
                            if n8 == 3 or n8 == 7:
                                tt.then_inc(peT, 1)

            @b.vector
            def _(ve):
                for it in range(NU + 1):
                    if it < NU:
                        u = it
                        j, gg, s = uinfo(u)
                        c0 = gg * 512
                        g3d = cm2[:, 0 * CQ + (gg * 8) * 16 + j:0 * CQ + (gg * 8 + 8) * 16 + j:16] \
                            .unsqueeze(2).broadcast_to([128, 8, 64])
                        glb3d = cm2[:, 1 * CQ + (gg * 8) * 16 + j:1 * CQ + (gg * 8 + 8) * 16 + j:16] \
                            .unsqueeze(2).broadcast_to([128, 8, 64])
                        d2v = d2[:, :].rearrange("p (n s) -> p n s", n=8)
                        eEs = eE2[:, s * 1536:(s + 1) * 1536]
                        ve.wait_ge(peG, 4 * u + 1)
                        if u >= 1:
                            ve.wait_ge(seE, 3 * u)
                        ve.tensor_tensor(d2v, pb[2][:, 0:512].rearrange("p (n s) -> p n s", n=8),
                                         g3d, op=ALU.subtract)
                        ve.drain()
                        ve.tensor_tensor(tq[:, :], d2[:, :], cs["c_ubias_i"][:, :], op=ALU.add).then_inc(veM, 1)
                        ve.wait_ge(seE, 3 * u + 1)
                        ve.wait_ge(peG, 4 * u + 4)
                        if u >= 2:
                            ve.wait_ge(stx, 96 * (u - 1))
                        ve.tensor_tensor(ATs2[:, s * 512:(s + 1) * 512], eEs[:, 0:512],
                                         pb[1][:, 0:512], op=ALU.mult).then_inc(veM, 1)
                        ve.tensor_tensor(d2v, pb[3][:, 0:512].rearrange("p (n s) -> p n s", n=8),
                                         g3d, op=ALU.subtract)
                        ve.drain()
                        ve.tensor_tensor(tq[:, :], d2[:, :], cs["c_ubias_s"][:, :], op=ALU.add).then_inc(veM, 1)
                        ve.wait_ge(seE, 3 * u + 2)
                        if u >= 1:
                            ve.wait_ge(peS, 21 * (u - 1))
                        ve.tensor_tensor(NTs2[:, s * 512:(s + 1) * 512], eEs[:, 512:1024],
                                         pb[0][:, 0:512], op=ALU.mult).then_inc(veM, 1)
                        ve.tensor_tensor(d2v, glb3d, pb[2][:, 0:512].rearrange("p (n s) -> p n s", n=8),
                                         op=ALU.subtract)
                        ve.drain()
                        ve.tensor_tensor(tq[:, :], d2[:, :], cs["c_lbias_s"][:, :], op=ALU.add).then_inc(veM, 1)
                        ve.wait_ge(seE, 3 * u + 3)
                        ve.tensor_tensor(Ns2[:, s * 512:(s + 1) * 512], eEs[:, 1024:1536],
                                         pb[0][:, 0:512], op=ALU.mult).then_inc(veM, 1)
                    if it >= 1:
                        u = it - 1
                        j, gg, s = uinfo(u)
                        for gidx, (kind, lvl) in enumerate(SCHED):
                            if kind in ("Pn", "Pt"):
                                continue
                            ve.wait_ge(peS, 21 * u + gidx + 1)
                            k = lvl
                            wv = 0 if kind == "A0" else 1
                            if gidx == 2:
                                ve.wait_ge(poB, 2 * u + 2)
                            if gidx in DVE_DRAIN:
                                ve.drain()
                            srcb, dstb = xslots(k)
                            cA = 4 * wv * 256
                            ve.tensor_tensor(X_all[:, dstb + cA:dstb + cA + 512],
                                             pb[6][:, 0:512],
                                             X_all[:, srcb + cA:srcb + cA + 512], op=ALU.add)
                            ve.tensor_tensor(X_all[:, dstb + cA + 512:dstb + cA + 1024],
                                             pb[7][:, 0:512],
                                             X_all[:, srcb + cA + 512:srcb + cA + 1024],
                                             op=ALU.add).then_inc(veX, 1)
                    if it < NU:
                        u = it
                        j, gg, s = uinfo(u)
                        c0 = gg * 512
                        ve.wait_ge(peQ, 2 * u + 1)
                        ve.tensor_tensor(qds2[:, s * 1024:s * 1024 + 512], qn[:, c0:c0 + 512],
                                         pb[3][:, 0:512], op=ALU.mult).then_inc(veM, 1)
                        ve.wait_ge(peQ, 2 * u + 2)
                        ve.tensor_tensor(qds2[:, s * 1024 + 512:s * 1024 + 1024], qn[:, c0:c0 + 512],
                                         pb[2][:, 0:512], op=ALU.mult).then_inc(veM, 1)
                    if it < NU:
                        u = it
                        j, gg, s = uinfo(u)
                        ve.wait_ge(seT, 2 * u + 2)
                        if u >= 1:
                            ve.wait_ge(stx, 96 * u)
                        for n8 in range(8):
                            col = (gg * 8 + n8) * 16 + j
                            kv = kTvT[:, s * 2048 + n8 * 256:s * 2048 + n8 * 256 + 128]
                            vv = kTvT[:, s * 2048 + n8 * 256 + 128:s * 2048 + (n8 + 1) * 256]
                            ve.tensor_scalar(kd_all[:, n8 * 128:(n8 + 1) * 128], kv,
                                             cm2[:, 4 * CQ + col:4 * CQ + col + 1], None, op0=ALU.mult)
                            op = ve.tensor_scalar(kdn_all[:, n8 * 128:(n8 + 1) * 128], kv,
                                                  cm2[:, 5 * CQ + col:5 * CQ + col + 1], None, op0=ALU.mult)
                            if n8 == 7:
                                op.then_inc(poB, 1)
                        for n8 in range(8):
                            col = (gg * 8 + n8) * 16 + j
                            kv = kTvT[:, s * 2048 + n8 * 256:s * 2048 + n8 * 256 + 128]
                            vv = kTvT[:, s * 2048 + n8 * 256 + 128:s * 2048 + (n8 + 1) * 256]
                            ve.tensor_scalar(X_all[:, n8 * 256:n8 * 256 + 128], vv,
                                             cm2[:, 2 * CQ + col:2 * CQ + col + 1], None, op0=ALU.mult)
                            op = ve.tensor_scalar(X_all[:, n8 * 256 + 128:(n8 + 1) * 256], kv,
                                                  cm2[:, 3 * CQ + col:3 * CQ + col + 1], None, op0=ALU.mult)
                            if n8 == 7:
                                op.then_inc(poB, 1)

            @b.scalar
            def _(sc):
                for u in range(NU):
                    j, gg, s = uinfo(u)
                    eEs = eE2[:, s * 1536:(s + 1) * 1536]
                    for e in range(3):
                        sc.wait_ge(veM, 8 * u + 2 * e + 1)
                        sc.activation(eEs[:, e * 512:(e + 1) * 512], tq[:, :], AF.Exp).then_inc(seE, 1)
                    if u >= 1:
                        up = u - 1
                        for gidx, (kind, lvl) in enumerate(SCHED):
                            if kind == "Pn":
                                sc.wait_ge(peS, 21 * up + gidx + 1)
                                ps = (lvl - 1) % 2
                                sc.copy(Pn2[:, ps * 512:(ps + 1) * 512],
                                        pb[4][:, 0:512]).then_inc(seX, 1)
                            elif kind == "Pt":
                                sc.wait_ge(peS, 21 * up + gidx + 1)
                                sc.copy(Pt_all[:, (lvl - 1) * 512:lvl * 512],
                                        pb[5][:, 0:512]).then_inc(seX, 1)
                    sc.wait_ge(peT, 2 * u + 1)
                    if u >= 2:
                        sc.wait_ge(poB, 2 * (u - 1))
                    sc.copy(kTvT[:, s * 2048:s * 2048 + 1024], pT6[:, 0:1024]).then_inc(seT, 1)
                    sc.wait_ge(peT, 2 * u + 2)
                    sc.copy(kTvT[:, s * 2048 + 1024:s * 2048 + 2048], pT7[:, 0:1024]).then_inc(seT, 1)
                up = NU - 1
                for gidx, (kind, lvl) in enumerate(SCHED):
                    if kind == "Pn":
                        sc.wait_ge(peS, 21 * up + gidx + 1)
                        ps = (lvl - 1) % 2
                        sc.copy(Pn2[:, ps * 512:(ps + 1) * 512],
                                pb[4][:, 0:512]).then_inc(seX, 1)
                    elif kind == "Pt":
                        sc.wait_ge(peS, 21 * up + gidx + 1)
                        sc.copy(Pt_all[:, (lvl - 1) * 512:lvl * 512],
                                pb[5][:, 0:512]).then_inc(seX, 1)

            @b.gpsimd
            def _(gp):
                for it in range(NU + 1):
                    if it < NU and it % NG8 == 0:
                        j = it // NG8
                        gp.wait_ge(lda, 64 * (j + 1))
                        if j >= 1:
                            gp.wait_ge(peG, 4 * it)
                        gp.tensor_scalar(kng[:, :], kn[:, :], -1.0, None, op0=ALU.mult).then_inc(poKg, 1)
                    if it >= 1:
                        u = it - 1
                        j, gg, s = uinfo(u)
                        gp.wait_ge(veX, 12 * u + 12)
                        gp.wait_ge(veM, 8 * u + 8)
                        r0 = gg * 128 + j
                        gp.dma_start(dd["sc_kd2"][r0:r0 + 113:16, :, :].rearrange("n p c -> p n c"),
                                     kd_all[:, :].rearrange("p (n c) -> p n c", c=128)).then_inc(stx, 16)
                        gp.dma_start(dd["sc_kdn"][r0:r0 + 113:16, :, :].rearrange("n p c -> p n c"),
                                     kdn_all[:, :].rearrange("p (n c) -> p n c", c=128)).then_inc(stx, 16)
                        gp.dma_start(dd["sc_X"][r0:r0 + 113:16, :, :].rearrange("n p c -> p n c"),
                                     X_all[:, 0:2048].rearrange("p (n c) -> p n c", c=256)).then_inc(stx, 16)
                        gp.dma_start(dd["sc_AT"][gg * 16 + j, :, :],
                                     ATs2[:, s * 512:(s + 1) * 512]).then_inc(stx, 16)
                        gp.dma_start(dd["sc_qdT"][j * NG8 + gg, :, :],
                                     qds2[:, s * 1024:s * 1024 + 512]).then_inc(stx, 16)
                        gp.dma_start(dd["sc_qdT"][(j + 16) * NG8 + gg, :, :],
                                     qds2[:, s * 1024 + 512:s * 1024 + 1024]).then_inc(stx, 16)
                gp.wait_ge(stx, 96 * NU)

    def phase_gdn_scan(S32, Sbf, egC_sb, pass2):
        NG8 = NCG // 8
        NSL = T // 256
        L2 = 2
        lda = [P.sems(2) for _ in range(L2)]
        ldz = P.sems(L2)
        ldS = P.sems(L2)
        peW = P.sems(L2)
        veW = P.sems(L2)
        peP = P.sems(L2)
        veS = P.sems(L2)
        seS = P.sems(L2)
        peA = P.sems(L2)
        veQm = P.sems(L2)
        peO = P.sems(L2)
        veO = P.sems(L2)
        seZ = P.sems(L2)
        veN = P.sems(L2)
        sqd = P.sems(L2)
        peR = P.sems(L2)
        seR = P.sems(L2)
        veF = P.sems(L2)
        veR8 = P.sems(L2)
        stS = P.sems(L2)
        stG = P.sems(L2)
        Xs = [P.sbuf(f"g5_X{L}", [128, 2 * 256], BF16) for L in range(L2)]
        kds = [P.sbuf(f"g5_kd{L}", [128, 2 * 128], BF16) for L in range(L2)]
        kdns = [P.sbuf(f"g5_kdn{L}", [128, 2 * 128], BF16) for L in range(L2)]
        wmt = [P.sbuf(f"g5_wmt{L}", [128, 2 * 128], BF16) for L in range(L2)]
        Sini = [P.sbuf(f"g5_Sini{L}", [128, 128], F32) for L in range(L2)]
        if pass2:
            ATs = [P.sbuf(f"g5_ATs{L}", [128, 2 * 512], BF16) for L in range(L2)]
            qdTs = [P.sbuf(f"g5_qdT{L}", [128, 4 * 512], BF16) for L in range(L2)]
            qm = [P.sbuf(f"g5_qm{L}", [128, 2 * 64], BF16) for L in range(L2)]
            zsl = [P.sbuf(f"g5_z{L}", [128, 2 * T], BF16) for L in range(L2)]
            xg = [P.sbuf(f"g5_xg{L}", [128, 2 * T], BF16) for L in range(L2)]
            rs5 = [P.sbuf(f"g5_rs{L}", [128, 2 * 256], F32) for L in range(L2)]
            sqz = [P.sbuf(f"g5_sq{L}", [128, T], BF16) for L in range(L2)]

        def ld_counts():
            cnt = [0, 0]
            table = []
            for n in range(NCG):
                sl = n % 2
                cnt[sl] += 3
                if pass2 and n % 8 == 0:
                    cnt[sl] += 3
                table.append((sl, cnt[sl]))
            return table, cnt
        LDT, LDJ = ld_counts()

        with nc.Block() as bP:
            @bP.sync
            def _(sync):
                for jp in range(8):
                    for n in range(NCG):
                        for L in range(L2):
                            j = 2 * jp + L
                            sl, cn = LDT[n]
                            if pass2 and n == 0:
                                for a in range(2):
                                    vh = j + 16 * a
                                    if 2 * jp + a > 0:
                                        sync.wait_ge(veN[L], 2 * jp + a)
                                    sync.dma_start(Sini[L][:, :],
                                                   dd["cc3_out"][vh * 128:(vh + 1) * 128, :]).then_inc(ldS[L], 16)
                            if n >= 2 or jp > 0:
                                tgt = max(0, 2 * NCG * jp + 2 * (n - 1))
                                sync.wait_ge(veS[L], tgt)
                                sync.wait_ge(peW[L], tgt)
                                sync.wait_ge(peP[L], tgt)
                                if pass2:
                                    sync.wait_ge(peA[L], tgt)
                                    sync.wait_ge(peO[L], tgt)
                                    sync.wait_ge(veQm[L], tgt)
                            sync.dma_start(Xs[L][:, sl * 256:(sl + 1) * 256],
                                           dd["sc_X"][n * 16 + j, :, :]).then_inc(lda[L][sl], 16)
                            sync.dma_start(kds[L][:, sl * 128:(sl + 1) * 128],
                                           dd["sc_kd2"][n * 16 + j, :, :]).then_inc(lda[L][sl], 16)
                            sync.dma_start(kdns[L][:, sl * 128:(sl + 1) * 128],
                                           dd["sc_kdn"][n * 16 + j, :, :]).then_inc(lda[L][sl], 16)
                            if pass2 and n % 8 == 0:
                                sync.dma_start(ATs[L][:, (n // 8 % 2) * 512:(n // 8 % 2 + 1) * 512],
                                               dd["sc_AT"][(n // 8) * 16 + j, :, :]).then_inc(lda[L][sl], 16)
                                for a in range(2):
                                    vh = j + 16 * a
                                    sync.dma_start(qdTs[L][:, (2 * (n // 8 % 2) + a) * 512:
                                                        (2 * (n // 8 % 2) + a + 1) * 512],
                                                   dd["sc_qdT"][vh * NG8 + n // 8, :, :]).then_inc(lda[L][sl], 16)

            if pass2:
                @bP.scalar
                def _(sc):
                    for jp in range(8):
                        for L in range(L2):
                            j = 2 * jp + L
                            for a in range(2):
                                vh = j + 16 * a
                                if jp > 0:
                                    sc.wait_ge(veF[L], 2 * (jp - 1) + a + 1)
                                sc.dma_start(zsl[L][:, a * T:(a + 1) * T],
                                             dd["sc_z"][vh * 128:(vh + 1) * 128, :]).then_inc(ldz[L], 16)
                            sc.wait_ge(ldz[L], 32 * (jp + 1))
                            for a in range(2):
                                sc.activation(zsl[L][:, a * T:(a + 1) * T], zsl[L][:, a * T:(a + 1) * T],
                                              AF.Silu).then_inc(seZ[L], 1)
                        for n in range(NCG):
                            for L in range(L2):
                                for a in range(2):
                                    i2 = (jp * NCG + n) * 2 + a
                                    sc.wait_ge(peW[L], i2 + 1)
                                    sc.copy(wmt[L][:, a * 128:(a + 1) * 128],
                                            pb[a][:, L * 128:(L + 1) * 128]).then_inc(veW[L], 1)
                            for L in range(L2):
                                for a in range(2):
                                    i2 = (jp * NCG + n) * 2 + a
                                    sc.wait_ge(veS[L], i2 + 1)
                                    sc.copy(Sbf[:, (2 * jp + L + 16 * a) * 128:(2 * jp + L + 16 * a + 1) * 128],
                                            S32[:, (2 * jp + L + 16 * a) * 128:(2 * jp + L + 16 * a + 1) * 128]) \
                                        .then_inc(seS[L], 1)
                        for L in range(L2):
                            j = 2 * jp + L
                            for a in range(2):
                                vh = j + 16 * a
                                sc.wait_ge(veO[L], 2 * NCG * jp + 2 * (NCG - 1) + a + 1)
                                if 2 * jp + a > 0:
                                    sc.wait_ge(peR[L], NSL * (2 * jp + a))
                                sc.activation(sqz[L][:, :], xg[L][:, a * T:(a + 1) * T], AF.Square)
                                sc.drain()
                                sc.nop().then_inc(sqd[L], 1)
                                for s_ in range(NSL):
                                    si8 = (2 * jp + a) * NSL + s_
                                    sc.wait_ge(peR[L], si8 + 1)
                                    if si8 >= 2:
                                        sc.wait_ge(veR8[L], si8 - 1)
                                    sc.activation(rs5[L][:, (si8 % 2) * 256:(si8 % 2) * 256 + 256],
                                                  pb[2 * (si8 % 2) + L][:, 256:512],
                                                  AF.Ln, scale=1.0 / 128.0,
                                                  bias=eps_rms[:, 0:1])
                                    sc.drain()
                                    sc.activation(rs5[L][:, (si8 % 2) * 256:(si8 % 2) * 256 + 256],
                                                  rs5[L][:, (si8 % 2) * 256:(si8 % 2) * 256 + 256],
                                                  AF.Exp, scale=-0.5).then_inc(seR[L], 1)
            else:
                @bP.scalar
                def _(sc):
                    for jp in range(8):
                        for n in range(NCG):
                            for L in range(L2):
                                for a in range(2):
                                    i2 = (jp * NCG + n) * 2 + a
                                    sc.wait_ge(peW[L], i2 + 1)
                                    sc.copy(wmt[L][:, a * 128:(a + 1) * 128],
                                            pb[a][:, L * 128:(L + 1) * 128]).then_inc(veW[L], 1)
                            for L in range(L2):
                                for a in range(2):
                                    i2 = (jp * NCG + n) * 2 + a
                                    sc.wait_ge(veS[L], i2 + 1)
                                    sc.copy(Sbf[:, (2 * jp + L + 16 * a) * 128:(2 * jp + L + 16 * a + 1) * 128],
                                            S32[:, (2 * jp + L + 16 * a) * 128:(2 * jp + L + 16 * a + 1) * 128]) \
                                        .then_inc(seS[L], 1)

            @bP.vector
            def _(ve):
                for jp in range(8):
                    for L in range(L2):
                        j = 2 * jp + L
                        for a in range(2):
                            vh = j + 16 * a
                            if pass2:
                                ve.wait_ge(ldS[L], 16 * (2 * jp + a + 1))
                                ve.tensor_scalar(S32[:, vh * 128:(vh + 1) * 128], Sini[L][:, :],
                                                 cs["parity"][:, 0:1], None, op0=ALU.mult)
                            else:
                                ve.memset(S32[:, vh * 128:(vh + 1) * 128], 0.0)
                            ve.drain()
                            ve.tensor_copy(Sbf[:, vh * 128:(vh + 1) * 128],
                                           S32[:, vh * 128:(vh + 1) * 128]).then_inc(veN[L], 1)
                    for n in range(NCG):
                        for L in range(L2):
                            j = 2 * jp + L
                            sl = n % 2
                            for a in range(2):
                                vh = j + 16 * a
                                i2 = (jp * NCG + n) * 2 + a
                                if pass2:
                                    ve.wait_ge(peA[L], i2 + 1)
                                    ve.tensor_tensor(qm[L][:, a * 64:(a + 1) * 64],
                                                     qdTs[L][:, (2 * (n // 8 % 2) + a) * 512 + (n % 8) * 64:
                                                          (2 * (n // 8 % 2) + a) * 512 + (n % 8 + 1) * 64],
                                                     pb[4 + a][:, L * 64:(L + 1) * 64],
                                                     op=ALU.subtract).then_inc(veQm[L], 1)
                                ve.wait_ge(peP[L], i2 + 1)
                                if pass2:
                                    ve.wait_ge(peO[L], i2 + 1)
                                ve.drain()
                                ve.scalar_tensor_tensor(S32[:, vh * 128:(vh + 1) * 128],
                                                        S32[:, vh * 128:(vh + 1) * 128],
                                                        egC_sb[:, vh * NCG + n:vh * NCG + n + 1],
                                                        pb[2 + a][:, L * 128:(L + 1) * 128],
                                                        op0=ALU.mult, op1=ALU.subtract).then_inc(veS[L], 1)
                                if pass2:
                                    ve.wait_ge(seZ[L], 2 * jp + a + 1)
                                    if n == 0 and a == 0 and jp > 0:
                                        ve.wait_ge(stG[L], 32 * jp)
                                    ve.tensor_tensor(xg[L][:, a * T + n * 64:a * T + (n + 1) * 64],
                                                     pb[6 + a][:, L * 64:(L + 1) * 64],
                                                     zsl[L][:, a * T + n * 64:a * T + (n + 1) * 64],
                                                     op=ALU.mult).then_inc(veO[L], 1)
                    if pass2:
                        for L in range(L2):
                            for a in range(2):
                                base = (2 * jp + a) * NSL
                                for s_ in range(NSL):
                                    si8 = base + s_
                                    sslot = (si8 % 2) * 256
                                    ve.wait_ge(seR[L], si8 + 1)
                                    ve.drain()
                                    ve.scalar_tensor_tensor(xg[L][:, a * T + s_ * 256:a * T + (s_ + 1) * 256],
                                                            xg[L][:, a * T + s_ * 256:a * T + (s_ + 1) * 256],
                                                            cs["normw"][:, 0:1],
                                                            rs5[L][:, sslot:sslot + 256],
                                                            op0=ALU.mult, op1=ALU.mult)
                                    ve.drain()
                                    ve.nop().then_inc(veR8[L], 1)
                                ve.nop().then_inc(veF[L], 1)

            @bP.tensor
            def _(te):
                for jp in range(8):
                    for n in range(NCG):
                        for L in range(L2):
                            j = 2 * jp + L
                            sl, cn = LDT[n]
                            for a in range(2):
                                vh = j + 16 * a
                                i2 = (jp * NCG + n) * 2 + a
                                if a == 0:
                                    te.wait_ge(lda[L][sl], 16 * (jp * LDJ[sl] + cn))
                                if i2 >= 2:
                                    te.wait_ge(veW[L], i2 - 1)
                                te.matmul(pb[a][:, L * 128:(L + 1) * 128],
                                          Xs[L][a * 64:a * 64 + 64, sl * 256 + 128:sl * 256 + 256],
                                          kds[L][a * 64:a * 64 + 64, sl * 128:(sl + 1) * 128],
                                          start=True, stop=True, tile_position=(a * 64, 0)).then_inc(peW[L], 1)
                                if i2 >= 2:
                                    te.wait_ge(veS[L], i2 - 1)
                                te.wait_ge(veN[L], 2 * jp + a + 1)
                                te.matmul(pb[2 + a][:, L * 128:(L + 1) * 128],
                                          kdns[L][a * 64:a * 64 + 64, sl * 128:(sl + 1) * 128],
                                          Xs[L][a * 64:a * 64 + 64, sl * 256:sl * 256 + 128],
                                          start=True, stop=False, tile_position=(a * 64, 0))
                                te.wait_ge(veW[L], i2 + 1)
                                if i2 >= 2:
                                    te.wait_ge(seS[L], i2 - 1)
                                te.matmul(pb[2 + a][:, L * 128:(L + 1) * 128], wmt[L][:, a * 128:(a + 1) * 128],
                                          Sbf[:, vh * 128:(vh + 1) * 128],
                                          start=False, stop=True, tile_position=(0, 0)).then_inc(peP[L], 1)
                                if pass2:
                                    if i2 >= 2:
                                        te.wait_ge(veQm[L], i2 - 1)
                                    te.matmul(pb[4 + a][:, L * 64:(L + 1) * 64],
                                              Xs[L][a * 64:a * 64 + 64, sl * 256 + 128:sl * 256 + 256],
                                              ATs[L][a * 64:a * 64 + 64, (n // 8 % 2) * 512 + (n % 8) * 64:
                                                  (n // 8 % 2) * 512 + (n % 8 + 1) * 64],
                                              start=True, stop=True, tile_position=(a * 64, 0)).then_inc(peA[L], 1)
                                    if i2 >= 2:
                                        te.wait_ge(veO[L], i2 - 1)
                                    te.matmul(pb[6 + a][:, L * 64:(L + 1) * 64],
                                              Xs[L][a * 64:a * 64 + 64, sl * 256:sl * 256 + 128],
                                              ATs[L][a * 64:a * 64 + 64, (n // 8 % 2) * 512 + (n % 8) * 64:
                                                  (n // 8 % 2) * 512 + (n % 8 + 1) * 64],
                                              start=True, stop=False, tile_position=(a * 64, 0))
                                    te.wait_ge(veQm[L], i2 + 1)
                                    te.matmul(pb[6 + a][:, L * 64:(L + 1) * 64],
                                              Sbf[:, vh * 128:(vh + 1) * 128],
                                              qm[L][:, a * 64:(a + 1) * 64],
                                              start=False, stop=True, tile_position=(0, 0)).then_inc(peO[L], 1)
                    if pass2:
                        for L in range(L2):
                            for a in range(2):
                                te.wait_ge(sqd[L], 2 * jp + a + 1)
                                for s_ in range(NSL):
                                    si8 = (2 * jp + a) * NSL + s_
                                    if si8 >= 2:
                                        te.wait_ge(seR[L], si8 - 1)
                                    te.matmul(pb[2 * (si8 % 2) + L][:, 256:512], cs["c_ones_bf"][:, :],
                                              sqz[L][:, s_ * 256:(s_ + 1) * 256],
                                              start=True, stop=True).then_inc(peR[L], 1)

            @bP.gpsimd
            def _(gp):
                for jp in range(8):
                    for L in range(L2):
                        j = 2 * jp + L
                        if not pass2:
                            for a in range(2):
                                vh = j + 16 * a
                                gp.wait_ge(veS[L], 2 * NCG * jp + 2 * (NCG - 1) + a + 1)
                                gp.dma_start(dd["cc3_in"][vh * 128:(vh + 1) * 128, :],
                                             S32[:, vh * 128:(vh + 1) * 128]).then_inc(stS[L], 16)
                        else:
                            for a in range(2):
                                vh = j + 16 * a
                                gp.wait_ge(veF[L], 2 * jp + a + 1)
                                gp.dma_start(dd["sc_xgn"][vh * 128:(vh + 1) * 128, :],
                                             xg[L][:, a * T:(a + 1) * T]).then_inc(stG[L], 16)
                for L in range(L2):
                    if not pass2:
                        gp.wait_ge(stS[L], 32 * 8)
                    else:
                        gp.wait_ge(stG[L], 32 * 8)

    def phase_gdn_xchg3():
        s = P.sem()
        with nc.Block() as b:
            @b.gpsimd
            def _(gp):
                gp.collective_compute("AllGather", ALU.bypass,
                                      ins=[dd["cc3_in"][:, :].opt()],
                                      outs=[dd["cc3_out"][:, :].opt()],
                                      replica_groups=GROUPS).then_inc(s)
                gp.wait_ge(s, 1)

    def seg0():
        phase_consts()
    def seg1():
        phase_ln_standalone("x0")
    def seg2():
        xh_fm = P.sbuf("xh_fm", [128, 16 * T], BF16)
        wbuf = P.sbuf("wbuf", [128, 4 * 2048], BF16)
        evb = P.sbuf("evb", [128, 2 * MAXC], BF16)
        phase_xh_load(xh_fm)
        phase_fm_proj(xh_fm, wbuf, evb, "w_qT", D, "sc_q", "b_q")
        phase_fm_proj(xh_fm, wbuf, evb, "w_kT", D, "sc_k", "b_k")
        phase_fm_proj(xh_fm, wbuf, evb, "w_vT", D, "sc_v", "b_v")
    def seg3():
        phase_lightning()
    def seg4():
        phase_tm_proj_ln("oproj", "sc_o", 16, "sc_wo_bf", "b_o_rep", "x0", "sc_x1")
    def seg5():
        xh_fm = P.sbuf("xh_fm", [128, 16 * T], BF16)
        wbuf = P.sbuf("wbuf", [128, 4 * 2048], BF16)
        phase_xh_load(xh_fm)
        phase_ffn_gateup(xh_fm, wbuf)
    def seg6():
        phase_tm_proj_ln("ffn1d", "sc_h", 64, "sc_wd_bf", None, "sc_x1", "sc_x2",
                         stat_tiled=True)
    def seg7():
        own3 = P.sbuf("gdn_own3", [128, 256], BF16)
        with P.scope():
            xh_fm = P.sbuf("xh_fm", [128, 16 * T], BF16)
            wbuf = P.sbuf("wbuf", [128, 4 * 2048], BF16)
            phase_xh_load(xh_fm)
            phase_gdn_qkvz(xh_fm, wbuf, own3)
        with P.scope():
            phase_gdn_fixup(own3)
        egC_sb = P.sbuf("gdn_egC", [128, HV * NCG], F32)
        with P.scope():
            sel1 = P.sbuf("gdn_sel1", [32, HV * 128], F32)
            sel2 = P.sbuf("gdn_sel2", [32, HK * 128], F32)
            cs["c_sel1"] = sel1
            cs["c_sel2"] = sel2
            egl = P.sbuf("gdn_egl", [32, NCG], F32)
            cm2 = P.sbuf("gdn_cm2", [128, 6 * NCG * 16], F32)
            sls = P.sem()
            with nc.Block() as bsel:
                @bsel.sync
                def _(sync):
                    sync.dma_start(sel1[:, :], dd["c_sel1"][:, :]).then_inc(sls, 16)
                    sync.dma_start(sel2[:, :], dd["c_sel2"][:, :]).then_inc(sls, 16)
                    sync.wait_ge(sls, 32)
            with P.scope():
                phase_gdn_prep(cm2, egC_sb, egl)
            with P.scope():
                phase_gdn_l2norm()
            with P.scope():
                phase_gdn_egc(egC_sb, egl)
            with P.scope():
                phase_gdn_core(cm2, egC_sb, None)
        S32 = P.sbuf("gdn_S32", [128, HV * 128], F32)
        Sbf = P.sbuf("gdn_Sbf", [128, HV * 128], BF16)
        with P.scope():
            phase_gdn_scan(S32, Sbf, egC_sb, pass2=False)
        phase_gdn_xchg3()
        with P.scope():
            phase_gdn_scan(S32, Sbf, egC_sb, pass2=True)

    def seg7b():
        phase_tm_proj_ln("bproj", "sc_xgn", 32, "sc_wob_bf", None, "sc_x2", "sc_x3")
    def seg8():
        xh_fm = P.sbuf("xh_fm", [128, 16 * T], BF16)
        wbuf = P.sbuf("wbuf", [128, 4 * 2048], BF16)
        phase_xh_load(xh_fm)
        phase_ffn_gateup(xh_fm, wbuf)
    def seg9():
        phase_tm_proj_ln("ffn2d", "sc_h", 64, "sc_wd_bf", None, "sc_x3", "y",
                         final_out=True, stat_tiled=True)
    for i, seg in enumerate([seg0, seg1, seg2, seg3, seg4, seg5, seg6, seg7, seg7b, seg8, seg9]):
        if i < upto:
            with P.scope():
                seg()
    P.es.close()
    return P


def kernel(**inputs):
    x = np.asarray(inputs["L0_attn_a_x"])
    ins = {k: np.asarray(v) for k, v in inputs.items()}
    consts = host_constants()
    w = host_weights(ins)
    P = build(4096, dbg=False)
    in_maps = []
    for c in range(8):
        b, p = c // 2, c % 2
        m = {"x0": np.ascontiguousarray(x[b, p * 4096:(p + 1) * 4096, :])}
        m.update(w)
        m.update(consts)
        m["parity"] = np.full((128, 1), float(p), np.float32)
        in_maps.append(m)
    res = run_bass_kernel_spmd(P.nc, in_maps, core_ids=list(range(8)))
    out = np.empty((4, 8192, 2048), np.float32)
    for c in range(8):
        b, p = c // 2, c % 2
        out[b, p * 4096:(p + 1) * 4096, :] = res.results[c]["y"]
    return out


# version 12
# speedup vs baseline: 1.1082x; 1.0599x over previous
import contextlib
import math

import numpy as np
import ml_dtypes

import concourse.bass as bass
import concourse.mybir as mybir
from concourse.bass_utils import run_bass_kernel_spmd

F32 = mybir.dt.float32
BF16 = mybir.dt.bfloat16
AF = mybir.ActivationFunctionType
ALU = mybir.AluOpType
BF = ml_dtypes.bfloat16

D = 2048
DH = 8192
HA = 16
CA = 128
HK, HV, DK, DV = 16, 32, 128, 128
CK = 64
NEG = -1e30
LN_EPS = 1e-5
L2_EPS = 1e-6
RMS_EPS = 1e-6
SCALE_DK = DK ** -0.5


def _slopes16():
    start = 2.0 ** (-2.0 ** (-(math.log2(16) - 3)))
    return np.array([start ** (i + 1) for i in range(16)], dtype=np.float32)


def host_constants():
    c = {}
    lam = np.exp(-_slopes16()).astype(np.float64)
    pos = np.arange(CA, dtype=np.float64)
    dm = np.where(pos[None, :, None] >= pos[None, None, :],
                  lam[:, None, None] ** (pos[None, :, None] - pos[None, None, :]),
                  0.0)
    dmT = dm.transpose(2, 0, 1)
    c["c_dmaskT"] = np.ascontiguousarray(dmT.reshape(CA, HA * CA)).astype(BF)
    dq = lam[:, None] ** (pos[None, :] + 1.0)
    c["c_decq"] = np.ascontiguousarray(np.broadcast_to(dq[None], (128, HA, CA)).reshape(128, HA * CA)).astype(BF)
    dk_ = lam[:, None] ** (CA - 1.0 - pos[None, :])
    c["c_deck"] = np.ascontiguousarray(dk_.T.astype(np.float32))
    c["c_lamc"] = np.ascontiguousarray(np.broadcast_to((lam ** CA).astype(np.float32)[None], (128, HA)).copy())
    c["c_ones_bf"] = np.ones((128, 128), np.float32).astype(BF)
    c["c_ident_bf"] = np.eye(128, dtype=np.float32).astype(BF)
    tril_s = np.tril(np.ones((CK, CK), np.float32), -1)
    tril_i = np.tril(np.ones((CK, CK), np.float32))
    c["c_ubias_i"] = np.ascontiguousarray(np.tile(np.where(tril_i.T > 0, 0.0, NEG).astype(np.float32), (2, 8)))
    c["c_ubias_s"] = np.ascontiguousarray(np.tile(np.where(tril_s.T > 0, 0.0, NEG).astype(np.float32), (2, 8)))
    c["c_lbias_s"] = np.ascontiguousarray(np.tile(np.where(tril_s > 0, 0.0, NEG).astype(np.float32), (2, 8)))
    b2 = np.zeros((32, HK, 128), np.float32)
    for j in range(HK):
        b2[j, j, 0:64] = 1.0
        b2[j + 16, j, 64:128] = 1.0
    c["c_sel2"] = np.ascontiguousarray(b2.reshape(32, HK * 128))
    c["c_ident_f32"] = np.eye(128, dtype=np.float32)
    b1 = np.zeros((32, HV, 128), np.float32)
    for h in range(HV):
        b1[h, h, :] = 1.0
    c["c_sel1"] = np.ascontiguousarray(b1.reshape(32, HV * 128))
    return c


def host_weights(i):
    w = {}
    wq = i["L0_attn_a_w_qkv"]
    w["w_qT"] = np.ascontiguousarray(wq[0:D].T)
    w["w_kT"] = np.ascontiguousarray(wq[D:2 * D].T)
    w["w_vT"] = np.ascontiguousarray(wq[2 * D:3 * D].T)
    bq = i["L0_attn_a_b_qkv"]
    w["b_q"] = np.ascontiguousarray(bq[0:D].reshape(16, 128).T)
    w["b_k"] = np.ascontiguousarray(bq[D:2 * D].reshape(16, 128).T)
    w["b_v_rep"] = np.ascontiguousarray(np.broadcast_to(bq[2 * D:3 * D][None, :], (128, D)).copy())
    w["b_v"] = np.ascontiguousarray(bq[2 * D:3 * D].reshape(16, 128).T)
    w["w_oT"] = np.ascontiguousarray(i["L0_attn_a_w_out"].T)
    w["b_o_rep"] = np.ascontiguousarray(np.broadcast_to(i["L0_attn_a_b_out"][None, :], (128, D)).copy())
    w["w_gT"] = np.ascontiguousarray(i["L0_ffn_w_gate"].T)
    w["w_uT"] = np.ascontiguousarray(i["L0_ffn_w_up"].T)
    w["w_dT"] = np.ascontiguousarray(i["L0_ffn_w_down"].T)
    VPERM = [2 * j for j in range(16)] + [2 * j + 1 for j in range(16)]
    Wqkvz = i["L0_attn_b_W_qkvz"]
    blk = 2 * DK + 2 * 2 * DV
    cols = []
    for h in range(HK):
        cols.append(np.arange(h * blk, h * blk + DK))
    for h in range(HK):
        cols.append(np.arange(h * blk + DK, h * blk + 2 * DK))
    for dv in VPERM:
        h, r = dv // 2, dv % 2
        cols.append(np.arange(h * blk + 2 * DK + r * DV, h * blk + 2 * DK + (r + 1) * DV))
    for dv in VPERM:
        h, r = dv // 2, dv % 2
        cols.append(np.arange(h * blk + 2 * DK + 2 * DV + r * DV, h * blk + 2 * DK + 2 * DV + (r + 1) * DV))
    w["w_qkvzT"] = np.ascontiguousarray(Wqkvz[:, np.concatenate(cols)])
    Wba = i["L0_attn_b_W_ba"]
    bcols = [(ov // 2) * 4 + (ov % 2) for ov in VPERM]
    acols = [(ov // 2) * 4 + 2 + (ov % 2) for ov in VPERM]
    w["w_baT"] = np.ascontiguousarray(Wba[:, bcols + acols])
    cw = i["L0_attn_b_conv_w"].reshape(2, HK, DK, 4) if False else i["L0_attn_b_conv_w"]
    vrows = np.concatenate([np.arange(4096 + ov * DV, 4096 + (ov + 1) * DV) for ov in VPERM])
    rows = np.concatenate([np.arange(0, 4096), vrows])
    cwp = cw[rows]
    w["convw"] = np.ascontiguousarray(cwp.reshape(64, 128, 4).transpose(1, 0, 2).reshape(128, 256))
    w["dt_bias"] = np.ascontiguousarray(i["L0_attn_b_dt_bias"][VPERM].reshape(32, 1))
    w["A_log"] = np.ascontiguousarray(i["L0_attn_b_A_log"][VPERM].reshape(32, 1))
    w["normw"] = np.ascontiguousarray(i["L0_attn_b_norm_weight"].reshape(128, 1))
    wrows = np.concatenate([np.arange(ov * DV, (ov + 1) * DV) for ov in VPERM])
    w["w_outB"] = np.ascontiguousarray(i["L0_attn_b_W_out"][wrows])
    return w


WEIGHT_SHAPES = {
    "w_qT": [D, D], "w_kT": [D, D], "w_vT": [D, D], "w_oT": [D, D],
    "b_q": [128, 16], "b_k": [128, 16], "b_v": [128, 16], "b_v_rep": [128, D], "b_o_rep": [128, D],
    "w_gT": [D, DH], "w_uT": [D, DH], "w_dT": [DH, D],
    "w_qkvzT": [D, 12288], "w_baT": [D, 64], "convw": [128, 256],
    "dt_bias": [32, 1], "A_log": [32, 1], "normw": [128, 1], "w_outB": [4096, D],
}
CONST_SHAPES = {
    "c_dmaskT": ([CA, HA * CA], BF16), "c_decq": ([128, HA * CA], BF16),
    "c_deck": ([128, HA], F32), "c_lamc": ([128, HA], F32),
    "c_ones_bf": ([128, 128], BF16), "c_ident_bf": ([128, 128], BF16),
    "c_ubias_i": ([128, 8 * CK], F32), "c_ubias_s": ([128, 8 * CK], F32),
    "c_lbias_s": ([128, 8 * CK], F32),
    "c_sel2": ([32, HK * 128], F32), "c_sel1": ([32, HV * 128], F32),
    "c_ident_f32": ([128, 128], F32),
    "parity": ([128, 1], F32),
}
SB_CONSTS = ["c_dmaskT", "c_decq", "c_deck", "c_lamc", "c_ones_bf", "c_ident_bf",
             "c_strictneg", "c_striuneg", "c_inclbias", "c_upperbias",
             "c_sel2", "c_sel1", "parity", "b_q", "b_k", "dt_bias", "A_log", "normw"]


class Prog:
    def __init__(self, T=4096, dbg=False):
        self.nc = bass.Bass("TRN2", target_bir_lowering=False)
        self.T = T
        self.dbg = dbg
        self.es = contextlib.ExitStack()
        self.scopes = [self.es]
        self.scope_sems = [[]]
        self.d = {}
        self._sem_n = 0
        self._sb_n = 0

    def dram(self, name, shape, dtype, kind=None):
        t = self.nc.dram_tensor(name, shape, dtype) if kind is None else \
            self.nc.dram_tensor(name, shape, dtype, kind=kind)
        self.d[name] = t.ap()

    def scratch(self, name, shape, dtype):
        self.dram(name, shape, dtype, "ExternalOutput" if self.dbg else None)

    def sbuf(self, name, shape, dtype):
        self._sb_n += 1
        return self.scopes[-1].enter_context(
            self.nc.sbuf_tensor(f"{name}_{self._sb_n}", shape, dtype))

    def sem(self):
        self._sem_n += 1
        h = self.nc.alloc_semaphore(f"s{self._sem_n}")
        self.scope_sems[-1].append(h)
        return h

    def sems(self, n):
        return [self.sem() for _ in range(n)]

    @contextlib.contextmanager
    def scope(self):
        es = contextlib.ExitStack()
        self.scopes.append(es)
        self.scope_sems.append([])
        try:
            with es:
                yield
        finally:
            self.scopes.pop()
            freed = self.scope_sems.pop()
            if freed:
                self.nc.clear_and_free_semaphores(freed)


def build(T=4096, dbg=False, upto=99, nocc=False):
    P = Prog(T, dbg)
    nc = P.nc
    NT = T // 128
    NCA = T // CA
    NCG = T // CK
    MAXC = max(T, 2048)

    P.dram("x0", [T, D], F32, "ExternalInput")
    for n, sh in WEIGHT_SHAPES.items():
        P.dram(n, sh, F32, "ExternalInput")
    for n, (sh, dt) in CONST_SHAPES.items():
        P.dram(n, sh, dt, "ExternalInput")
    P.dram("y", [T, D], F32, "ExternalOutput")

    P.scratch("sc_xh", [T, D], BF16)
    P.scratch("sc_x1", [T, D], F32)
    P.scratch("sc_x2", [T, D], F32)
    P.scratch("sc_x3", [T, D], F32)
    P.scratch("sc_q", [D, T], BF16)
    P.scratch("sc_k", [D, T], BF16)
    P.scratch("sc_v", [D, T], BF16)
    P.scratch("sc_kd", [D, T], BF16)
    P.scratch("sc_vt", [D, T], BF16)
    P.scratch("sc_o", [D, T], BF16)
    P.scratch("sc_h", [(DH // 128) * (T // 128), 128, 128], BF16)
    P.scratch("sc_wo_bf", [D, D], BF16)
    P.scratch("sc_wd_bf", [DH, D], BF16)
    P.scratch("sc_wob_bf", [4096, D], BF16)
    P.scratch("sc_qg", [D, T], BF16)
    P.scratch("sc_kg", [D, T], BF16)
    P.scratch("sc_vg", [HV * DV, T], BF16)
    P.scratch("sc_z", [HV * DV, T], BF16)
    P.scratch("sc_xgn", [HV * DV, T], BF16)
    P.scratch("sc_ba", [64, T], F32)
    P.scratch("sc_gba", [96, T], F32)
    P.scratch("sc_cm", [5, NCG // 2, 128, 32], F32)
    P.scratch("sc_X", [NCG * 16, 128, 256], BF16)
    P.scratch("sc_kd2", [NCG * 16, 128, 128], BF16)
    P.scratch("sc_kdn", [NCG * 16, 128, 128], BF16)
    P.scratch("sc_AT", [(NCG // 8) * 16, 128, 512], BF16)
    P.scratch("sc_qdT", [HV * (NCG // 8), 128, 512], BF16)
    P.dram("cc1_in", [HA * 128, 128], F32)
    P.dram("cc1_out", [2 * HA * 128, 128], F32)
    P.dram("cc2_in", [DH, 4], F32)
    P.dram("cc2_out", [2 * DH, 4], F32)
    P.dram("cc3_in", [HV * 128, 128], F32)
    P.dram("cc3_out", [2 * HV * 128, 128], F32)
    GROUPS = [[0, 1], [2, 3], [4, 5], [6, 7]]
    dd = P.d

    cs = {}
    for n in ["c_deck", "c_lamc", "c_ones_bf", "c_ident_bf", "c_ident_f32", "parity",
              "c_ubias_i", "c_ubias_s", "c_lbias_s",
              "b_q", "b_k", "b_v", "dt_bias", "A_log", "normw"]:
        if n in CONST_SHAPES:
            sh, dt = CONST_SHAPES[n]
        elif n == "b_v":
            sh, dt = [128, 16], F32
        else:
            sh, dt = WEIGHT_SHAPES[n], F32
        cs[n] = P.sbuf("sb_" + n, sh, dt)
    eps_ln = P.sbuf("eps_ln", [128, 1], F32)
    eps_l2 = P.sbuf("eps_l2", [128, 1], F32)
    eps_rms = P.sbuf("eps_rms", [128, 1], F32)


    pb = [P.es.enter_context(nc.psum_tensor(f"pb{i}", [128, 512], F32)) for i in range(8)]

    s_pre = P.sem()

    def phase_consts():
        s = P.sem()
        names = ["c_deck", "c_lamc", "c_ones_bf", "c_ident_bf", "c_ident_f32", "parity",
                 "c_ubias_i", "c_ubias_s", "c_lbias_s",
                 "b_q", "b_k", "b_v", "dt_bias", "A_log", "normw"]
        with nc.Block() as b:
            @b.sync
            def _(sync):
                for n in names:
                    sync.dma_start(cs[n][:, :], dd[n][:, :]).then_inc(s, 16)
                sync.wait_ge(s, 16 * len(names))

            @b.vector
            def _(ve):
                ve.memset(eps_ln[:, :], LN_EPS)
                ve.memset(eps_l2[:, :], L2_EPS)
                ve.memset(eps_rms[:, :], RMS_EPS)

            @b.gpsimd
            def _(gp):
                gp.dma_start(dd["sc_wo_bf"][:, :], dd["w_oT"][:, :]).then_inc(s_pre, 16)
                gp.dma_start(dd["sc_wd_bf"][:, :], dd["w_dT"][:, :]).then_inc(s_pre, 16)
                gp.dma_start(dd["sc_wob_bf"][:, :], dd["w_outB"][:, :]).then_inc(s_pre, 16)


    def phase_ln_standalone(srcname):
        lda = P.sems(2)
        sts = P.sems(2)
        ves, ses = P.sem(), P.sem()
        xt = P.sbuf("xt", [128, 2 * D], F32)
        xht = P.sbuf("xht", [128, 2 * D], BF16)
        stats = P.sbuf("stats", [128, 64], F32)
        aggr = P.sbuf("aggr", [128, 8], F32)
        with nc.Block() as b:
            @b.sync
            def _(sync):
                for i in range(NT):
                    bu = i % 2
                    if i >= 2:
                        sync.wait_ge(ves, 2 * (i - 1))
                    sync.dma_start(xt[:, bu * D:(bu + 1) * D],
                                   dd[srcname][i * 128:(i + 1) * 128, :]).then_inc(lda[bu], 16)

            @b.vector
            def _(ve):
                for i in range(NT):
                    bu = i % 2
                    ve.wait_ge(lda[bu], 16 * (i // 2 + 1))
                    for j in range(4):
                        ve.bn_stats(stats[:, bu * 32 + 6 * j:bu * 32 + 6 * j + 6],
                                    xt[:, bu * D + 512 * j:bu * D + 512 * j + 512])
                    ve.drain()
                    ve.bn_aggr(aggr[:, bu * 4:bu * 4 + 2], stats[:, bu * 32:bu * 32 + 24]).then_inc(ves, 1)
                    ve.wait_ge(ses, i + 1)
                    ve.reciprocal(aggr[:, bu * 4 + 3:bu * 4 + 4], aggr[:, bu * 4 + 2:bu * 4 + 3])
                    ve.drain()
                    if i >= 2:
                        ve.wait_ge(sts[bu], 16 * (i // 2))
                    ve.tensor_scalar(xht[:, bu * D:(bu + 1) * D], xt[:, bu * D:(bu + 1) * D],
                                     aggr[:, bu * 4:bu * 4 + 1], aggr[:, bu * 4 + 3:bu * 4 + 4],
                                     op0=ALU.subtract, op1=ALU.mult).then_inc(ves, 1)

            @b.scalar
            def _(sc):
                for i in range(NT):
                    bu = i % 2
                    sc.wait_ge(ves, 2 * i + 1)
                    sc.activation(aggr[:, bu * 4 + 2:bu * 4 + 3], aggr[:, bu * 4 + 1:bu * 4 + 2],
                                  AF.Sqrt, bias=eps_ln[:, 0:1], scale=1.0).then_inc(ses, 1)

            @b.gpsimd
            def _(gp):
                for i in range(NT):
                    bu = i % 2
                    gp.wait_ge(ves, 2 * (i + 1))
                    gp.dma_start(dd["sc_xh"][i * 128:(i + 1) * 128, :],
                                 xht[:, bu * D:(bu + 1) * D]).then_inc(sts[bu], 16)
                gp.wait_ge(sts[0], 16 * ((NT + 1) // 2))
                gp.wait_ge(sts[1], 16 * (NT // 2))

    def phase_xh_load(xh_fm):
        s = P.sem()
        with nc.Block() as b:
            @b.sync
            def _(sync):
                for t in range(16):
                    sync.dma_start_transpose(xh_fm[:, t * T:(t + 1) * T],
                                             dd["sc_xh"][:, t * 128:(t + 1) * 128]).then_inc(s, 16)
                sync.wait_ge(s, 256)

    def phase_fm_proj(xh_fm, wbuf, evb, wname, dout, dstname, bias_col=None, bias_off=0, row0=0):
        ndt = dout // 128
        nst = max(T // 512, 1)
        halves = 2 if nst >= 2 else 1
        nh = nst // halves
        ldw = P.sems(4)
        sts = P.sems(2)
        pes, ves = P.sem(), P.sem()
        wd = dd[wname]
        dst = dd[dstname]
        with nc.Block() as b:
            @b.gpsimd
            def _(gp):
                for t in range(ndt):
                    wb = t % 4
                    if t >= 4:
                        gp.wait_ge(pes, halves * (t - 3))
                    srcw = wd[:, t * 128:(t + 1) * 128].rearrange("(a p) m -> p a m", p=128)
                    dstw = wbuf[:, wb * 2048:(wb + 1) * 2048].rearrange("p (a m) -> p a m", a=16)
                    gp.dma_start(dstw, srcw).then_inc(ldw[wb], 16)

            @b.tensor
            def _(te):
                for t in range(ndt):
                    wb = t % 4
                    te.wait_ge(ldw[wb], 16 * (t // 4 + 1))
                    for half in range(halves):
                        g = halves * t + half
                        if g >= halves:
                            te.wait_ge(ves, g - halves + 1)
                        for k in range(16):
                            for s_ in range(nh):
                                si = half * nh + s_
                                mm = te.matmul(pb[half * 4 + s_][:, 0:512],
                                               wbuf[:, wb * 2048 + k * 128:wb * 2048 + (k + 1) * 128],
                                               xh_fm[:, k * T + si * 512:k * T + si * 512 + 512],
                                               start=(k == 0), stop=(k == 15))
                        mm.then_inc(pes, 1)

            @b.vector
            def _(ve):
                for t in range(ndt):
                    eb = (t % 2) * MAXC
                    for half in range(halves):
                        g = halves * t + half
                        ve.wait_ge(pes, g + 1)
                        if half == 0 and t >= 2:
                            ve.wait_ge(sts[t % 2], 16 * (t // 2))
                        for s_ in range(nh):
                            si = half * nh + s_
                            op = ve.tensor_scalar(evb[:, eb + si * 512:eb + (si + 1) * 512],
                                                  pb[half * 4 + s_][:, 0:512],
                                                  cs[bias_col][:, bias_off + t:bias_off + t + 1] if bias_col else 0.0,
                                                  None, op0=ALU.add)
                        op.then_inc(ves, 1)

            @b.sync
            def _(sync):
                for t in range(ndt):
                    eb = (t % 2) * MAXC
                    sync.wait_ge(ves, halves * (t + 1))
                    sync.dma_start(dst[row0 + t * 128:row0 + (t + 1) * 128, :],
                                   evb[:, eb:eb + T]).then_inc(sts[t % 2], 16)
                sync.wait_ge(sts[0], 16 * ((ndt + 1) // 2))
                sync.wait_ge(sts[1], 16 * (ndt // 2))

    def phase_lightning():
        NC = NCA
        nh2 = HA // 2
        dmask = P.sbuf("la_dmask", [CA, HA * CA], BF16)
        decq = P.sbuf("la_decq", [128, HA * CA], BF16)
        qf = [P.sbuf(f"la_qf{s}", [128, T], BF16) for s in range(2)]
        kf = [P.sbuf(f"la_kf{s}", [128, T], BF16) for s in range(2)]
        vf = [P.sbuf(f"la_vf{s}", [128, T], BF16) for s in range(2)]
        vt = [P.sbuf(f"la_vt{s}", [128, T], BF16) for s in range(2)]
        ktm = [P.sbuf(f"la_ktm{s}", [128, T], BF16) for s in range(2)]
        scT = [P.sbuf(f"la_scT{s}", [128, 128], BF16) for s in range(2)]
        qdc = [P.sbuf(f"la_qdc{s}", [128, 2 * 128], BF16) for s in range(2)]
        S32 = [P.sbuf(f"la_S32{s}", [128, 128], F32) for s in range(2)]
        Sbf = [P.sbuf(f"la_Sbf{s}", [128, 128], BF16) for s in range(2)]
        Sini = [P.sbuf(f"la_Sini{s}", [128, 128], F32) for s in range(2)]
        ob = [P.sbuf(f"la_ob{s}", [128, T], BF16) for s in range(2)]
        pT = [pb[6][:, :].bitcast(BF16), pb[7][:, :].bitcast(BF16)]

        sc_load = P.sem()

        def pass1():
            lda = [P.sems(1)[0], P.sems(1)[0]]
            peT = P.sems(2); peS = P.sems(2); peO = P.sems(2); peKV = P.sems(2)
            veK = P.sems(2); veS = P.sems(2); veV = P.sems(2); veO = P.sems(2); veKV = P.sems(2)
            sK = P.sems(2); sS = P.sems(2); sO = P.sems(2)
            with nc.Block() as b:
                @b.sync
                def _(sync):
                    for hi in range(nh2):
                        for s in range(2):
                            h = 2 * hi + s
                            if hi >= 1:
                                sync.wait_ge(veKV[s], hi * NC)
                            sync.dma_start(qf[s][:, :], dd["sc_q"][h * 128:(h + 1) * 128, :]).then_inc(lda[s], 16)
                            sync.dma_start(kf[s][:, :], dd["sc_k"][h * 128:(h + 1) * 128, :]).then_inc(lda[s], 16)
                            sync.dma_start(vf[s][:, :], dd["sc_v"][h * 128:(h + 1) * 128, :]).then_inc(lda[s], 16)

                @b.tensor
                def _(te):
                    for hi in range(nh2):
                        for c in range(NC):
                            for s in range(2):
                                i = hi * NC + c
                                if c == 0:
                                    te.wait_ge(lda[s], 48 * (hi + 1))
                                if i > 0:
                                    te.wait_ge(veK[s], i)
                                te.transpose(pT[s][:, 0:128], vf[s][:, c * 128:(c + 1) * 128],
                                             cs["c_ident_bf"][:, :]).then_inc(peT[s], 1)
                                if i > 0:
                                    te.wait_ge(veS[s], i)
                                te.matmul(pb[3 * s][:, 0:128], kf[s][:, c * 128:(c + 1) * 128],
                                          qf[s][:, c * 128:(c + 1) * 128],
                                          start=True, stop=True).then_inc(peS[s], 1)
                                te.wait_ge(veV[s], i + 1)
                                te.transpose(pT[s][:, 128:256], kf[s][:, c * 128:(c + 1) * 128],
                                             cs["c_ident_bf"][:, :]).then_inc(peT[s], 1)
                                te.wait_ge(veS[s], i + 1)
                                if i > 0:
                                    te.wait_ge(veO[s], i)
                                te.matmul(pb[3 * s + 1][:, 0:128], vt[s][:, c * 128:(c + 1) * 128],
                                          scT[s][:, 0:128], start=True, stop=True).then_inc(peO[s], 1)
                                te.wait_ge(veK[s], i + 1)
                                if i > 0:
                                    te.wait_ge(veKV[s], i)
                                te.matmul(pb[3 * s + 2][:, 0:128], ktm[s][:, c * 128:(c + 1) * 128],
                                          vt[s][:, c * 128:(c + 1) * 128],
                                          start=True, stop=True).then_inc(peKV[s], 1)

                @b.vector
                def _(ve):
                    ve.wait_ge(sc_load, 32)
                    for hi in range(nh2):
                        for c in range(NC):
                            for s in range(2):
                                h = 2 * hi + s
                                i = hi * NC + c
                                if c == 0 and hi >= 1:
                                    ve.wait_ge(sK[s], 32 * hi)
                                ve.wait_ge(peT[s], 2 * i + 1)
                                ve.tensor_copy(vt[s][:, c * 128:(c + 1) * 128], pT[s][:, 0:128]).then_inc(veV[s], 1)
                                ve.wait_ge(peT[s], 2 * i + 2)
                                ve.tensor_scalar(ktm[s][:, c * 128:(c + 1) * 128], pT[s][:, 128:256],
                                                 cs["c_deck"][:, h:h + 1], None, op0=ALU.mult).then_inc(veK[s], 1)
                                ve.wait_ge(peS[s], i + 1)
                                ve.tensor_tensor(scT[s][:, 0:128], pb[3 * s][:, 0:128],
                                                 dmask[:, h * 128:(h + 1) * 128],
                                                 op=ALU.mult).then_inc(veS[s], 1)
                                ve.wait_ge(peKV[s], i + 1)
                                if c == 0:
                                    if hi >= 1:
                                        ve.wait_ge(sS[s], 16 * hi)
                                    ve.tensor_copy(S32[s][:, :], pb[3 * s + 2][:, 0:128]).then_inc(veKV[s], 1)
                                else:
                                    ve.drain()
                                    ve.scalar_tensor_tensor(S32[s][:, :], S32[s][:, :],
                                                            cs["c_lamc"][:, h:h + 1], pb[3 * s + 2][:, 0:128],
                                                            op0=ALU.mult, op1=ALU.add).then_inc(veKV[s], 1)

                @b.scalar
                def _(sc):
                    sc.dma_start(dmask[:, :], dd["c_dmaskT"][:, :]).then_inc(sc_load, 16)
                    sc.dma_start(decq[:, :], dd["c_decq"][:, :]).then_inc(sc_load, 16)
                    for hi in range(nh2):
                        for c in range(NC):
                            for s in range(2):
                                h = 2 * hi + s
                                i = hi * NC + c
                                if c == 0 and hi >= 1:
                                    sc.wait_ge(sO[s], 16 * hi)
                                sc.wait_ge(peO[s], i + 1)
                                sc.copy(ob[s][:, c * 128:(c + 1) * 128],
                                        pb[3 * s + 1][:, 0:128]).then_inc(veO[s], 1)

                @b.gpsimd
                def _(gp):
                    for hi in range(nh2):
                        for s in range(2):
                            h = 2 * hi + s
                            gp.wait_ge(veK[s], (hi + 1) * NC)
                            gp.dma_start(dd["sc_kd"][h * 128:(h + 1) * 128, :], ktm[s][:, :]).then_inc(sK[s], 16)
                            gp.wait_ge(veV[s], (hi + 1) * NC)
                            gp.dma_start(dd["sc_vt"][h * 128:(h + 1) * 128, :], vt[s][:, :]).then_inc(sK[s], 16)
                            gp.wait_ge(veKV[s], (hi + 1) * NC)
                            gp.dma_start(dd["cc1_in"][h * 128:(h + 1) * 128, :], S32[s][:, :]).then_inc(sS[s], 16)
                            gp.wait_ge(veO[s], (hi + 1) * NC)
                            gp.dma_start(dd["sc_o"][h * 128:(h + 1) * 128, :], ob[s][:, :]).then_inc(sO[s], 16)
                    for s in range(2):
                        gp.wait_ge(sK[s], 32 * nh2)
                        gp.wait_ge(sS[s], 16 * nh2)
                        gp.wait_ge(sO[s], 16 * nh2)

        def xchg1():
            s = P.sem()
            with nc.Block() as b:
                @b.gpsimd
                def _(gp):
                    if nocc:
                        gp.dma_start(dd["cc1_out"][0:HA * 128, :], dd["cc1_in"][:, :]).then_inc(s, 16)
                        gp.wait_ge(s, 16)
                    else:
                        gp.collective_compute("AllGather", ALU.bypass,
                                              ins=[dd["cc1_in"][:, :].opt()],
                                              outs=[dd["cc1_out"][:, :].opt()],
                                              replica_groups=GROUPS).then_inc(s)
                        gp.wait_ge(s, 1)

        def pass2():
            lda = [P.sems(1)[0], P.sems(1)[0]]
            ldS = P.sems(2)
            peI = P.sems(2); peKV = P.sems(2)
            veQ = P.sems(2); veO = P.sems(2); veKV = P.sems(2); veSb = P.sems(2)
            sO = P.sems(2)
            with nc.Block() as b:
                @b.sync
                def _(sync):
                    for hi in range(nh2):
                        for s in range(2):
                            h = 2 * hi + s
                            if hi >= 1:
                                sync.wait_ge(veKV[s], hi * NC)
                                sync.wait_ge(sO[s], 16 * hi)
                            sync.dma_start(qf[s][:, :], dd["sc_q"][h * 128:(h + 1) * 128, :]).then_inc(lda[s], 16)
                            sync.dma_start(ktm[s][:, :], dd["sc_kd"][h * 128:(h + 1) * 128, :]).then_inc(lda[s], 16)
                            sync.dma_start(vt[s][:, :], dd["sc_vt"][h * 128:(h + 1) * 128, :]).then_inc(lda[s], 16)
                            sync.dma_start(ob[s][:, :], dd["sc_o"][h * 128:(h + 1) * 128, :]).then_inc(lda[s], 16)

                @b.scalar
                def _(sc):
                    for hi in range(nh2):
                        for s in range(2):
                            h = 2 * hi + s
                            if hi >= 1:
                                sc.wait_ge(veSb[s], (hi - 1) * (NC + 1) + 1)
                            sc.dma_start(Sini[s][:, :], dd["cc1_out"][h * 128:(h + 1) * 128, :]).then_inc(ldS[s], 16)

                @b.vector
                def _(ve):
                    for hi in range(nh2):
                        for c in range(NC):
                            for s in range(2):
                                h = 2 * hi + s
                                i = hi * NC + c
                                if c == 0:
                                    ve.wait_ge(ldS[s], 16 * (hi + 1))
                                    ve.tensor_scalar(S32[s][:, :], Sini[s][:, :], cs["parity"][:, 0:1],
                                                     None, op0=ALU.mult)
                                    ve.drain()
                                    ve.tensor_copy(Sbf[s][:, :], S32[s][:, :]).then_inc(veSb[s], 1)
                                ve.wait_ge(lda[s], 64 * (hi + 1))
                                if i > 0:
                                    ve.wait_ge(peI[s], i)
                                ve.tensor_tensor(qdc[s][:, (c % 2) * 128:(c % 2) * 128 + 128],
                                                 qf[s][:, c * 128:(c + 1) * 128],
                                                 decq[:, h * 128:(h + 1) * 128],
                                                 op=ALU.mult).then_inc(veQ[s], 1)
                                ve.wait_ge(peI[s], i + 1)
                                ve.tensor_tensor(ob[s][:, c * 128:(c + 1) * 128],
                                                 pb[3 * s + 1][:, 0:128],
                                                 ob[s][:, c * 128:(c + 1) * 128],
                                                 op=ALU.add).then_inc(veO[s], 1)
                                ve.wait_ge(peKV[s], i + 1)
                                ve.drain()
                                ve.scalar_tensor_tensor(S32[s][:, :], S32[s][:, :],
                                                        cs["c_lamc"][:, h:h + 1], pb[3 * s + 2][:, 0:128],
                                                        op0=ALU.mult, op1=ALU.add).then_inc(veKV[s], 1)
                                ve.drain()
                                ve.tensor_copy(Sbf[s][:, :], S32[s][:, :]).then_inc(veSb[s], 1)

                @b.tensor
                def _(te):
                    for hi in range(nh2):
                        for c in range(NC):
                            for s in range(2):
                                i = hi * NC + c
                                te.wait_ge(veSb[s], i + hi + 1)
                                te.wait_ge(veQ[s], i + 1)
                                if i > 0:
                                    te.wait_ge(veO[s], i)
                                te.matmul(pb[3 * s + 1][:, 0:128], Sbf[s][:, :],
                                          qdc[s][:, (c % 2) * 128:(c % 2) * 128 + 128],
                                          start=True, stop=True).then_inc(peI[s], 1)
                                if i > 0:
                                    te.wait_ge(veKV[s], i)
                                te.matmul(pb[3 * s + 2][:, 0:128], ktm[s][:, c * 128:(c + 1) * 128],
                                          vt[s][:, c * 128:(c + 1) * 128],
                                          start=True, stop=True).then_inc(peKV[s], 1)

                @b.gpsimd
                def _(gp):
                    for hi in range(nh2):
                        for s in range(2):
                            h = 2 * hi + s
                            gp.wait_ge(veO[s], (hi + 1) * NC)
                            gp.dma_start(dd["sc_o"][h * 128:(h + 1) * 128, :], ob[s][:, :]).then_inc(sO[s], 16)
                    gp.wait_ge(sO[0], 16 * nh2)
                    gp.wait_ge(sO[1], 16 * nh2)

        pass1()
        xchg1()
        pass2()

    def phase_tm_proj_ln(tag, stat_name, nk, w_bf_name, bias_rep_name, xold_name, xnew_name,
                         final_out=False, stat_tiled=False):
        ldw = P.sems(2)
        lds = P.sems(2)
        ldx = P.sems(4)
        ldbr = P.sem()
        pes, vesP, vesL, ses = P.sem(), P.sem(), P.sem(), P.sem()
        sto = P.sems(2)
        KG = 4
        npc = nk // KG
        wpc = P.sbuf(f"wpc_{tag}", [128, 2 * KG * 2048], BF16)
        xnw = P.sbuf(f"xnw_{tag}", [128, 2 * 2 * D], F32)
        xht = P.sbuf(f"xht_{tag}", [128, 2 * D], BF16)
        stats = P.sbuf(f"stats_{tag}", [128, 64], F32)
        aggr = P.sbuf(f"aggr_{tag}", [128, 8], F32)
        stb = P.sbuf(f"stb_{tag}", [128, 2 * nk * 256], BF16)
        brep = P.sbuf(f"brep_{tag}", [128, D], F32) if bias_rep_name else None
        xold = P.sbuf(f"xold_{tag}", [128, 2 * 2 * D], F32)
        wsrc = dd[w_bf_name]
        ssrc = dd[stat_name]
        NFL = NT // 2
        per = 1 if final_out else 2
        with nc.Block() as b:
            @b.sync
            def _(sync):
                sync.wait_ge(s_pre, 48)
                if bias_rep_name:
                    sync.dma_start(brep[:, :], dd[bias_rep_name][:, :]).then_inc(ldbr, 16)
                npt = 0
                for fl in range(NFL):
                    slot = fl % 2
                    if fl >= 2:
                        sync.wait_ge(pes, npc * (fl - 1))
                    ssl = stb[:, slot * nk * 256:(slot + 1) * nk * 256].rearrange(
                        "p (k c) -> p k c", k=nk)
                    if stat_tiled:
                        for tt in range(2):
                            t = 2 * fl + tt
                            sync.dma_start(ssl[:, :, tt * 128:(tt + 1) * 128],
                                           ssrc[t:nk * NT:NT, :, :].rearrange("k p c -> p k c")) \
                                .then_inc(lds[slot], 16)
                    else:
                        sync.dma_start(ssl, ssrc[:, 2 * fl * 128:(2 * fl + 2) * 128]
                                       .rearrange("(k p) c -> p k c", p=128)).then_inc(lds[slot], 16)
                    for tt in range(2):
                        t = 2 * fl + tt
                        if fl >= 2:
                            sync.wait_ge(vesL, 4 * (fl - 1))
                        sync.dma_start(xold[:, (slot * 2 + tt) * D:(slot * 2 + tt + 1) * D],
                                       dd[xold_name][t * 128:(t + 1) * 128, :]).then_inc(ldx[slot * 2 + tt], 16)
                    for pc in range(npc):
                        wslot = npt % 2
                        if npt >= 2:
                            sync.wait_ge(pes, npt - 1)
                        wsl = wpc[:, wslot * KG * 2048:(wslot + 1) * KG * 2048].rearrange("p (a m) -> p a m", a=KG)
                        sync.dma_start(wsl, wsrc[pc * KG * 128:(pc + 1) * KG * 128, :]
                                       .rearrange("(a p) m -> p a m", p=128)).then_inc(ldw[wslot], 16)
                        npt += 1

            @b.tensor
            def _(te):
                stper = 2 if stat_tiled else 1
                npt = 0
                for fl in range(NFL):
                    slot = fl % 2
                    te.wait_ge(lds[slot], 16 * stper * (fl // 2 + 1))
                    for pc in range(npc):
                        wslot = npt % 2
                        te.wait_ge(ldw[wslot], 16 * (npt // 2 + 1))
                        if pc == 0 and fl >= 1:
                            te.wait_ge(vesP, 2 * fl)
                        for kk in range(KG):
                            k = pc * KG + kk
                            for tt in range(2):
                                for q in range(4):
                                    mm = te.matmul(pb[tt * 4 + q][:, 0:512],
                                                   stb[:, slot * nk * 256 + k * 256 + tt * 128:
                                                       slot * nk * 256 + k * 256 + (tt + 1) * 128],
                                                   wpc[:, wslot * KG * 2048 + kk * 2048 + q * 512:
                                                       wslot * KG * 2048 + kk * 2048 + (q + 1) * 512],
                                                   start=(k == 0), stop=(k == nk - 1))
                        mm.then_inc(pes, 1)
                        npt += 1

            @b.vector
            def _(ve):
                if bias_rep_name:
                    ve.wait_ge(ldbr, 16)
                for fl in range(NFL):
                    slot = fl % 2
                    ve.wait_ge(pes, npc * (fl + 1))
                    ve.wait_ge(ldx[slot * 2], 16 * (fl // 2 + 1))
                    ve.wait_ge(ldx[slot * 2 + 1], 16 * (fl // 2 + 1))
                    for tt in range(2):
                        t = 2 * fl + tt
                        bu = slot * 2 + tt
                        if fl >= 2:
                            ve.wait_ge(vesL, 4 * (fl - 1))
                        for q in range(4):
                            op = ve.tensor_tensor(xnw[:, bu * D + q * 512:bu * D + (q + 1) * 512],
                                                  pb[tt * 4 + q][:, 0:512],
                                                  xold[:, bu * D + q * 512:bu * D + (q + 1) * 512],
                                                  op=ALU.add)
                        op.then_inc(vesP, 1)
                    for tt in range(2):
                        t = 2 * fl + tt
                        bu = slot * 2 + tt
                        xb = bu % 2
                        if bias_rep_name:
                            ve.drain()
                            for q in range(4):
                                ve.tensor_tensor(xnw[:, bu * D + q * 512:bu * D + (q + 1) * 512],
                                                 xnw[:, bu * D + q * 512:bu * D + (q + 1) * 512],
                                                 brep[:, q * 512:(q + 1) * 512], op=ALU.add)
                        ve.drain()
                        if final_out:
                            ve.tensor_copy(aggr[:, 4 * xb:4 * xb + 1], xnw[:, bu * D:bu * D + 1]).then_inc(vesL, 2)
                            continue
                        for j in range(4):
                            ve.bn_stats(stats[:, xb * 32 + 6 * j:xb * 32 + 6 * j + 6],
                                        xnw[:, bu * D + 512 * j:bu * D + 512 * j + 512])
                        ve.drain()
                        ve.bn_aggr(aggr[:, xb * 4:xb * 4 + 2], stats[:, xb * 32:xb * 32 + 24]).then_inc(vesL, 1)
                        ve.wait_ge(ses, t + 1)
                        ve.reciprocal(aggr[:, xb * 4 + 3:xb * 4 + 4], aggr[:, xb * 4 + 2:xb * 4 + 3])
                        ve.drain()
                        if fl >= 1:
                            ve.wait_ge(sto[tt], (16 * per) * fl)
                        ve.tensor_scalar(xht[:, xb * D:(xb + 1) * D], xnw[:, bu * D:(bu + 1) * D],
                                         aggr[:, xb * 4:xb * 4 + 1], aggr[:, xb * 4 + 3:xb * 4 + 4],
                                         op0=ALU.subtract, op1=ALU.mult).then_inc(vesL, 1)

            @b.scalar
            def _(sc):
                if not final_out:
                    for t in range(NT):
                        xb = t % 2
                        sc.wait_ge(vesL, 2 * t + 1)
                        sc.activation(aggr[:, xb * 4 + 2:xb * 4 + 3], aggr[:, xb * 4 + 1:xb * 4 + 2],
                                      AF.Sqrt, bias=eps_ln[:, 0:1], scale=1.0).then_inc(ses, 1)

            @b.gpsimd
            def _(gp):
                for fl in range(NFL):
                    slot = fl % 2
                    for tt in range(2):
                        t = 2 * fl + tt
                        bu = slot * 2 + tt
                        xb = bu % 2
                        gp.wait_ge(vesL, 4 * fl + 2 * (tt + 1))
                        if final_out:
                            gp.dma_start(dd["y"][t * 128:(t + 1) * 128, :],
                                         xnw[:, bu * D:(bu + 1) * D]).then_inc(sto[tt], 16)
                        else:
                            gp.dma_start(dd[xnew_name][t * 128:(t + 1) * 128, :],
                                         xnw[:, bu * D:(bu + 1) * D]).then_inc(sto[tt], 16)
                            gp.dma_start(dd["sc_xh"][t * 128:(t + 1) * 128, :],
                                         xht[:, xb * D:(xb + 1) * D]).then_inc(sto[tt], 16)
                gp.wait_ge(sto[0], (16 * per) * NFL)
                gp.wait_ge(sto[1], (16 * per) * NFL)

    def phase_ffn_gateup(xh_fm, wbuf):
        NHT = DH // 128
        ldg = P.sems(2)
        ldu = P.sems(2)
        sth = P.sems(2)
        pes, ves, ses = P.sem(), P.sem(), P.sem()
        hstage = P.sbuf("hstage", [128, 2 * 1024], BF16)
        sstage = P.sbuf("sstage", [128, 2 * 1024], BF16)
        ngr = max(T // 1024, 1)
        nstr = min(2, max(T // 512, 1))
        with nc.Block() as b:
            @b.gpsimd
            def _(gp):
                for ht in range(NHT):
                    wb = ht % 2
                    if ht >= 2:
                        gp.wait_ge(pes, ngr * (ht - 1))
                    sg = dd["w_gT"][:, ht * 128:(ht + 1) * 128].rearrange("(a p) m -> p a m", p=128)
                    su = dd["w_uT"][:, ht * 128:(ht + 1) * 128].rearrange("(a p) m -> p a m", p=128)
                    gp.dma_start(wbuf[:, wb * 2048:(wb + 1) * 2048].rearrange("p (a m) -> p a m", a=16),
                                 sg).then_inc(ldg[wb], 16)
                    gp.dma_start(wbuf[:, (2 + wb) * 2048:(3 + wb) * 2048].rearrange("p (a m) -> p a m", a=16),
                                 su).then_inc(ldu[wb], 16)

            @b.tensor
            def _(te):
                for ht in range(NHT):
                    wb = ht % 2
                    te.wait_ge(ldg[wb], 16 * (ht // 2 + 1))
                    te.wait_ge(ldu[wb], 16 * (ht // 2 + 1))
                    for g in range(ngr):
                        gi = ht * ngr + g
                        bb = (gi % 2) * 4
                        if gi >= 2:
                            te.wait_ge(ves, gi - 1)
                        for k in range(16):
                            for s_ in range(nstr):
                                col = (2 * g + s_) * 512
                                mm = te.matmul(pb[bb + s_][:, 0:512],
                                               wbuf[:, wb * 2048 + k * 128:wb * 2048 + (k + 1) * 128],
                                               xh_fm[:, k * T + col:k * T + col + 512],
                                               start=(k == 0), stop=(k == 15))
                                mm2 = te.matmul(pb[bb + 2 + s_][:, 0:512],
                                                wbuf[:, (2 + wb) * 2048 + k * 128:(2 + wb) * 2048 + (k + 1) * 128],
                                                xh_fm[:, k * T + col:k * T + col + 512],
                                                start=(k == 0), stop=(k == 15))
                        mm2.then_inc(pes, 1)

            @b.scalar
            def _(sc):
                for ht in range(NHT):
                    for g in range(ngr):
                        gi = ht * ngr + g
                        bb = (gi % 2) * 4
                        sl = gi % 2
                        sc.wait_ge(pes, gi + 1)
                        if gi >= 2:
                            sc.wait_ge(ves, gi - 1)
                        for s_ in range(nstr):
                            sc.activation(sstage[:, sl * 1024 + s_ * 512:sl * 1024 + (s_ + 1) * 512],
                                          pb[bb + s_][:, 0:512], AF.Silu).then_inc(ses, 1)

            @b.vector
            def _(ve):
                for ht in range(NHT):
                    for g in range(ngr):
                        gi = ht * ngr + g
                        bb = (gi % 2) * 4
                        sl = gi % 2
                        ve.wait_ge(ses, nstr * (gi + 1))
                        if gi >= 2:
                            ve.wait_ge(sth[sl], 16 * (gi // 2))
                        for s_ in range(nstr):
                            op = ve.tensor_tensor(hstage[:, sl * 1024 + s_ * 512:sl * 1024 + (s_ + 1) * 512],
                                                  pb[bb + 2 + s_][:, 0:512],
                                                  sstage[:, sl * 1024 + s_ * 512:sl * 1024 + (s_ + 1) * 512],
                                                  op=ALU.mult)
                        op.then_inc(ves, 1)

            @b.sync
            def _(sync):
                for ht in range(NHT):
                    for g in range(ngr):
                        gi = ht * ngr + g
                        sl = gi % 2
                        nt_g = nstr * 4
                        sync.wait_ge(ves, gi + 1)
                        dst = dd["sc_h"][ht * NT + g * nt_g:ht * NT + (g + 1) * nt_g, :, :] \
                            .rearrange("t p c -> p t c")
                        sync.dma_start(dst, hstage[:, sl * 1024:sl * 1024 + nt_g * 128]
                                       .rearrange("p (t c) -> p t c", c=128)).then_inc(sth[sl], 16)
                sync.wait_ge(sth[0], 16 * ((NHT * ngr + 1) // 2))
                sync.wait_ge(sth[1], 16 * ((NHT * ngr) // 2))

    def phase_gdn_qkvz(xh_fm, wbuf, own3_sb):
        NCT = 96
        nst = T // 512
        spq = min(2, nst)
        nqt = max(nst // spq, 1)
        ldw = P.sems(4)
        ldc = P.sem()
        peB, veB = P.sem(), P.sem()
        peA, veA, veD, veM, peC, seC = P.sem(), P.sem(), P.sem(), P.sem(), P.sem(), P.sem()
        stQ = P.sems(2)
        stH = P.sems(2)
        stB = P.sem()
        ba_st = P.sbuf("ba_st", [64, 2 * 512], F32)
        mixp = P.sbuf("mixp", [128, 2 * (T + 3)], BF16)
        dstat = P.sbuf("dstat", [128, 2 * 4 * 128], BF16)
        cvw = P.sbuf("cvw", [128, 256], F32)
        wba_sb = P.sbuf("wba_sb", [128, 1024], BF16)
        ldc2 = P.sem()
        halo3 = P.sbuf("halo3", [128, 2 * 4], BF16)
        evq = P.sbuf("evq", [128, 2 * T], BF16)
        NCV = 64

        with nc.Block() as b:
            @b.gpsimd
            def _(gp):
                gp.dma_start(wba_sb[:, :].rearrange("p (a m) -> p a m", a=16),
                             dd["w_baT"][:, :].rearrange("(a p) m -> p a m", p=128)).then_inc(ldc2, 16)
                for si in range(nst):
                    sl = si % 2
                    gp.wait_ge(veB, si + 1)
                    gp.dma_start(dd["sc_ba"][0:64, si * 512:(si + 1) * 512],
                                 ba_st[0:64, sl * 512:(sl + 1) * 512]).then_inc(stB, 16)
                for ct in range(NCT):
                    wb = ct % 4
                    if ct >= 4:
                        gp.wait_ge(peA, nqt * (ct - 3))
                    srcw = dd["w_qkvzT"][:, ct * 128:(ct + 1) * 128].rearrange("(a p) m -> p a m", p=128)
                    gp.dma_start(wbuf[:, wb * 2048:(wb + 1) * 2048].rearrange("p (a m) -> p a m", a=16),
                                 srcw).then_inc(ldw[wb], 16)
                    hc = ct - 2
                    if 0 <= hc < NCV:
                        gp.wait_ge(veM, hc + 1)
                        gp.dma_start(dd["cc2_in"][hc * 128:(hc + 1) * 128, 0:4],
                                     halo3[:, (hc % 2) * 4:(hc % 2) * 4 + 4]).then_inc(stH[hc % 2], 16)
                for hc in range(NCT - 2, NCV):
                    gp.wait_ge(veM, hc + 1)
                    gp.dma_start(dd["cc2_in"][hc * 128:(hc + 1) * 128, 0:4],
                                 halo3[:, (hc % 2) * 4:(hc % 2) * 4 + 4]).then_inc(stH[hc % 2], 16)
                gp.wait_ge(stH[0], 16 * (NCV // 2))
                gp.wait_ge(stH[1], 16 * (NCV // 2))
                gp.wait_ge(stB, 16 * nst)

            @b.sync
            def _(sync):
                sync.dma_start(cvw[:, :], dd["convw"][:, :]).then_inc(ldc, 16)
                for ct in range(NCT):
                    sl = ct % 2
                    if ct < NCV:
                        sync.wait_ge(seC, nst * (ct + 1))
                    else:
                        sync.wait_ge(veA, nqt * (ct + 1))
                    dn = "sc_qg" if ct < 16 else ("sc_kg" if ct < 32 else ("sc_vg" if ct < NCV else "sc_z"))
                    r0 = (ct % 16) * 128 if ct < 32 else ((ct - 32) * 128 if ct < NCV else (ct - NCV) * 128)
                    sync.dma_start(dd[dn][r0:r0 + 128, :], evq[:, sl * T:sl * T + T]).then_inc(stQ[sl], 16)
                sync.wait_ge(stQ[0], 16 * (NCT // 2))
                sync.wait_ge(stQ[1], 16 * (NCT // 2))

            @b.tensor
            def _(te):
                te.wait_ge(ldc2, 16)
                for si in range(nst):
                    if si >= 1:
                        te.wait_ge(veB, si)
                    for k in range(16):
                        mm = te.matmul(pb[6][0:64, 0:512], wba_sb[:, k * 64:(k + 1) * 64],
                                       xh_fm[:, k * T + si * 512:k * T + si * 512 + 512],
                                       start=(k == 0), stop=(k == 15))
                    mm.then_inc(peB, 1)
                for ct in range(NCT):
                    wb = ct % 4
                    sl = ct % 2
                    te.wait_ge(ldw[wb], 16 * (ct // 4 + 1))
                    rd = min(nqt, 2)
                    for qt in range(nqt):
                        g = nqt * ct + qt
                        if g >= rd:
                            te.wait_ge(veA, g - rd + 1)
                        for k in range(16):
                            for s_ in range(spq):
                                si = spq * qt + s_
                                mm = te.matmul(pb[(qt % 2) * 2 + s_][:, 0:512],
                                               wbuf[:, wb * 2048 + k * 128:wb * 2048 + (k + 1) * 128],
                                               xh_fm[:, k * T + si * 512:k * T + si * 512 + 512],
                                               start=(k == 0), stop=(k == 15))
                        mm.then_inc(peA, 1)
                    if ct < NCV:
                        te.wait_ge(veM, ct + 1)
                        te.wait_ge(veD, ct + 1)
                        for s_ in range(nst):
                            ci = ct * nst + s_
                            if ci >= 4:
                                te.wait_ge(seC, ci - 3)
                            for j in range(4):
                                mm = te.matmul(pb[4 + ci % 4][:, 0:512],
                                               dstat[:, sl * 512 + j * 128:sl * 512 + (j + 1) * 128],
                                               mixp[:, sl * (T + 3) + s_ * 512 + j:sl * (T + 3) + s_ * 512 + j + 512],
                                               start=(j == 0), stop=(j == 3))
                            mm.then_inc(peC, 1)

            @b.vector
            def _(ve):
                ve.memset(halo3[:, :], 0.0)
                for si in range(nst):
                    sl = si % 2
                    ve.wait_ge(peB, si + 1)
                    if si >= 2:
                        ve.wait_ge(stB, 16 * (si - 1))
                    ve.tensor_copy(ba_st[0:64, sl * 512:(sl + 1) * 512], pb[6][0:64, 0:512]).then_inc(veB, 1)
                for ct in range(NCT):
                    sl = ct % 2
                    if ct < NCV:
                        if ct == 0:
                            ve.wait_ge(ldc, 16)
                        if ct >= 2:
                            ve.wait_ge(peC, nst * (ct - 1))
                        for j in range(4):
                            ve.tensor_scalar(dstat[:, sl * 512 + j * 128:sl * 512 + (j + 1) * 128],
                                             cs["c_ident_bf"][:, :], cvw[:, ct * 4 + j:ct * 4 + j + 1],
                                             None, op0=ALU.mult)
                        ve.drain()
                        ve.nop().then_inc(veD, 1)
                    for qt in range(nqt):
                        g = nqt * ct + qt
                        ve.wait_ge(peA, g + 1)
                        if ct < NCV:
                            if ct >= 2 and qt == 0:
                                ve.wait_ge(peC, nst * (ct - 1))
                            for s_ in range(spq):
                                si = spq * qt + s_
                                op = ve.tensor_copy(
                                    mixp[:, sl * (T + 3) + 3 + si * 512:sl * (T + 3) + 3 + (si + 1) * 512],
                                    pb[(qt % 2) * 2 + s_][:, 0:512])
                            op.then_inc(veA, 1)
                        else:
                            if ct >= 2 and qt == 0:
                                ve.wait_ge(stQ[sl], 16 * ((ct - 2) // 2 + 1))
                            for s_ in range(spq):
                                si = spq * qt + s_
                                op = ve.tensor_copy(evq[:, sl * T + si * 512:sl * T + (si + 1) * 512],
                                                    pb[(qt % 2) * 2 + s_][:, 0:512])
                            op.then_inc(veA, 1)
                    if ct < NCV:
                        ve.drain()
                        if ct >= 2:
                            ve.wait_ge(stH[ct % 2], 16 * (ct // 2))
                        ve.memset(mixp[:, sl * (T + 3):sl * (T + 3) + 3], 0.0)
                        ve.tensor_copy(own3_sb[:, ct * 4:ct * 4 + 3],
                                       mixp[:, sl * (T + 3) + 3:sl * (T + 3) + 6])
                        ve.tensor_copy(halo3[:, sl * 4:sl * 4 + 3],
                                       mixp[:, sl * (T + 3) + T:sl * (T + 3) + T + 3])
                        ve.drain()
                        ve.nop().then_inc(veM, 1)

            @b.scalar
            def _(sc):
                for ct in range(NCV):
                    sl = ct % 2
                    for s_ in range(nst):
                        ci = ct * nst + s_
                        sc.wait_ge(peC, ci + 1)
                        if s_ == 0 and ct >= 2:
                            sc.wait_ge(stQ[sl], 16 * ((ct - 2) // 2 + 1))
                        sc.activation(evq[:, sl * T + s_ * 512:sl * T + (s_ + 1) * 512],
                                      pb[4 + ci % 4][:, 0:512], AF.Silu).then_inc(seC, 1)

    def phase_gdn_fixup(own3_sb):
        s = P.sem()
        lda, ves, ses, sto = P.sem(), P.sem(), P.sem(), P.sem()
        halB = P.sbuf("fx_hal", [128, 64 * 3], F32)
        w6B = P.sbuf("fx_w6", [128, 64 * 6], F32)
        accB = P.sbuf("fx_acc", [128, 64 * 3], F32)
        tmpB = P.sbuf("fx_tmp", [128, 64 * 3], F32)
        fxB = P.sbuf("fx_fx", [128, 64 * 3], BF16)
        cvw = P.sbuf("cvw2", [128, 256], F32)
        with nc.Block() as b:
            @b.gpsimd
            def _(gp):
                gp.collective_compute("AllGather", ALU.bypass,
                                      ins=[dd["cc2_in"][:, :].opt()],
                                      outs=[dd["cc2_out"][:, :].opt()],
                                      replica_groups=GROUPS).then_inc(s)
                gp.wait_ge(s, 1)

        with nc.Block() as b2:
            @b2.sync
            def _(sync):
                sync.dma_start(cvw[:, :], dd["convw"][:, :]).then_inc(lda, 16)
                sync.dma_start(halB[:, :].rearrange("p (c w) -> p c w", w=3),
                               dd["cc2_out"][0:8192, 0:3].rearrange("(c p) w -> p c w", p=128)) \
                    .then_inc(lda, 16)

            @b2.vector
            def _(ve):
                h3 = halB[:, :].rearrange("p (c w) -> p c w", w=3)
                w6v = w6B[:, :].rearrange("p (c w) -> p c w", w=6)
                av = accB[:, :].rearrange("p (c w) -> p c w", w=3)
                tv = tmpB[:, :].rearrange("p (c w) -> p c w", w=3)
                cv3 = cvw[:, :].rearrange("p (c j) -> p c j", j=4)
                ve.wait_ge(lda, 32)
                ve.tensor_scalar(w6v[:, :, 0:3], h3, cs["parity"][:, 0:1], None, op0=ALU.mult)
                ve.tensor_copy(w6v[:, :, 3:6],
                               own3_sb[:, :].rearrange("p (c w) -> p c w", w=4)[:, :, 0:3])
                ve.drain()
                ve.tensor_tensor(av, w6v[:, :, 0:3],
                                 cv3[:, :, 0:1].broadcast_to([128, 64, 3]), op=ALU.mult)
                for jj in range(1, 4):
                    ve.tensor_tensor(tv, w6v[:, :, jj:jj + 3],
                                     cv3[:, :, jj:jj + 1].broadcast_to([128, 64, 3]), op=ALU.mult)
                    ve.drain()
                    ve.tensor_tensor(av, av, tv, op=ALU.add)
                    ve.drain()
                ve.nop().then_inc(ves, 1)

            @b2.scalar
            def _(sc):
                sc.wait_ge(ves, 1)
                sc.activation(fxB[:, :], accB[:, :], AF.Silu).then_inc(ses, 1)

            @b2.gpsimd
            def _(gp):
                gp.wait_ge(ses, 1)
                f3 = fxB[:, :].rearrange("p (c w) -> p c w", w=3)
                gp.dma_start(dd["sc_qg"][0:2048, 0:3].rearrange("(c p) w -> p c w", p=128),
                             f3[:, 0:16, :]).then_inc(sto, 16)
                gp.dma_start(dd["sc_kg"][0:2048, 0:3].rearrange("(c p) w -> p c w", p=128),
                             f3[:, 16:32, :]).then_inc(sto, 16)
                gp.dma_start(dd["sc_vg"][0:4096, 0:3].rearrange("(c p) w -> p c w", p=128),
                             f3[:, 32:64, :]).then_inc(sto, 16)
                gp.wait_ge(sto, 48)

    def phase_gdn_prep(cm2, egC_sb, egl):
        TC = min(1024, T)
        NTC = T // TC
        NP8 = TC // 128
        CQ = NCG * 16
        lda, pet, vcp, stg, stc = P.sem(), P.sem(), P.sem(), P.sem(), P.sem()
        ba2 = P.sbuf("g3_ba2", [32, 2 * TC], F32)
        cmstage = P.sbuf("g3_cmstage", [128, 5 * 32], F32)
        tmp = P.sbuf("g3_tmp", [32, 8 * TC], F32)
        onec = P.sbuf("g3_onec", [128, 1], F32)
        negc = P.sbuf("g3_negc", [128, 1], F32)
        eac = P.sbuf("g3_eac", [32, 4], F32)

        def tc_(q):
            return tmp[0:32, q * TC:(q + 1) * TC]

        with nc.Block() as b:
            @b.vector
            def _(ve):
                ve.memset(onec[:, :], 1.0)
                ve.memset(negc[:, :], -1.0)
            @b.scalar
            def _(sc):
                sc.activation(eac[:, 0:1], cs["A_log"][0:32, 0:1], AF.Exp)
                sc.drain()
                sc.mul(eac[:, 1:2], eac[:, 0:1], -1.0)

        for i in range(NTC):
            bv = ba2[0:32, 0:TC]
            av = ba2[0:32, TC:2 * TC]
            with nc.Block() as bL:
                @bL.sync
                def _(sync):
                    sync.dma_start(bv, dd["sc_ba"][0:32, i * TC:(i + 1) * TC]).then_inc(lda, 16)
                    sync.dma_start(av, dd["sc_ba"][32:64, i * TC:(i + 1) * TC]).then_inc(lda, 16)
                    sync.wait_ge(lda, 32 * (i + 1))
            with nc.Block() as bS1:
                @bS1.scalar
                def _(sc):
                    sc.activation(tc_(0), bv, AF.Exp, scale=-1.0)
                    sc.activation(tc_(1), bv, AF.Abs)
                    sc.drain()
                    sc.activation(tc_(1), tc_(1), AF.Exp, scale=-1.0)
                    sc.drain()
                    sc.activation(tc_(1), tc_(1), AF.Ln, bias=onec[0:32, 0:1])
            with nc.Block() as bV1:
                @bV1.vector
                def _(ve):
                    ve.tensor_scalar(tc_(0), tc_(0), 1.0, None, op0=ALU.add)
                    ve.drain()
                    ve.reciprocal(tc_(0), tc_(0))
                    ve.tensor_scalar(tc_(3), bv, negc[0:32, 0:1], 0.0, op0=ALU.mult, op1=ALU.max)
                    ve.drain()
                    ve.tensor_tensor(tc_(1), tc_(3), tc_(1), op=ALU.add)
                    ve.drain()
                    ve.tensor_scalar(tc_(1), tc_(1), negc[0:32, 0:1], None, op0=ALU.mult)
                    ve.tensor_scalar(tc_(4), av, cs["dt_bias"][0:32, 0:1], None, op0=ALU.add)
            with nc.Block() as bS2:
                @bS2.scalar
                def _(sc):
                    sc.activation(tc_(2), tc_(4), AF.Abs)
                    sc.drain()
                    sc.activation(tc_(2), tc_(2), AF.Exp, scale=-1.0)
                    sc.drain()
                    sc.activation(tc_(2), tc_(2), AF.Ln, bias=onec[0:32, 0:1])
            with nc.Block() as bV2:
                @bV2.vector
                def _(ve):
                    ve.tensor_scalar(tc_(7), tc_(4), 0.0, None, op0=ALU.max)
                    ve.drain()
                    ve.tensor_tensor(tc_(2), tc_(7), tc_(2), op=ALU.add)
                    ve.drain()
                    ve.tensor_scalar(tc_(2), tc_(2), eac[0:32, 1:2], None, op0=ALU.mult)
                    ve.drain()
                    for n in range(TC // CK):
                        ve.tensor_tensor_scan(tmp[0:32, 3 * TC + n * CK:3 * TC + (n + 1) * CK],
                                              tmp[0:32, 2 * TC + n * CK:2 * TC + (n + 1) * CK],
                                              tmp[0:32, 2 * TC + n * CK:2 * TC + (n + 1) * CK],
                                              0.0, op0=ALU.add, op1=ALU.bypass)
                    ve.drain()
                    ve.tensor_tensor(tc_(4), tc_(3) if False else tmp[0:32, 3 * TC:4 * TC], tc_(1), op=ALU.add)
                    glv = tmp[0:32, 3 * TC:4 * TC].rearrange("p (n c) -> p n c", c=CK)[:, :, CK - 1:CK] \
                        .broadcast_to([32, TC // CK, CK])
                    gv3 = tmp[0:32, 3 * TC:4 * TC].rearrange("p (n c) -> p n c", c=CK)
                    ve.tensor_tensor(tc_(7).rearrange("p (n c) -> p n c", c=CK), glv, gv3, op=ALU.subtract)
            with nc.Block() as bS3:
                @bS3.scalar
                def _(sc):
                    sc.activation(tc_(5), tc_(3), AF.Exp)
                    sc.activation(tc_(7), tc_(7), AF.Exp)
                    sc.activation(egl[0:32, i * NP8 * 2:(i + 1) * NP8 * 2],
                                  tmp[0:32, 3 * TC + CK - 1:4 * TC:CK], AF.Exp)
            with nc.Block() as bV3:
                @bV3.vector
                def _(ve):
                    ve.tensor_tensor(tc_(6), tc_(0), tc_(5), op=ALU.mult)
            with nc.Block() as bC:
                @bC.tensor
                def _(te):
                    for p8 in range(NP8):
                        for qi, q in enumerate([3, 4, 0, 6, 7]):
                            gi = (i * NP8 + p8) * 5 + qi
                            m5, q5 = divmod(gi - 2, 5)
                            if gi >= 2:
                                te.wait_ge(vcp, m5 * 6 + q5 + 1)
                            te.transpose(pb[gi % 2][:, 0:32],
                                         tmp[0:32, q * TC + p8 * 128:q * TC + (p8 + 1) * 128],
                                         cs["c_ident_f32"][0:32, 0:32]).then_inc(pet, 1)

                @bC.vector
                def _(ve):
                    for p8 in range(NP8):
                        m = i * NP8 + p8
                        for qi in range(5):
                            gi = m * 5 + qi
                            ve.wait_ge(pet, gi + 1)
                            if m >= 1 and qi == 0:
                                ve.wait_ge(stc, 16 * m)
                            op = ve.tensor_copy(cmstage[:, qi * 32:(qi + 1) * 32], pb[gi % 2][:, 0:32])
                            op.then_inc(vcp, 1)
                        ve.drain()
                        ve.nop().then_inc(vcp, 1)

                @bC.gpsimd
                def _(gp):
                    for p8 in range(NP8):
                        m = i * NP8 + p8
                        gp.wait_ge(vcp, (m + 1) * 6)
                        gp.dma_start(dd["sc_cm"][:, m, :, :].rearrange("q p c -> p q c"),
                                     cmstage[:, :].rearrange("p (q c) -> p q c", q=5)).then_inc(stc, 16)
                    gp.wait_ge(stc, 16 * (i + 1) * NP8)

            with nc.Block() as bST:
                @bST.sync
                def _(sync):
                    for ri, q in enumerate([3, 4, 5]):
                        sync.dma_start(dd["sc_gba"][32 * ri:32 * (ri + 1), i * TC:(i + 1) * TC],
                                       tc_(q)).then_inc(stg, 16)
                    sync.wait_ge(stg, 48 * (i + 1))

        rbs = P.sem()
        with nc.Block() as bR:
            @bR.sync
            def _(sync):
                for q in range(5):
                    for a in range(2):
                        srcap = dd["sc_cm"][q, :, :, :].rearrange("m (np c) j -> c m np j", np=2)[:, :, :, 16 * a:16 * a + 16]
                        dstap = cm2[a * 64:(a + 1) * 64, q * CQ:(q + 1) * CQ].rearrange(
                            "c (m np j) -> c m np j", np=2, j=16)
                        sync.dma_start(dstap, srcap).then_inc(rbs, 16)
                sync.wait_ge(rbs, 160)

        with nc.Block() as bN:
            @bN.vector
            def _(ve):
                ve.tensor_scalar(cm2[:, 5 * CQ:6 * CQ], cm2[:, 4 * CQ:5 * CQ], negc[:, 0:1],
                                 None, op0=ALU.mult)

    def phase_gdn_l2norm():
        lda = P.sems(2)
        pes, ves, ses = P.sem(), P.sem(), P.sem()
        sto = P.sems(2)
        hb = P.sbuf("l2_hb", [128, 2 * T], BF16)
        sq = P.sbuf("l2_sq", [128, T], BF16)
        rs = P.sbuf("l2_rs", [128, 2 * T], F32)
        epsq = P.sbuf("l2_epsq", [128, 1], F32)
        nst = T // 512
        with nc.Block() as b:
            @b.vector
            def _(ve):
                ve.memset(epsq[:, :], L2_EPS * DK)
        with nc.Block() as b2:
            @b2.sync
            def _(sync):
                for it in range(2 * HK):
                    sl = it % 2
                    dn = "sc_qg" if it < HK else "sc_kg"
                    r0 = (it % HK) * 128
                    if it >= 2:
                        sync.wait_ge(ves, nst * (it - 1))
                        sync.wait_ge(sto[sl], 16 * (it // 2))
                    sync.dma_start(hb[:, sl * T:(sl + 1) * T], dd[dn][r0:r0 + 128, :]).then_inc(lda[sl], 16)

            @b2.scalar
            def _(sc):
                for it in range(2 * HK):
                    sl = it % 2
                    sc.wait_ge(lda[sl], 16 * (it // 2 + 1))
                    if it >= 2:
                        sc.wait_ge(pes, nst * (it - 1))
                    sc.activation(sq[:, :], hb[:, sl * T:(sl + 1) * T], AF.Square)
                    sc.drain()
                    sc.nop().then_inc(ses, 1)
                    for s_ in range(nst):
                        sc.wait_ge(pes, nst * it + s_ + 1)
                        scl = float(DK) if it < HK else 1.0
                        epsap = epsq[:, 0:1] if it < HK else eps_l2[:, 0:1]
                        sc.activation(rs[:, sl * T + s_ * 512:sl * T + (s_ + 1) * 512],
                                      pb[s_ % 2][:, 0:512], AF.Sqrt, scale=scl, bias=epsap).then_inc(ses, 1)

            @b2.tensor
            def _(te):
                for it in range(2 * HK):
                    te.wait_ge(ses, (nst + 1) * it + 1)
                    for s_ in range(nst):
                        if nst * it + s_ >= 2:
                            te.wait_ge(ses, (nst + 1) * it + s_ - 1 + 1 + 1) if False else None
                        if nst * it + s_ >= 2:
                            te.wait_ge(ves, 0) if False else None
                        gi = nst * it + s_
                        if gi >= 2:
                            prev_it, prev_s = divmod(gi - 2, nst)
                            te.wait_ge(ses, (nst + 1) * prev_it + prev_s + 2)
                        te.matmul(pb[s_ % 2][:, 0:512], cs["c_ones_bf"][:, :],
                                  sq[:, s_ * 512:(s_ + 1) * 512], start=True, stop=True).then_inc(pes, 1)

            @b2.vector
            def _(ve):
                for it in range(2 * HK):
                    sl = it % 2
                    for s_ in range(nst):
                        ve.wait_ge(ses, (nst + 1) * it + s_ + 2)
                        ve.reciprocal(rs[:, sl * T + s_ * 512:sl * T + (s_ + 1) * 512],
                                      rs[:, sl * T + s_ * 512:sl * T + (s_ + 1) * 512])
                        ve.drain()
                        if it >= 2 and s_ == 0:
                            ve.wait_ge(sto[sl], 16 * (it // 2))
                        ve.tensor_tensor(hb[:, sl * T + s_ * 512:sl * T + (s_ + 1) * 512],
                                         hb[:, sl * T + s_ * 512:sl * T + (s_ + 1) * 512],
                                         rs[:, sl * T + s_ * 512:sl * T + (s_ + 1) * 512],
                                         op=ALU.mult)
                        ve.drain()
                    ve.nop().then_inc(ves, nst)

            @b2.gpsimd
            def _(gp):
                for it in range(2 * HK):
                    sl = it % 2
                    dn = "sc_qg" if it < HK else "sc_kg"
                    r0 = (it % HK) * 128
                    gp.wait_ge(ves, nst * (it + 1))
                    gp.dma_start(dd[dn][r0:r0 + 128, :], hb[:, sl * T:(sl + 1) * T]).then_inc(sto[sl], 16)
                gp.wait_ge(sto[0], 16 * HK)
                gp.wait_ge(sto[1], 16 * HK)

    def phase_gdn_egc(egC_sb, egl):
        pes, ves = P.sem(), P.sem()
        with nc.Block() as b:
            @b.tensor
            def _(te):
                for vh in range(HV):
                    if vh >= 2:
                        te.wait_ge(ves, vh - 1)
                    te.matmul(pb[vh % 2][:, 0:NCG], cs["c_sel1"][:, vh * 128:(vh + 1) * 128],
                              egl[0:32, 0:NCG], start=True, stop=True).then_inc(pes, 1)

            @b.vector
            def _(ve):
                for vh in range(HV):
                    ve.wait_ge(pes, vh + 1)
                    ve.tensor_copy(egC_sb[:, vh * NCG:(vh + 1) * NCG], pb[vh % 2][:, 0:NCG]).then_inc(ves, 1)

    def phase_gdn_core(cm2, egC_sb, gba_sb):
        CQ = NCG * 16
        NG8 = NCG // 8
        NU = HK * NG8
        lda, ldg = P.sem(), P.sem()
        peG, peQ, veM, seE = P.sem(), P.sem(), P.sem(), P.sem()
        peT, seT, poB, poKg = P.sem(), P.sem(), P.sem(), P.sem()
        peS, veX, seX, stx = P.sem(), P.sem(), P.sem(), P.sem()
        qn = P.sbuf("g4_qn", [128, T], BF16)
        kn = P.sbuf("g4_kn", [128, T], BF16)
        kng = P.sbuf("g4_kng", [128, T], BF16)
        vga = P.sbuf("g4_vga", [128, T], BF16)
        vgb = P.sbuf("g4_vgb", [128, T], BF16)
        gslc2 = P.sbuf("g4_gslc", [32, 2 * 1536], F32)
        d2 = P.sbuf("g4_d2", [128, 512], F32)
        tq = P.sbuf("g4_tq", [128, 512], F32)
        eE2 = P.sbuf("g4_eE", [128, 2 * 1536], BF16)
        ATs2 = P.sbuf("g4_AT", [128, 2 * 512], BF16)
        NTs2 = P.sbuf("g4_NT", [128, 2 * 512], BF16)
        Ns2 = P.sbuf("g4_N", [128, 2 * 512], BF16)
        qds2 = P.sbuf("g4_qds", [128, 2 * 1024], BF16)
        kTvT = P.sbuf("g4_kTvT", [128, 2 * 2048], BF16)
        kd_all = P.sbuf("g4_kd", [128, 1024], BF16)
        kdn_all = P.sbuf("g4_kdn", [128, 1024], BF16)
        X_all = P.sbuf("g4_X", [128, 2 * 2048], BF16)
        Pn2 = P.sbuf("g4_Pn", [128, 2 * 512], BF16)
        Pt_all = P.sbuf("g4_Pt", [128, 5 * 512], BF16)
        pT6 = pb[6][:, :].bitcast(BF16)
        pT7 = pb[7][:, :].bitcast(BF16)

        def uinfo(u):
            return u // NG8, u % NG8, u % 2

        SCHED = [("Pn", 1), ("Pt", 1), ("A0", 0), ("A1", 0),
                 ("Pn", 2), ("Pt", 2), ("A0", 1), ("A1", 1),
                 ("Pn", 3), ("Pt", 3), ("A0", 2), ("A1", 2),
                 ("Pn", 4), ("Pt", 4), ("A0", 3), ("A1", 3),
                 ("Pt", 5), ("A0", 4), ("A1", 4), ("A0", 5), ("A1", 5)]
        PE_W = [[("sx", 1, 7)], [("sx", 1, 9)], [("vx", 1, 12)], [("vx", 0, 1)],
                [("sx", 0, 2)], [("sx", 0, 2)], [("sx", 0, 2), ("vx", 0, 2)],
                [("vx", 0, 3)], [("sx", 0, 4)], [("sx", 0, 4)], [("vx", 0, 4)],
                [("vx", 0, 5)], [("sx", 0, 6)], [("sx", 0, 6)], [("vx", 0, 6)],
                [("vx", 0, 7)], [("sx", 0, 8)], [("vx", 0, 8)], [("vx", 0, 9)],
                [("sx", 0, 9), ("vx", 0, 10)], [("vx", 0, 11)]]
        DVE_DRAIN = {6, 10, 14, 17, 19}

        def xslots(k):
            return (0, 2048) if k % 2 == 0 else (2048, 0)

        with nc.Block() as b:
            @b.sync
            def _(sync):
                for u in range(NU):
                    j, gg, s = uinfo(u)
                    if gg == 0:
                        if j >= 1:
                            sync.wait_ge(peT, 2 * u)
                            sync.wait_ge(peG, 4 * u)
                            sync.wait_ge(veM, 8 * u)
                        sync.dma_start(qn[:, :], dd["sc_qg"][j * 128:(j + 1) * 128, :]).then_inc(lda, 16)
                        sync.dma_start(kn[:, :], dd["sc_kg"][j * 128:(j + 1) * 128, :]).then_inc(lda, 16)
                        sync.dma_start(vga[:, :], dd["sc_vg"][j * 128:(j + 1) * 128, :]).then_inc(lda, 16)
                        sync.dma_start(vgb[:, :], dd["sc_vg"][(j + 16) * 128:(j + 17) * 128, :]).then_inc(lda, 16)
                    if u >= 2:
                        sync.wait_ge(peG, 4 * (u - 1))
                        sync.wait_ge(peQ, 2 * (u - 1))
                    c0 = gg * 512
                    for ri in range(3):
                        sync.dma_start(gslc2[0:32, s * 1536 + ri * 512:s * 1536 + (ri + 1) * 512],
                                       dd["sc_gba"][32 * ri:32 * ri + 32, c0:c0 + 512]).then_inc(ldg, 16)

            @b.tensor
            def _(te):
                for it in range(NU + 1):
                    if it < NU:
                        u = it
                        j, gg, s = uinfo(u)
                        c0 = gg * 512
                        te.wait_ge(ldg, 48 * (u + 1))
                        if gg == 0:
                            te.wait_ge(lda, 64 * (j + 1))
                            te.wait_ge(poKg, j + 1)
                        if u >= 1:
                            te.wait_ge(veM, 8 * u)
                        gsl = gslc2[0:32, s * 1536:s * 1536 + 1536]
                        te.matmul(pb[2][:, 0:512], cs["c_sel2"][:, j * 128:(j + 1) * 128],
                                  gsl[:, 0:512], start=True, stop=True).then_inc(peG, 1)
                        te.matmul(pb[3][:, 0:512], cs["c_sel2"][:, j * 128:(j + 1) * 128],
                                  gsl[:, 512:1024], start=True, stop=True).then_inc(peG, 1)
                        for n8 in range(8):
                            cc = c0 + n8 * 64
                            for tp in range(2):
                                mm = te.matmul(pb[0][tp * 64:(tp + 1) * 64, n8 * 64:(n8 + 1) * 64],
                                               kn[:, cc:cc + 64], kng[:, cc:cc + 64],
                                               start=True, stop=True, tile_position=(0, tp * 64))
                        mm.then_inc(peG, 1)
                        for n8 in range(8):
                            cc = c0 + n8 * 64
                            for tp in range(2):
                                mm = te.matmul(pb[1][tp * 64:(tp + 1) * 64, n8 * 64:(n8 + 1) * 64],
                                               qn[:, cc:cc + 64], qn[:, cc:cc + 64],
                                               start=True, stop=True, tile_position=(0, tp * 64))
                        mm.then_inc(peG, 1)
                    if it >= 1:
                        u = it - 1
                        j, gg, s = uinfo(u)
                        for gidx, (kind, lvl) in enumerate(SCHED):
                            if gidx == 0:
                                te.wait_ge(veM, 8 * u + 6)
                            for (sk, back, off) in PE_W[gidx]:
                                rate = 9 if sk == "sx" else 12
                                tgt = rate * (u - back) + off
                                if tgt > 0:
                                    te.wait_ge(seX if sk == "sx" else veX, tgt)
                            if kind == "Pn" or kind == "Pt":
                                dstb = pb[4] if kind == "Pn" else pb[5]
                                for n8 in range(8):
                                    for a in range(2):
                                        if lvl == 1:
                                            lT = NTs2[a * 64:(a + 1) * 64, s * 512 + n8 * 64:s * 512 + (n8 + 1) * 64]
                                            nN = Ns2[a * 64:(a + 1) * 64, s * 512 + n8 * 64:s * 512 + (n8 + 1) * 64]
                                        else:
                                            lT = Pt_all[a * 64:(a + 1) * 64,
                                                        (lvl - 2) * 512 + n8 * 64:(lvl - 2) * 512 + (n8 + 1) * 64]
                                            ps = (lvl - 2) % 2
                                            nN = Pn2[a * 64:(a + 1) * 64, ps * 512 + n8 * 64:ps * 512 + (n8 + 1) * 64]
                                        l_, r_ = (lT, nN) if kind == "Pn" else (nN, lT)
                                        mm = te.matmul(dstb[a * 64:(a + 1) * 64, n8 * 64:(n8 + 1) * 64],
                                                       l_, r_, start=True, stop=True,
                                                       tile_position=(a * 64, a * 64))
                                mm.then_inc(peS, 1)
                            else:
                                k = lvl
                                wv = 0 if kind == "A0" else 1
                                if gidx == 2:
                                    te.wait_ge(poB, 2 * u + 2)
                                    te.wait_ge(seT, 2 * u + 2)
                                srcb, _dst = xslots(k)
                                for n8 in range(4 * wv, 4 * wv + 4):
                                    bank = pb[6] if (n8 % 4) < 2 else pb[7]
                                    colb = (n8 % 2) * 256
                                    for a in range(2):
                                        if k == 0:
                                            lT = NTs2[a * 64:(a + 1) * 64, s * 512 + n8 * 64:s * 512 + (n8 + 1) * 64]
                                        else:
                                            lT = Pt_all[a * 64:(a + 1) * 64,
                                                        (k - 1) * 512 + n8 * 64:(k - 1) * 512 + (n8 + 1) * 64]
                                        mm = te.matmul(bank[a * 64:(a + 1) * 64, colb:colb + 256],
                                                       lT,
                                                       X_all[a * 64:(a + 1) * 64, srcb + n8 * 256:srcb + (n8 + 1) * 256],
                                                       start=True, stop=True,
                                                       tile_position=(a * 64, a * 64))
                                mm.then_inc(peS, 1)
                    if it < NU:
                        u = it
                        j, gg, s = uinfo(u)
                        gsl = gslc2[0:32, s * 1536:s * 1536 + 1536]
                        te.wait_ge(veM, 8 * u + 3)
                        te.matmul(pb[3][:, 0:512], cs["c_sel1"][:, j * 128:(j + 1) * 128],
                                  gsl[:, 1024:1536], start=True, stop=True).then_inc(peQ, 1)
                        te.wait_ge(veM, 8 * u + 5)
                        te.matmul(pb[2][:, 0:512], cs["c_sel1"][:, (j + 16) * 128:(j + 17) * 128],
                                  gsl[:, 1024:1536], start=True, stop=True).then_inc(peQ, 1)
                    if it < NU:
                        u = it
                        j, gg, s = uinfo(u)
                        c0 = gg * 512
                        if u >= 1:
                            te.wait_ge(veX, 12 * (u - 1) + 12)
                        for n8 in range(8):
                            cc = c0 + n8 * 64
                            pbT = pT6 if n8 < 4 else pT7
                            cb = (n8 % 4) * 256
                            te.transpose(pbT[0:64, cb:cb + 128], kn[:, cc:cc + 64],
                                         cs["c_ident_bf"][:, :], tile_position=(0, 0))
                            te.transpose(pbT[64:128, cb:cb + 128], kn[:, cc:cc + 64],
                                         cs["c_ident_bf"][:, :], tile_position=(0, 64))
                            te.transpose(pbT[0:64, cb + 128:cb + 256], vga[:, cc:cc + 64],
                                         cs["c_ident_bf"][:, :], tile_position=(0, 0))
                            tt = te.transpose(pbT[64:128, cb + 128:cb + 256], vgb[:, cc:cc + 64],
                                              cs["c_ident_bf"][:, :], tile_position=(0, 64))
                            if n8 == 3 or n8 == 7:
                                tt.then_inc(peT, 1)

            @b.vector
            def _(ve):
                for it in range(NU + 1):
                    if it < NU:
                        u = it
                        j, gg, s = uinfo(u)
                        c0 = gg * 512
                        g3d = cm2[:, 0 * CQ + (gg * 8) * 16 + j:0 * CQ + (gg * 8 + 8) * 16 + j:16] \
                            .unsqueeze(2).broadcast_to([128, 8, 64])
                        glb3d = cm2[:, 1 * CQ + (gg * 8) * 16 + j:1 * CQ + (gg * 8 + 8) * 16 + j:16] \
                            .unsqueeze(2).broadcast_to([128, 8, 64])
                        d2v = d2[:, :].rearrange("p (n s) -> p n s", n=8)
                        eEs = eE2[:, s * 1536:(s + 1) * 1536]
                        ve.wait_ge(peG, 4 * u + 1)
                        if u >= 1:
                            ve.wait_ge(seE, 3 * u)
                        ve.tensor_tensor(d2v, pb[2][:, 0:512].rearrange("p (n s) -> p n s", n=8),
                                         g3d, op=ALU.subtract)
                        ve.drain()
                        ve.tensor_tensor(tq[:, :], d2[:, :], cs["c_ubias_i"][:, :], op=ALU.add).then_inc(veM, 1)
                        ve.wait_ge(seE, 3 * u + 1)
                        ve.wait_ge(peG, 4 * u + 4)
                        if u >= 2:
                            ve.wait_ge(stx, 96 * (u - 1))
                        ve.tensor_tensor(ATs2[:, s * 512:(s + 1) * 512], eEs[:, 0:512],
                                         pb[1][:, 0:512], op=ALU.mult).then_inc(veM, 1)
                        ve.tensor_tensor(d2v, pb[3][:, 0:512].rearrange("p (n s) -> p n s", n=8),
                                         g3d, op=ALU.subtract)
                        ve.drain()
                        ve.tensor_tensor(tq[:, :], d2[:, :], cs["c_ubias_s"][:, :], op=ALU.add).then_inc(veM, 1)
                        ve.wait_ge(seE, 3 * u + 2)
                        if u >= 1:
                            ve.wait_ge(peS, 21 * (u - 1))
                        ve.tensor_tensor(NTs2[:, s * 512:(s + 1) * 512], eEs[:, 512:1024],
                                         pb[0][:, 0:512], op=ALU.mult).then_inc(veM, 1)
                        ve.tensor_tensor(d2v, glb3d, pb[2][:, 0:512].rearrange("p (n s) -> p n s", n=8),
                                         op=ALU.subtract)
                        ve.drain()
                        ve.tensor_tensor(tq[:, :], d2[:, :], cs["c_lbias_s"][:, :], op=ALU.add).then_inc(veM, 1)
                        ve.wait_ge(seE, 3 * u + 3)
                        ve.tensor_tensor(Ns2[:, s * 512:(s + 1) * 512], eEs[:, 1024:1536],
                                         pb[0][:, 0:512], op=ALU.mult).then_inc(veM, 1)
                    if it >= 1:
                        u = it - 1
                        j, gg, s = uinfo(u)
                        for gidx, (kind, lvl) in enumerate(SCHED):
                            if kind in ("Pn", "Pt"):
                                continue
                            ve.wait_ge(peS, 21 * u + gidx + 1)
                            k = lvl
                            wv = 0 if kind == "A0" else 1
                            if gidx == 2:
                                ve.wait_ge(poB, 2 * u + 2)
                            if gidx in DVE_DRAIN:
                                ve.drain()
                            srcb, dstb = xslots(k)
                            cA = 4 * wv * 256
                            ve.tensor_tensor(X_all[:, dstb + cA:dstb + cA + 512],
                                             pb[6][:, 0:512],
                                             X_all[:, srcb + cA:srcb + cA + 512], op=ALU.add)
                            ve.tensor_tensor(X_all[:, dstb + cA + 512:dstb + cA + 1024],
                                             pb[7][:, 0:512],
                                             X_all[:, srcb + cA + 512:srcb + cA + 1024],
                                             op=ALU.add).then_inc(veX, 1)
                    if it < NU:
                        u = it
                        j, gg, s = uinfo(u)
                        c0 = gg * 512
                        ve.wait_ge(peQ, 2 * u + 1)
                        ve.tensor_tensor(qds2[:, s * 1024:s * 1024 + 512], qn[:, c0:c0 + 512],
                                         pb[3][:, 0:512], op=ALU.mult).then_inc(veM, 1)
                        ve.wait_ge(peQ, 2 * u + 2)
                        ve.tensor_tensor(qds2[:, s * 1024 + 512:s * 1024 + 1024], qn[:, c0:c0 + 512],
                                         pb[2][:, 0:512], op=ALU.mult).then_inc(veM, 1)
                    if it < NU:
                        u = it
                        j, gg, s = uinfo(u)
                        ve.wait_ge(seT, 2 * u + 2)
                        if u >= 1:
                            ve.wait_ge(stx, 96 * u)
                        for n8 in range(8):
                            col = (gg * 8 + n8) * 16 + j
                            kv = kTvT[:, s * 2048 + n8 * 256:s * 2048 + n8 * 256 + 128]
                            vv = kTvT[:, s * 2048 + n8 * 256 + 128:s * 2048 + (n8 + 1) * 256]
                            ve.tensor_scalar(kd_all[:, n8 * 128:(n8 + 1) * 128], kv,
                                             cm2[:, 4 * CQ + col:4 * CQ + col + 1], None, op0=ALU.mult)
                            op = ve.tensor_scalar(kdn_all[:, n8 * 128:(n8 + 1) * 128], kv,
                                                  cm2[:, 5 * CQ + col:5 * CQ + col + 1], None, op0=ALU.mult)
                            if n8 == 7:
                                op.then_inc(poB, 1)
                        for n8 in range(8):
                            col = (gg * 8 + n8) * 16 + j
                            kv = kTvT[:, s * 2048 + n8 * 256:s * 2048 + n8 * 256 + 128]
                            vv = kTvT[:, s * 2048 + n8 * 256 + 128:s * 2048 + (n8 + 1) * 256]
                            ve.tensor_scalar(X_all[:, n8 * 256:n8 * 256 + 128], vv,
                                             cm2[:, 2 * CQ + col:2 * CQ + col + 1], None, op0=ALU.mult)
                            op = ve.tensor_scalar(X_all[:, n8 * 256 + 128:(n8 + 1) * 256], kv,
                                                  cm2[:, 3 * CQ + col:3 * CQ + col + 1], None, op0=ALU.mult)
                            if n8 == 7:
                                op.then_inc(poB, 1)

            @b.scalar
            def _(sc):
                for u in range(NU):
                    j, gg, s = uinfo(u)
                    eEs = eE2[:, s * 1536:(s + 1) * 1536]
                    for e in range(3):
                        sc.wait_ge(veM, 8 * u + 2 * e + 1)
                        sc.activation(eEs[:, e * 512:(e + 1) * 512], tq[:, :], AF.Exp).then_inc(seE, 1)
                    if u >= 1:
                        up = u - 1
                        for gidx, (kind, lvl) in enumerate(SCHED):
                            if kind == "Pn":
                                sc.wait_ge(peS, 21 * up + gidx + 1)
                                ps = (lvl - 1) % 2
                                sc.copy(Pn2[:, ps * 512:(ps + 1) * 512],
                                        pb[4][:, 0:512]).then_inc(seX, 1)
                            elif kind == "Pt":
                                sc.wait_ge(peS, 21 * up + gidx + 1)
                                sc.copy(Pt_all[:, (lvl - 1) * 512:lvl * 512],
                                        pb[5][:, 0:512]).then_inc(seX, 1)
                    sc.wait_ge(peT, 2 * u + 1)
                    if u >= 2:
                        sc.wait_ge(poB, 2 * (u - 1))
                    sc.copy(kTvT[:, s * 2048:s * 2048 + 1024], pT6[:, 0:1024]).then_inc(seT, 1)
                    sc.wait_ge(peT, 2 * u + 2)
                    sc.copy(kTvT[:, s * 2048 + 1024:s * 2048 + 2048], pT7[:, 0:1024]).then_inc(seT, 1)
                up = NU - 1
                for gidx, (kind, lvl) in enumerate(SCHED):
                    if kind == "Pn":
                        sc.wait_ge(peS, 21 * up + gidx + 1)
                        ps = (lvl - 1) % 2
                        sc.copy(Pn2[:, ps * 512:(ps + 1) * 512],
                                pb[4][:, 0:512]).then_inc(seX, 1)
                    elif kind == "Pt":
                        sc.wait_ge(peS, 21 * up + gidx + 1)
                        sc.copy(Pt_all[:, (lvl - 1) * 512:lvl * 512],
                                pb[5][:, 0:512]).then_inc(seX, 1)

            @b.gpsimd
            def _(gp):
                for it in range(NU + 1):
                    if it < NU and it % NG8 == 0:
                        j = it // NG8
                        gp.wait_ge(lda, 64 * (j + 1))
                        if j >= 1:
                            gp.wait_ge(peG, 4 * it)
                        gp.tensor_scalar(kng[:, :], kn[:, :], -1.0, None, op0=ALU.mult).then_inc(poKg, 1)
                    if it >= 1:
                        u = it - 1
                        j, gg, s = uinfo(u)
                        gp.wait_ge(veX, 12 * u + 12)
                        gp.wait_ge(veM, 8 * u + 8)
                        r0 = gg * 128 + j
                        gp.dma_start(dd["sc_kd2"][r0:r0 + 113:16, :, :].rearrange("n p c -> p n c"),
                                     kd_all[:, :].rearrange("p (n c) -> p n c", c=128)).then_inc(stx, 16)
                        gp.dma_start(dd["sc_kdn"][r0:r0 + 113:16, :, :].rearrange("n p c -> p n c"),
                                     kdn_all[:, :].rearrange("p (n c) -> p n c", c=128)).then_inc(stx, 16)
                        gp.dma_start(dd["sc_X"][r0:r0 + 113:16, :, :].rearrange("n p c -> p n c"),
                                     X_all[:, 0:2048].rearrange("p (n c) -> p n c", c=256)).then_inc(stx, 16)
                        gp.dma_start(dd["sc_AT"][gg * 16 + j, :, :],
                                     ATs2[:, s * 512:(s + 1) * 512]).then_inc(stx, 16)
                        gp.dma_start(dd["sc_qdT"][j * NG8 + gg, :, :],
                                     qds2[:, s * 1024:s * 1024 + 512]).then_inc(stx, 16)
                        gp.dma_start(dd["sc_qdT"][(j + 16) * NG8 + gg, :, :],
                                     qds2[:, s * 1024 + 512:s * 1024 + 1024]).then_inc(stx, 16)
                gp.wait_ge(stx, 96 * NU)

    def phase_gdn_scan(S32, Sbf, egC_sb, pass2):
        NG8 = NCG // 8
        NSL = T // 256
        L2 = 2
        lda = [P.sems(2) for _ in range(L2)]
        ldz = P.sems(L2)
        ldS = P.sems(L2)
        peW = P.sems(L2)
        veW = P.sems(L2)
        peP = P.sems(L2)
        veS = P.sems(L2)
        veS2 = P.sems(L2)
        peA = P.sems(L2)
        veQm = P.sems(L2)
        peO = P.sems(L2)
        veO = P.sems(L2)
        seZ = P.sems(L2)
        veN = P.sems(L2)
        sqd = P.sems(L2)
        peR = P.sems(L2)
        seR = P.sems(L2)
        veF = P.sems(L2)
        veR8 = P.sems(L2)
        stS = P.sems(L2)
        stG = P.sems(L2)
        Xs = [P.sbuf(f"g5_X{L}", [128, 2 * 256], BF16) for L in range(L2)]
        kds = [P.sbuf(f"g5_kd{L}", [128, 2 * 128], BF16) for L in range(L2)]
        kdns = [P.sbuf(f"g5_kdn{L}", [128, 2 * 128], BF16) for L in range(L2)]
        wmt = [P.sbuf(f"g5_wmt{L}", [128, 2 * 128], BF16) for L in range(L2)]
        Sini = [P.sbuf(f"g5_Sini{L}", [128, 128], F32) for L in range(L2)]
        if pass2:
            ATs = [P.sbuf(f"g5_ATs{L}", [128, 2 * 512], BF16) for L in range(L2)]
            qdTs = [P.sbuf(f"g5_qdT{L}", [128, 4 * 512], BF16) for L in range(L2)]
            qm = [P.sbuf(f"g5_qm{L}", [128, 2 * 64], BF16) for L in range(L2)]
            zsl = [P.sbuf(f"g5_z{L}", [128, 2 * T], BF16) for L in range(L2)]
            xg = [P.sbuf(f"g5_xg{L}", [128, 2 * T], BF16) for L in range(L2)]
            rs5 = [P.sbuf(f"g5_rs{L}", [128, 2 * 256], F32) for L in range(L2)]
            sqz = [P.sbuf(f"g5_sq{L}", [128, T], BF16) for L in range(L2)]

        def ld_counts():
            cnt = [0, 0]
            table = []
            for n in range(NCG):
                sl = n % 2
                cnt[sl] += 3
                if pass2 and n % 8 == 0:
                    cnt[sl] += 3
                table.append((sl, cnt[sl]))
            return table, cnt
        LDT, LDJ = ld_counts()

        with nc.Block() as bP:
            @bP.sync
            def _(sync):
                for jp in range(8):
                    for n in range(NCG):
                        for L in range(L2):
                            j = 2 * jp + L
                            sl, cn = LDT[n]
                            if pass2 and n == 0:
                                for a in range(2):
                                    vh = j + 16 * a
                                    if 2 * jp + a > 0:
                                        sync.wait_ge(veN[L], 2 * jp + a)
                                    sync.dma_start(Sini[L][:, :],
                                                   dd["cc3_out"][vh * 128:(vh + 1) * 128, :]).then_inc(ldS[L], 16)
                            if n >= 2 or jp > 0:
                                tgt = max(0, 2 * NCG * jp + 2 * (n - 1))
                                sync.wait_ge(veS[L], tgt)
                                sync.wait_ge(peW[L], tgt)
                                sync.wait_ge(peP[L], tgt)
                                if pass2:
                                    sync.wait_ge(peA[L], tgt)
                                    sync.wait_ge(peO[L], tgt)
                                    sync.wait_ge(veQm[L], tgt)
                            sync.dma_start(Xs[L][:, sl * 256:(sl + 1) * 256],
                                           dd["sc_X"][n * 16 + j, :, :]).then_inc(lda[L][sl], 16)
                            sync.dma_start(kds[L][:, sl * 128:(sl + 1) * 128],
                                           dd["sc_kd2"][n * 16 + j, :, :]).then_inc(lda[L][sl], 16)
                            sync.dma_start(kdns[L][:, sl * 128:(sl + 1) * 128],
                                           dd["sc_kdn"][n * 16 + j, :, :]).then_inc(lda[L][sl], 16)
                            if pass2 and n % 8 == 0:
                                sync.dma_start(ATs[L][:, (n // 8 % 2) * 512:(n // 8 % 2 + 1) * 512],
                                               dd["sc_AT"][(n // 8) * 16 + j, :, :]).then_inc(lda[L][sl], 16)
                                for a in range(2):
                                    vh = j + 16 * a
                                    sync.dma_start(qdTs[L][:, (2 * (n // 8 % 2) + a) * 512:
                                                        (2 * (n // 8 % 2) + a + 1) * 512],
                                                   dd["sc_qdT"][vh * NG8 + n // 8, :, :]).then_inc(lda[L][sl], 16)

            if pass2:
                @bP.scalar
                def _(sc):
                    for jp in range(8):
                        for L in range(L2):
                            j = 2 * jp + L
                            for a in range(2):
                                vh = j + 16 * a
                                if jp > 0:
                                    sc.wait_ge(veF[L], 2 * (jp - 1) + a + 1)
                                sc.dma_start(zsl[L][:, a * T:(a + 1) * T],
                                             dd["sc_z"][vh * 128:(vh + 1) * 128, :]).then_inc(ldz[L], 16)
                            sc.wait_ge(ldz[L], 32 * (jp + 1))
                            for a in range(2):
                                sc.activation(zsl[L][:, a * T:(a + 1) * T], zsl[L][:, a * T:(a + 1) * T],
                                              AF.Silu).then_inc(seZ[L], 1)
                        for n in range(NCG):
                            for L in range(L2):
                                for a in range(2):
                                    i2 = (jp * NCG + n) * 2 + a
                                    sc.wait_ge(peW[L], i2 + 1)
                                    sc.copy(wmt[L][:, a * 128:(a + 1) * 128],
                                            pb[a][:, L * 128:(L + 1) * 128]).then_inc(veW[L], 1)
                        for L in range(L2):
                            j = 2 * jp + L
                            for a in range(2):
                                vh = j + 16 * a
                                sc.wait_ge(veO[L], 2 * NCG * jp + 2 * (NCG - 1) + a + 1)
                                if 2 * jp + a > 0:
                                    sc.wait_ge(peR[L], NSL * (2 * jp + a))
                                sc.activation(sqz[L][:, :], xg[L][:, a * T:(a + 1) * T], AF.Square)
                                sc.drain()
                                sc.nop().then_inc(sqd[L], 1)
                                for s_ in range(NSL):
                                    si8 = (2 * jp + a) * NSL + s_
                                    sc.wait_ge(peR[L], si8 + 1)
                                    if si8 >= 2:
                                        sc.wait_ge(veR8[L], si8 - 1)
                                    sc.activation(rs5[L][:, (si8 % 2) * 256:(si8 % 2) * 256 + 256],
                                                  pb[2 * (si8 % 2) + L][:, 256:512],
                                                  AF.Ln, scale=1.0 / 128.0,
                                                  bias=eps_rms[:, 0:1])
                                    sc.drain()
                                    sc.activation(rs5[L][:, (si8 % 2) * 256:(si8 % 2) * 256 + 256],
                                                  rs5[L][:, (si8 % 2) * 256:(si8 % 2) * 256 + 256],
                                                  AF.Exp, scale=-0.5).then_inc(seR[L], 1)
            else:
                @bP.scalar
                def _(sc):
                    for jp in range(8):
                        for n in range(NCG):
                            for L in range(L2):
                                for a in range(2):
                                    i2 = (jp * NCG + n) * 2 + a
                                    sc.wait_ge(peW[L], i2 + 1)
                                    sc.copy(wmt[L][:, a * 128:(a + 1) * 128],
                                            pb[a][:, L * 128:(L + 1) * 128]).then_inc(veW[L], 1)

            @bP.vector
            def _(ve):
                for jp in range(8):
                    for L in range(L2):
                        j = 2 * jp + L
                        for a in range(2):
                            vh = j + 16 * a
                            if pass2:
                                ve.wait_ge(ldS[L], 16 * (2 * jp + a + 1))
                                ve.tensor_scalar(S32[:, vh * 128:(vh + 1) * 128], Sini[L][:, :],
                                                 cs["parity"][:, 0:1], None, op0=ALU.mult)
                            else:
                                ve.memset(S32[:, vh * 128:(vh + 1) * 128], 0.0)
                            ve.drain()
                            ve.tensor_copy(Sbf[:, vh * 128:(vh + 1) * 128],
                                           S32[:, vh * 128:(vh + 1) * 128]).then_inc(veN[L], 1)
                    for n in range(NCG):
                        for L in range(L2):
                            j = 2 * jp + L
                            sl = n % 2
                            for a in range(2):
                                vh = j + 16 * a
                                i2 = (jp * NCG + n) * 2 + a
                                if pass2:
                                    ve.wait_ge(peA[L], i2 + 1)
                                    ve.tensor_tensor(qm[L][:, a * 64:(a + 1) * 64],
                                                     qdTs[L][:, (2 * (n // 8 % 2) + a) * 512 + (n % 8) * 64:
                                                          (2 * (n // 8 % 2) + a) * 512 + (n % 8 + 1) * 64],
                                                     pb[4 + a][:, L * 64:(L + 1) * 64],
                                                     op=ALU.subtract).then_inc(veQm[L], 1)
                                ve.wait_ge(peP[L], i2 + 1)
                                if pass2:
                                    ve.wait_ge(peO[L], i2 + 1)
                                ve.drain()
                                ve.scalar_tensor_tensor(Sbf[:, vh * 128:(vh + 1) * 128],
                                                        S32[:, vh * 128:(vh + 1) * 128],
                                                        egC_sb[:, vh * NCG + n:vh * NCG + n + 1],
                                                        pb[2 + a][:, L * 128:(L + 1) * 128],
                                                        op0=ALU.mult, op1=ALU.subtract).then_inc(veS2[L], 1)
                                ve.scalar_tensor_tensor(S32[:, vh * 128:(vh + 1) * 128],
                                                        S32[:, vh * 128:(vh + 1) * 128],
                                                        egC_sb[:, vh * NCG + n:vh * NCG + n + 1],
                                                        pb[2 + a][:, L * 128:(L + 1) * 128],
                                                        op0=ALU.mult, op1=ALU.subtract).then_inc(veS[L], 1)
                                if pass2:
                                    ve.wait_ge(seZ[L], 2 * jp + a + 1)
                                    if n == 0 and a == 0 and jp > 0:
                                        ve.wait_ge(stG[L], 32 * jp)
                                    ve.tensor_tensor(xg[L][:, a * T + n * 64:a * T + (n + 1) * 64],
                                                     pb[6 + a][:, L * 64:(L + 1) * 64],
                                                     zsl[L][:, a * T + n * 64:a * T + (n + 1) * 64],
                                                     op=ALU.mult).then_inc(veO[L], 1)
                    if pass2:
                        for L in range(L2):
                            for a in range(2):
                                base = (2 * jp + a) * NSL
                                for s_ in range(NSL):
                                    si8 = base + s_
                                    sslot = (si8 % 2) * 256
                                    ve.wait_ge(seR[L], si8 + 1)
                                    ve.drain()
                                    ve.scalar_tensor_tensor(xg[L][:, a * T + s_ * 256:a * T + (s_ + 1) * 256],
                                                            xg[L][:, a * T + s_ * 256:a * T + (s_ + 1) * 256],
                                                            cs["normw"][:, 0:1],
                                                            rs5[L][:, sslot:sslot + 256],
                                                            op0=ALU.mult, op1=ALU.mult)
                                    ve.drain()
                                    ve.nop().then_inc(veR8[L], 1)
                                ve.nop().then_inc(veF[L], 1)

            @bP.tensor
            def _(te):
                for jp in range(8):
                    for n in range(NCG):
                        for L in range(L2):
                            j = 2 * jp + L
                            sl, cn = LDT[n]
                            for a in range(2):
                                vh = j + 16 * a
                                i2 = (jp * NCG + n) * 2 + a
                                if a == 0:
                                    te.wait_ge(lda[L][sl], 16 * (jp * LDJ[sl] + cn))
                                if i2 >= 2:
                                    te.wait_ge(veW[L], i2 - 1)
                                te.matmul(pb[a][:, L * 128:(L + 1) * 128],
                                          Xs[L][a * 64:a * 64 + 64, sl * 256 + 128:sl * 256 + 256],
                                          kds[L][a * 64:a * 64 + 64, sl * 128:(sl + 1) * 128],
                                          start=True, stop=True, tile_position=(a * 64, 0)).then_inc(peW[L], 1)
                                if i2 >= 2:
                                    te.wait_ge(veS[L], i2 - 1)
                                te.wait_ge(veN[L], 2 * jp + a + 1)
                                te.matmul(pb[2 + a][:, L * 128:(L + 1) * 128],
                                          kdns[L][a * 64:a * 64 + 64, sl * 128:(sl + 1) * 128],
                                          Xs[L][a * 64:a * 64 + 64, sl * 256:sl * 256 + 128],
                                          start=True, stop=False, tile_position=(a * 64, 0))
                                te.wait_ge(veW[L], i2 + 1)
                                if i2 >= 2:
                                    te.wait_ge(veS2[L], i2 - 1)
                                te.matmul(pb[2 + a][:, L * 128:(L + 1) * 128], wmt[L][:, a * 128:(a + 1) * 128],
                                          Sbf[:, vh * 128:(vh + 1) * 128],
                                          start=False, stop=True, tile_position=(0, 0)).then_inc(peP[L], 1)
                                if pass2:
                                    if i2 >= 2:
                                        te.wait_ge(veQm[L], i2 - 1)
                                    te.matmul(pb[4 + a][:, L * 64:(L + 1) * 64],
                                              Xs[L][a * 64:a * 64 + 64, sl * 256 + 128:sl * 256 + 256],
                                              ATs[L][a * 64:a * 64 + 64, (n // 8 % 2) * 512 + (n % 8) * 64:
                                                  (n // 8 % 2) * 512 + (n % 8 + 1) * 64],
                                              start=True, stop=True, tile_position=(a * 64, 0)).then_inc(peA[L], 1)
                                    if i2 >= 2:
                                        te.wait_ge(veO[L], i2 - 1)
                                    te.matmul(pb[6 + a][:, L * 64:(L + 1) * 64],
                                              Xs[L][a * 64:a * 64 + 64, sl * 256:sl * 256 + 128],
                                              ATs[L][a * 64:a * 64 + 64, (n // 8 % 2) * 512 + (n % 8) * 64:
                                                  (n // 8 % 2) * 512 + (n % 8 + 1) * 64],
                                              start=True, stop=False, tile_position=(a * 64, 0))
                                    te.wait_ge(veQm[L], i2 + 1)
                                    te.matmul(pb[6 + a][:, L * 64:(L + 1) * 64],
                                              Sbf[:, vh * 128:(vh + 1) * 128],
                                              qm[L][:, a * 64:(a + 1) * 64],
                                              start=False, stop=True, tile_position=(0, 0)).then_inc(peO[L], 1)
                    if pass2:
                        for L in range(L2):
                            for a in range(2):
                                te.wait_ge(sqd[L], 2 * jp + a + 1)
                                for s_ in range(NSL):
                                    si8 = (2 * jp + a) * NSL + s_
                                    if si8 >= 2:
                                        te.wait_ge(seR[L], si8 - 1)
                                    te.matmul(pb[2 * (si8 % 2) + L][:, 256:512], cs["c_ones_bf"][:, :],
                                              sqz[L][:, s_ * 256:(s_ + 1) * 256],
                                              start=True, stop=True).then_inc(peR[L], 1)

            @bP.gpsimd
            def _(gp):
                for jp in range(8):
                    for L in range(L2):
                        j = 2 * jp + L
                        if not pass2:
                            for a in range(2):
                                vh = j + 16 * a
                                gp.wait_ge(veS[L], 2 * NCG * jp + 2 * (NCG - 1) + a + 1)
                                gp.dma_start(dd["cc3_in"][vh * 128:(vh + 1) * 128, :],
                                             S32[:, vh * 128:(vh + 1) * 128]).then_inc(stS[L], 16)
                        else:
                            for a in range(2):
                                vh = j + 16 * a
                                gp.wait_ge(veF[L], 2 * jp + a + 1)
                                gp.dma_start(dd["sc_xgn"][vh * 128:(vh + 1) * 128, :],
                                             xg[L][:, a * T:(a + 1) * T]).then_inc(stG[L], 16)
                for L in range(L2):
                    if not pass2:
                        gp.wait_ge(stS[L], 32 * 8)
                    else:
                        gp.wait_ge(stG[L], 32 * 8)

    def phase_gdn_xchg3():
        s = P.sem()
        with nc.Block() as b:
            @b.gpsimd
            def _(gp):
                gp.collective_compute("AllGather", ALU.bypass,
                                      ins=[dd["cc3_in"][:, :].opt()],
                                      outs=[dd["cc3_out"][:, :].opt()],
                                      replica_groups=GROUPS).then_inc(s)
                gp.wait_ge(s, 1)

    def seg0():
        phase_consts()
    def seg1():
        phase_ln_standalone("x0")
    def seg2():
        xh_fm = P.sbuf("xh_fm", [128, 16 * T], BF16)
        wbuf = P.sbuf("wbuf", [128, 4 * 2048], BF16)
        evb = P.sbuf("evb", [128, 2 * MAXC], BF16)
        phase_xh_load(xh_fm)
        phase_fm_proj(xh_fm, wbuf, evb, "w_qT", D, "sc_q", "b_q")
        phase_fm_proj(xh_fm, wbuf, evb, "w_kT", D, "sc_k", "b_k")
        phase_fm_proj(xh_fm, wbuf, evb, "w_vT", D, "sc_v", "b_v")
    def seg3():
        phase_lightning()
    def seg4():
        phase_tm_proj_ln("oproj", "sc_o", 16, "sc_wo_bf", "b_o_rep", "x0", "sc_x1")
    def seg5():
        xh_fm = P.sbuf("xh_fm", [128, 16 * T], BF16)
        wbuf = P.sbuf("wbuf", [128, 4 * 2048], BF16)
        phase_xh_load(xh_fm)
        phase_ffn_gateup(xh_fm, wbuf)
    def seg6():
        phase_tm_proj_ln("ffn1d", "sc_h", 64, "sc_wd_bf", None, "sc_x1", "sc_x2",
                         stat_tiled=True)
    def seg7():
        own3 = P.sbuf("gdn_own3", [128, 256], BF16)
        with P.scope():
            xh_fm = P.sbuf("xh_fm", [128, 16 * T], BF16)
            wbuf = P.sbuf("wbuf", [128, 4 * 2048], BF16)
            phase_xh_load(xh_fm)
            phase_gdn_qkvz(xh_fm, wbuf, own3)
        with P.scope():
            phase_gdn_fixup(own3)
        egC_sb = P.sbuf("gdn_egC", [128, HV * NCG], F32)
        with P.scope():
            sel1 = P.sbuf("gdn_sel1", [32, HV * 128], F32)
            sel2 = P.sbuf("gdn_sel2", [32, HK * 128], F32)
            cs["c_sel1"] = sel1
            cs["c_sel2"] = sel2
            egl = P.sbuf("gdn_egl", [32, NCG], F32)
            cm2 = P.sbuf("gdn_cm2", [128, 6 * NCG * 16], F32)
            sls = P.sem()
            with nc.Block() as bsel:
                @bsel.sync
                def _(sync):
                    sync.dma_start(sel1[:, :], dd["c_sel1"][:, :]).then_inc(sls, 16)
                    sync.dma_start(sel2[:, :], dd["c_sel2"][:, :]).then_inc(sls, 16)
                    sync.wait_ge(sls, 32)
            with P.scope():
                phase_gdn_prep(cm2, egC_sb, egl)
            with P.scope():
                phase_gdn_l2norm()
            with P.scope():
                phase_gdn_egc(egC_sb, egl)
            with P.scope():
                phase_gdn_core(cm2, egC_sb, None)
        S32 = P.sbuf("gdn_S32", [128, HV * 128], F32)
        Sbf = P.sbuf("gdn_Sbf", [128, HV * 128], BF16)
        with P.scope():
            phase_gdn_scan(S32, Sbf, egC_sb, pass2=False)
        phase_gdn_xchg3()
        with P.scope():
            phase_gdn_scan(S32, Sbf, egC_sb, pass2=True)

    def seg7b():
        phase_tm_proj_ln("bproj", "sc_xgn", 32, "sc_wob_bf", None, "sc_x2", "sc_x3")
    def seg8():
        xh_fm = P.sbuf("xh_fm", [128, 16 * T], BF16)
        wbuf = P.sbuf("wbuf", [128, 4 * 2048], BF16)
        phase_xh_load(xh_fm)
        phase_ffn_gateup(xh_fm, wbuf)
    def seg9():
        phase_tm_proj_ln("ffn2d", "sc_h", 64, "sc_wd_bf", None, "sc_x3", "y",
                         final_out=True, stat_tiled=True)
    for i, seg in enumerate([seg0, seg1, seg2, seg3, seg4, seg5, seg6, seg7, seg7b, seg8, seg9]):
        if i < upto:
            with P.scope():
                seg()
    P.es.close()
    return P


def kernel(**inputs):
    x = np.asarray(inputs["L0_attn_a_x"])
    ins = {k: np.asarray(v) for k, v in inputs.items()}
    consts = host_constants()
    w = host_weights(ins)
    P = build(4096, dbg=False)
    in_maps = []
    for c in range(8):
        b, p = c // 2, c % 2
        m = {"x0": np.ascontiguousarray(x[b, p * 4096:(p + 1) * 4096, :])}
        m.update(w)
        m.update(consts)
        m["parity"] = np.full((128, 1), float(p), np.float32)
        in_maps.append(m)
    res = run_bass_kernel_spmd(P.nc, in_maps, core_ids=list(range(8)))
    out = np.empty((4, 8192, 2048), np.float32)
    for c in range(8):
        b, p = c // 2, c % 2
        out[b, p * 4096:(p + 1) * 4096, :] = res.results[c]["y"]
    return out
